# version 01
# baseline (speedup 1.0000x reference)
import math
import contextlib
import numpy as np
import ml_dtypes
import concourse.bass as bass
import concourse.mybir as mybir
from concourse.bass import IndirectOffsetOnAxis
from concourse.bass_utils import run_bass_kernel_spmd

F32 = mybir.dt.float32
BF16 = mybir.dt.bfloat16
FP16 = mybir.dt.float16
FP8 = mybir.dt.float8e5
I32 = mybir.dt.int32
AF = mybir.ActivationFunctionType
ALU = mybir.AluOpType
AX = mybir.AxisListType

ENG = ("tensor", "vector", "scalar", "gpsimd", "sync")
import os
DBG = bool(int(os.environ.get("KDBG", "0")))
NSEQ = 2
T = 2048
D = 1024
NCH = D // 128
TT = 512
NTT = T // TT
NTC = T // 128
NHEAD = 16
HD = 64
CAP = 1024
NEXP = 8
DH = 4096
NCHH = DH // 128
RTM = 12582912.0


class Prog:
    def __init__(self, nc):
        self.nc = nc
        self.stack = contextlib.ExitStack()
        self.cs = {e: self.stack.enter_context(nc.semaphore(f"cs_{e}")) for e in ENG}
        self.ds = {e: self.stack.enter_context(nc.semaphore(f"ds_{e}")) for e in ENG}
        self.cc = {e: 0 for e in ENG}
        self.dc = {e: 0 for e in ENG}
        self.clock = {e: {} for e in ENG}
        self.last_w = {}
        self.readers = {}
        self.nops = 0

    def _wait(self, eng, kind, src, cnt, snap):
        if cnt <= 0:
            return
        if self.clock[eng].get((kind, src), 0) >= cnt:
            return
        sem = (self.cs if kind == "c" else self.ds)[src]
        getattr(self.nc, eng).wait_ge(sem, cnt)
        self.clock[eng][(kind, src)] = cnt
        if snap:
            for kk, v in snap.items():
                if self.clock[eng].get(kk, 0) < v:
                    self.clock[eng][kk] = v

    def do(self, eng, emit, reads=(), writes=(), dma=False):
        for b in reads:
            lw = self.last_w.get(b)
            if lw is not None:
                self._wait(eng, lw[0], lw[1], lw[2], lw[3])
        for b in writes:
            lw = self.last_w.get(b)
            if lw is not None:
                self._wait(eng, lw[0], lw[1], lw[2], lw[3])
            for rr in self.readers.get(b, ()):
                self._wait(eng, rr[0], rr[1], rr[2], rr[3])
        inst = emit()
        kind = "d" if dma else "c"
        if dma:
            self.dc[eng] += 16
            inst.then_inc(self.ds[eng], 16)
            cnt = self.dc[eng]
        else:
            self.cc[eng] += 1
            inst.then_inc(self.cs[eng], 1)
            cnt = self.cc[eng]
        snap = dict(self.clock[eng])
        snap[(kind, eng)] = cnt
        rec = (kind, eng, cnt, snap)
        for b in writes:
            self.last_w[b] = rec
            self.readers[b] = []
        for b in reads:
            self.readers.setdefault(b, []).append(rec)
        self.nops += 1
        return inst

    def finalize(self, eng="sync"):
        for e in ENG:
            self._wait(eng, "c", e, self.cc[e], None)
            self._wait(eng, "d", e, self.dc[e], None)

    def barrier(self):
        for eng in ENG:
            for e in ENG:
                self._wait(eng, "c", e, self.cc[e], None)
                self._wait(eng, "d", e, self.dc[e], None)

    def close(self):
        self.stack.close()


def names(*aps):
    out = []
    for a in aps:
        if a is None or isinstance(a, (int, float)):
            continue
        out.append(a if isinstance(a, str) else a.tensor.name)
    return out


class K:
    def __init__(self):
        self.nc = bass.Bass("TRN2", target_bir_lowering=False)
        self.p = Prog(self.nc)

    def inp(self, name, shape, dtype=F32):
        return self.nc.dram_tensor(name, shape, dtype, kind="ExternalInput").ap()

    def out(self, name, shape, dtype=F32):
        return self.nc.dram_tensor(name, shape, dtype, kind="ExternalOutput").ap()

    def dram(self, name, shape, dtype=F32):
        return self.nc.dram_tensor(name, shape, dtype, kind="Internal").ap()

    def sbuf(self, name, shape, dtype=F32):
        return self.nc.alloc_sbuf_tensor(name, shape, dtype).ap()

    def psum(self, name, shape, dtype=F32):
        return self.nc.alloc_psum_tensor(name, shape, dtype).ap()

    def scope(self):
        return _Scope(self)

    def barrier(self):
        self.p.barrier()

    def mm(self, out, lhsT, rhs, start=True, stop=True):
        self.p.do("tensor",
                  lambda: self.nc.tensor.matmul(out, lhsT, rhs, start=start, stop=stop),
                  reads=names(lhsT, rhs), writes=names(out))

    def tr(self, out, in_, ident):
        self.p.do("tensor", lambda: self.nc.tensor.transpose(out, in_, ident),
                  reads=names(in_, ident), writes=names(out))

    def act(self, out, in_, func, bias=None, scale=1.0, accum_out=None):
        kw = {}
        if bias is not None:
            kw["bias"] = bias
        if accum_out is not None:
            kw["accum_out"] = accum_out
        self.p.do("scalar",
                  lambda: self.nc.scalar.activation(out, in_, func, scale=scale, **kw),
                  reads=names(in_, bias if not isinstance(bias, (float, type(None))) else None,
                              scale if not isinstance(scale, (float, int)) else None),
                  writes=names(out, accum_out))

    def tt(self, out, a, b, op, eng="vector"):
        self.p.do(eng, lambda: getattr(self.nc, eng).tensor_tensor(out, a, b, op),
                  reads=names(a, b), writes=names(out))

    def ts(self, out, in_, s1, s2, op0, op1=None, eng="vector"):
        def e():
            if op1 is None:
                return getattr(self.nc, eng).tensor_single_scalar(out, in_, s1, op0)
            return getattr(self.nc, eng).tensor_scalar(out, in_, s1, s2, op0, op1)
        self.p.do(eng, e, reads=names(in_, s1, s2), writes=names(out))

    def stt(self, out, in0, scalar, in1, op0, op1, eng="vector"):
        self.p.do(eng,
                  lambda: getattr(self.nc, eng).scalar_tensor_tensor(out, in0, scalar, in1, op0, op1),
                  reads=names(in0, in1, scalar), writes=names(out))

    def red(self, out, in_, op, absval=False, eng="vector"):
        self.p.do(eng,
                  lambda: getattr(self.nc, eng).tensor_reduce(
                      out, in_, AX.X, op, apply_absolute_value=absval),
                  reads=names(in_), writes=names(out))

    def scan(self, out, a, b, init, eng="vector"):
        self.p.do(eng,
                  lambda: getattr(self.nc, eng).tensor_tensor_scan(out, a, b, init, ALU.mult, ALU.add),
                  reads=names(a, b, init), writes=names(out))

    def copy(self, out, in_, eng="vector"):
        self.p.do(eng, lambda: getattr(self.nc, eng).tensor_copy(out, in_),
                  reads=names(in_), writes=names(out))

    def memset(self, ap, v, eng="vector"):
        self.p.do(eng, lambda: getattr(self.nc, eng).memset(ap, v), writes=names(ap))

    def recip(self, out, in_):
        self.p.do("vector", lambda: self.nc.vector.reciprocal(out, in_),
                  reads=names(in_), writes=names(out))

    def dma(self, out, in_, eng="sync", **kw):
        self.p.do(eng, lambda: getattr(self.nc, eng).dma_start(out, in_, **kw),
                  reads=names(in_), writes=names(out), dma=True)

    def idma(self, out, out_off, in_, in_off):
        rd = names(in_)
        wr = names(out)
        if in_off is not None:
            rd += names(in_off[0])
        if out_off is not None:
            rd += names(out_off[0])

        def e():
            oo = IndirectOffsetOnAxis(out_off[0], out_off[1]) if out_off else None
            io = IndirectOffsetOnAxis(in_off[0], in_off[1]) if in_off else None
            return self.nc.gpsimd.indirect_dma_start(out, oo, in_, io)
        self.p.do("gpsimd", e, reads=rd, writes=wr, dma=True)


class _Scope:
    _uid = [0]

    def __init__(self, k):
        self.k = k
        self.stack = contextlib.ExitStack()
        _Scope._uid[0] += 1
        self.uid = _Scope._uid[0]

    def sbuf(self, name, shape, dtype=F32):
        return self.stack.enter_context(
            self.k.nc.sbuf_tensor(f"{name}__{self.uid}", shape, dtype)).ap()

    def psum(self, name, shape, dtype=F32):
        return self.stack.enter_context(
            self.k.nc.psum_tensor(f"{name}__{self.uid}", shape, dtype)).ap()

    def __enter__(self):
        return self

    def __exit__(self, *a):
        self.k.p.barrier()
        self.stack.close()
        return False


def fmv(X, c, t0=0, tlen=T, stride=T):
    return X[:, c * stride + t0: c * stride + t0 + tlen]


def load_consts(k):
    c = {}
    c["ones_col"] = k.sbuf("c_ones_col", [128, 1])
    c["ones_row"] = k.sbuf("c_ones_row", [1, 128])
    c["identb"] = k.sbuf("c_identb", [128, 128], BF16)
    k.dma(c["ones_col"], k.inp("ones_col", [128, 1]))
    k.dma(c["ones_row"], k.inp("ones_row", [1, 128]))
    k.dma(c["identb"], k.inp("identb", [128, 128], BF16))
    c["eps_ln"] = k.sbuf("c_epsln", [1, 1])
    k.memset(c["eps_ln"], 1e-5)
    c["ones_col_b"] = k.sbuf("c_ones_col_b", [128, 1], BF16)
    k.memset(c["ones_col_b"], 1.0)
    return c


def emit_ln(k, c, s, x_dram, xn, rows_keep=None):
    with k.scope() as sc:
        xt = sc.sbuf("ln_xt", [128, NCH * TT])
        sq = sc.sbuf("ln_sq", [128, TT])
        rows = sc.sbuf("ln_rows", [1, 4 * TT])
        psSx = sc.psum("ln_psx", [1, TT])
        psSq = sc.psum("ln_psq", [1, TT])
        psR = sc.psum("ln_psr", [128, TT])
        psM = sc.psum("ln_psm", [128, TT])
        for tt_ in range(NTT):
            t0 = tt_ * TT
            for ci in range(NCH):
                k.dma(xt[:, ci * TT:(ci + 1) * TT],
                      x_dram[s, ci * 128:(ci + 1) * 128, t0:t0 + TT])
            for ci in range(NCH):
                k.mm(psSx, c["ones_col"], xt[:, ci * TT:(ci + 1) * TT],
                     start=(ci == 0), stop=(ci == NCH - 1))
            for ci in range(NCH):
                k.act(sq, xt[:, ci * TT:(ci + 1) * TT], AF.Square)
                k.mm(psSq, c["ones_col"], sq, start=(ci == 0), stop=(ci == NCH - 1))
            rmu = rows[:, 0:TT]
            rvar = rows[:, TT:2 * TT]
            rr = rows[:, 2 * TT:3 * TT]
            rm = rows[:, 3 * TT:4 * TT]
            k.ts(rmu, psSx, 1.0 / D, None, ALU.mult)
            k.ts(rvar, psSq, 1.0 / D, None, ALU.mult)
            k.tt(rr, rmu, rmu, ALU.mult)
            k.tt(rvar, rvar, rr, ALU.subtract)
            k.act(rr, rvar, AF.Sqrt, bias=c["eps_ln"])
            k.recip(rr, rr)
            k.tt(rm, rmu, rr, ALU.mult)
            if rows_keep is not None:
                k.copy(rows_keep[0][:, t0:t0 + TT], rr)
                k.copy(rows_keep[1][:, t0:t0 + TT], rm)
            k.mm(psR, c["ones_row"], rr)
            k.mm(psM, c["ones_row"], rm)
            for ci in range(NCH):
                k.tt(sq, xt[:, ci * TT:(ci + 1) * TT], psR, ALU.mult)
                k.tt(fmv(xn, ci, t0, TT), sq, psM, ALU.subtract)


def emit_wscale_ternary(k, c, wT_dram, rows_total, q_dram, sname):
    cols = wT_dram.shape[1]
    nrch = rows_total // 128
    swrow = k.sbuf(f"qsw_{sname}", [1, 1])
    rsrow = k.sbuf(f"qrs_{sname}", [1, 1])
    swcol = k.sbuf(f"qswc_{sname}", [128, 1])
    rscol = k.sbuf(f"qrsc_{sname}", [128, 1])
    with k.scope() as sc:
        wbuf = sc.sbuf(f"qw_{sname}", [128, cols])
        q8 = sc.sbuf(f"qw8_{sname}", [128, cols], FP8)
        acc = sc.sbuf(f"qa_{sname}", [128, 1])
        tmp1 = sc.sbuf(f"qt_{sname}", [128, 1])
        psA = sc.psum(f"qp_{sname}", [1, 1])
        psB = sc.psum(f"qb_{sname}", [128, 1])
        k.memset(acc, 0.0)
        for r in range(nrch):
            k.dma(wbuf, wT_dram[r * 128:(r + 1) * 128, :])
            k.red(tmp1, wbuf, ALU.add, absval=True)
            k.tt(acc, acc, tmp1, ALU.add)
        k.mm(psA, c["ones_col"], acc)
        k.ts(swrow, psA, 1.0 / (rows_total * cols), None, ALU.mult)
        k.ts(swrow, swrow, 1e-5, None, ALU.max)
        k.recip(rsrow, swrow)
        k.mm(psB, c["ones_row"], swrow)
        k.copy(swcol, psB)
        k.mm(psB, c["ones_row"], rsrow)
        k.copy(rscol, psB)
        for r in range(nrch):
            k.dma(wbuf, wT_dram[r * 128:(r + 1) * 128, :])
            k.ts(wbuf, wbuf, rscol, RTM, ALU.mult, ALU.add)
            k.ts(wbuf, wbuf, RTM, None, ALU.subtract)
            k.ts(wbuf, wbuf, -1.0, 1.0, ALU.max, ALU.min)
            k.copy(q8, wbuf)
            k.dma(q_dram[r * 128:(r + 1) * 128, :], q8)
    return swcol, rscol, rsrow


def emit_actquant_fm(k, c, sc, xn, rq, nch_in=NCH, eps=1e-5):
    pstr = sc.psum("aq_ptr", [128, 128], BF16)
    xtm = sc.sbuf("aq_xtm", [128, nch_in * 128], BF16)
    str_ = sc.sbuf("aq_scr", [128, nch_in * 128])
    mcol = sc.sbuf("aq_m", [128, 1])
    ccol = sc.sbuf("aq_c", [128, 1])
    dcol = sc.sbuf("aq_d", [128, 1])
    for tc in range(NTC):
        for ci in range(nch_in):
            k.tr(pstr, fmv(xn, ci, tc * 128, 128), c["identb"])
            k.copy(xtm[:, ci * 128:(ci + 1) * 128], pstr)
        k.copy(str_, xtm)
        k.red(mcol, str_, ALU.max, absval=True)
        k.ts(mcol, mcol, eps, None, ALU.max)
        k.recip(ccol, mcol)
        k.ts(ccol, ccol, 127.0, None, ALU.mult)
        k.ts(dcol, mcol, 1.0 / 127.0, None, ALU.mult)
        k.ts(str_, xtm, ccol, RTM, ALU.mult, ALU.add)
        k.ts(str_, str_, RTM, None, ALU.subtract)
        k.ts(xtm, str_, dcol, None, ALU.mult)
        for ci in range(nch_in):
            k.tr(pstr, xtm[:, ci * 128:(ci + 1) * 128], c["identb"])
            k.copy(fmv(rq, ci, tc * 128, 128), pstr)


def build_L0():
    k = K()
    c = load_consts(k)
    x_in = k.inp("x_in", [NSEQ, D, T])
    whgT = k.inp("l0_whgT", [D, 3072])
    woutT_d = k.inp("l0_woutT", [1536, D])
    w1T = k.inp("l0_w1T", [D, DH])
    b1_d = k.inp("l0_b1", [DH, 1])
    w2T = k.inp("l0_w2T", [DH, D])
    b2_d = k.inp("l0_b2", [D, 1])
    x_out = k.out("x_out", [NSEQ, D, T])
    if DBG:
        s_dram = k.out("s_scr", [NSEQ, D, T], BF16)
        xn_dbg = k.out("xn_dbg", [NSEQ, D, T], BF16)
        g_dbg = k.out("g_dbg", [NSEQ, D, T], BF16)
        rq_dbg = k.out("rq_dbg", [NSEQ, D, T], BF16)
        h_dbg = k.out("h_dbg", [NSEQ, T, DH], BF16)
    else:
        s_dram = k.dram("s_scr", [NSEQ, D, T], BF16)
    w1q_d = k.dram("w1q8d", [D, DH], FP8)
    w2q_d = k.dram("w2q8d", [DH, D], FP8)

    xn = k.sbuf("xn", [128, NCH * T], BF16)

    sw1, rs1, rs1row = emit_wscale_ternary(k, c, w1T, D, w1q_d, "w1")
    sw2, rs2, rs2row = emit_wscale_ternary(k, c, w2T, DH, w2q_d, "w2")
    b1p = k.sbuf("b1p", [1, DH])
    b2col = k.sbuf("b2col", [128, NCH])
    k.dma(b1p, b1_d.rearrange("h o -> o h"))
    k.ts(b1p, b1p, rs1row, None, ALU.mult)
    b2c_d = k.inp("l0_b2c", [128, NCH])
    k.dma(b2col, b2c_d)
    woutT = k.sbuf("woutT", [128, 12 * D], BF16)
    for ec in range(12):
        k.dma(woutT[:, ec * D:(ec + 1) * D], woutT_d[ec * 128:(ec + 1) * 128, :],
              eng="gpsimd")

    for s in range(NSEQ):
        emit_ln(k, c, s, x_in, xn)
        if DBG:
            for ci in range(NCH):
                k.dma(xn_dbg[s, ci * 128:(ci + 1) * 128, :], fmv(xn, ci), eng="gpsimd")

        with k.scope() as sc:
            wh = sc.sbuf("wh", [128, 1024], BF16)
            wg = sc.sbuf("wg", [128, 1024], BF16)
            Hc = sc.sbuf("Hc", [128, T])
            Gc = sc.sbuf("Gc", [128, T])
            sa = sc.sbuf("sa", [128, T])
            sz = sc.sbuf("sz", [128, T])
            sh = sc.sbuf("sh", [128, T])
            SF = sc.sbuf("SF", [128, T])
            SB = sc.sbuf("SB", [128, T])
            Ssum = sc.sbuf("Ssum", [128, 12 * T], BF16)
            psH = sc.psum("psH", [128, TT])
            psG = sc.psum("psG", [128, TT])
            for cc_ in range(12):
                for wbuf, col0 in ((wh, cc_ * 128), (wg, 1536 + cc_ * 128)):
                    for dc in range(NCH):
                        k.dma(wbuf[:, dc * 128:(dc + 1) * 128],
                              whgT[dc * 128:(dc + 1) * 128, col0:col0 + 128],
                              eng="gpsimd")
                for tt_ in range(NTT):
                    t0 = tt_ * TT
                    for dc in range(NCH):
                        k.mm(psH, wh[:, dc * 128:(dc + 1) * 128], fmv(xn, dc, t0, TT),
                             start=(dc == 0), stop=(dc == NCH - 1))
                    k.copy(Hc[:, t0:t0 + TT], psH)
                    for dc in range(NCH):
                        k.mm(psG, wg[:, dc * 128:(dc + 1) * 128], fmv(xn, dc, t0, TT),
                             start=(dc == 0), stop=(dc == NCH - 1))
                    k.copy(Gc[:, t0:t0 + TT], psG)
                k.act(sa, Gc, AF.Sigmoid, scale=-1.0)
                k.act(sz, Gc, AF.Sigmoid)
                k.act(sh, Hc, AF.Sigmoid)
                k.ts(Gc, Hc, 0.0, None, ALU.is_ge)
                k.ts(Hc, Hc, 0.5, None, ALU.add)
                k.tt(Hc, Hc, sh, ALU.subtract)
                k.tt(Gc, Gc, Hc, ALU.mult)
                k.tt(Gc, Gc, sh, ALU.add)
                k.tt(sz, sz, Gc, ALU.mult)
                k.scan(SF, sa, sz, 0.0)
                k.scan(SB, sa[:, ::-1], sz[:, ::-1], 0.0)
                k.tt(Ssum[:, cc_ * T:(cc_ + 1) * T], SF, SB, ALU.add)
            for dc in range(NCH):
                for tt_ in range(NTT):
                    t0 = tt_ * TT
                    for ec in range(12):
                        k.mm(psH, woutT[:, ec * D + dc * 128: ec * D + (dc + 1) * 128],
                             Ssum[:, ec * T + t0: ec * T + t0 + TT],
                             start=(ec == 0), stop=(ec == 11))
                    k.copy(SF[:, 0:TT], psH)
                    k.dma(s_dram[s, dc * 128:(dc + 1) * 128, t0:t0 + TT], SF[:, 0:TT],
                          eng="gpsimd")

        with k.scope() as sc:
            rq = sc.sbuf("rq", [128, NCH * T], BF16)
            emit_actquant_fm(k, c, sc, xn, rq)
            if DBG:
                for ci in range(NCH):
                    k.dma(rq_dbg[s, ci * 128:(ci + 1) * 128, :], fmv(rq, ci), eng="gpsimd")
            w2q8 = sc.sbuf("w2q8", [128, NCHH * 1024], FP8)
            for hc in range(NCHH):
                k.dma(w2q8[:, hc * D:(hc + 1) * D], w2q_d[hc * 128:(hc + 1) * 128, :])
            w1b = sc.sbuf("w1b", [128, NCH * TT], FP8)
            htm = sc.sbuf("htm", [128, DH], BF16)
            hts = sc.sbuf("hts", [128, DH // 2])
            hqfm = sc.sbuf("hqfm", [128, NCHH * TT], BF16)
            gt = sc.sbuf("gt", [128, TT], BF16)
            st2 = sc.sbuf("st2", [128, TT])
            xt2 = sc.sbuf("xt2", [128, TT])
            hcm = sc.sbuf("hcm", [128, 1])
            hcm2 = sc.sbuf("hcm2", [128, 1])
            hcc = sc.sbuf("hcc", [128, 1])
            hcd = sc.sbuf("hcd", [128, 1])
            ps1 = sc.psum("ps1", [128, TT])
            ps2 = sc.psum("ps2", [128, TT])
            pstr2 = sc.psum("pstr2", [128, 128], BF16)
            for tt_ in range(NTT):
                t0 = tt_ * TT
                for tci in range(4):
                    tc = tt_ * 4 + tci
                    for hh in range(NCH):
                        for dc in range(NCH):
                            k.dma(w1b[:, dc * TT:(dc + 1) * TT],
                                  w1q_d[dc * 128:(dc + 1) * 128, hh * TT:(hh + 1) * TT])
                        for dc in range(NCH):
                            k.mm(ps1, rq[:, dc * T + tc * 128: dc * T + (tc + 1) * 128],
                                 w1b[:, dc * TT:(dc + 1) * TT],
                                 start=(dc == 0), stop=False)
                        k.mm(ps1, c["ones_row"], b1p[:, hh * TT:(hh + 1) * TT],
                             start=False, stop=True)
                        k.act(htm[:, hh * TT:(hh + 1) * TT], ps1, AF.Gelu, scale=sw1)
                    if DBG:
                        k.dma(h_dbg[s, tc * 128:(tc + 1) * 128, :], htm, eng="gpsimd")
                    k.copy(hts, htm[:, 0:DH // 2])
                    k.red(hcm, hts, ALU.max, absval=True)
                    k.copy(hts, htm[:, DH // 2:DH])
                    k.red(hcm2, hts, ALU.max, absval=True)
                    k.tt(hcm, hcm, hcm2, ALU.max)
                    k.ts(hcm, hcm, 1e-5, None, ALU.max)
                    k.recip(hcc, hcm)
                    k.ts(hcc, hcc, 127.0, None, ALU.mult)
                    k.ts(hcd, hcm, 1.0 / 127.0, None, ALU.mult)
                    for hf in range(2):
                        hsl = htm[:, hf * (DH // 2):(hf + 1) * (DH // 2)]
                        k.ts(hts, hsl, hcc, RTM, ALU.mult, ALU.add)
                        k.ts(hts, hts, RTM, None, ALU.subtract)
                        k.ts(hsl, hts, hcd, None, ALU.mult)
                    for hc in range(NCHH):
                        k.tr(pstr2, htm[:, hc * 128:(hc + 1) * 128], c["identb"])
                        k.copy(hqfm[:, hc * TT + tci * 128: hc * TT + tci * 128 + 128], pstr2)
                for dc in range(NCH):
                    for hc in range(NCHH):
                        k.mm(ps2, w2q8[:, hc * D + dc * 128: hc * D + (dc + 1) * 128],
                             hqfm[:, hc * TT:(hc + 1) * TT],
                             start=(hc == 0), stop=(hc == NCHH - 1))
                    k.act(gt, ps2, AF.Silu, bias=b2col[:, dc:dc + 1], scale=sw2)
                    if DBG:
                        k.dma(g_dbg[s, dc * 128:(dc + 1) * 128, t0:t0 + TT], gt, eng="gpsimd")
                    k.dma(st2, s_dram[s, dc * 128:(dc + 1) * 128, t0:t0 + TT],
                          eng="gpsimd")
                    k.dma(xt2, x_in[s, dc * 128:(dc + 1) * 128, t0:t0 + TT])
                    k.tt(st2, st2, gt, ALU.mult)
                    k.tt(xt2, xt2, st2, ALU.add)
                    k.dma(x_out[s, dc * 128:(dc + 1) * 128, t0:t0 + TT], xt2)

    k.p.finalize()
    return k




def build_L1():
    k = K()
    c = load_consts(k)
    x_in = k.inp("x_in", [NSEQ, D, T])
    wqkv_d = k.inp("l1_wqkv", [D, 3 * D])
    bqkc_d = k.inp("l1_bqkv_c", [128, 16])
    bvrow_d = k.inp("l1_bv_row", [1, D])
    wo_d = k.inp("l1_wo", [D, D])
    boc_d = k.inp("l1_bo_c", [128, NCH])
    w1T = k.inp("l1_w1T", [D, DH])
    cwc_d = k.inp("l1_convw_c", [128, NCHH * 3])
    cbc_d = k.inp("l1_convb_c", [128, NCHH])
    w2T = k.inp("l1_w2T", [DH, D])
    x_out = k.out("x_out", [NSEQ, D, T])
    s_dram = k.dram("s_scr", [NSEQ, D, T], BF16)
    h2_dram = k.dram("h2_scr", [DH, T], BF16)
    w1q_d = k.dram("w1q8d", [D, DH], FP8)
    w2q_d = k.dram("w2q8d", [DH, D], FP8)

    xn = k.sbuf("xn", [128, NCH * T], BF16)
    sw1, rs1, rs1row = emit_wscale_ternary(k, c, w1T, D, w1q_d, "w1")
    sw2, rs2, rs2row = emit_wscale_ternary(k, c, w2T, DH, w2q_d, "w2")
    bqkc = k.sbuf("bqkc", [128, 16])
    bvrow = k.sbuf("bvrow", [1, D])
    boc = k.sbuf("boc", [128, NCH])
    cwc = k.sbuf("cwc", [128, NCHH * 3])
    cbc = k.sbuf("cbc", [128, NCHH])
    bqh = k.sbuf("bqh", [64, NHEAD])
    bkh = k.sbuf("bkh", [64, NHEAD])
    bqh_d = k.inp("l1_bq_h", [64, NHEAD])
    bkh_d = k.inp("l1_bk_h", [64, NHEAD])
    k.dma(bqh, bqh_d)
    k.dma(bkh, bkh_d)
    k.ts(bqh, bqh, 0.125, None, ALU.mult)
    k.dma(bqkc, bqkc_d)
    k.dma(bvrow, bvrow_d)
    k.dma(boc, boc_d)
    k.dma(cwc, cwc_d)
    k.dma(cbc, cbc_d)
    k.ts(bqkc[:, 0:8], bqkc[:, 0:8], 0.125, None, ALU.mult)
    k.ts(cwc, cwc, sw1, None, ALU.mult)

    if DBG:
        s_dram = k.out("s_scr2", [NSEQ, D, T], BF16)

    for s in range(NSEQ):
        emit_ln(k, c, s, x_in, xn)

        with k.scope() as sc:
            wqs = sc.sbuf("wqs", [128, 1024], BF16)
            vtm = sc.sbuf("vtm", [128, NTC * NHEAD * 65], BF16)
            avall = sc.sbuf("avall", [128, NCH * T], BF16)
            wv = sc.sbuf("wv", [128, NCH * TT], BF16)
            psQ = sc.psum("psQ", [128, TT])
            k.memset(vtm, 1.0)
            for vh in range(2):
                for dc in range(NCH):
                    k.dma(wv[:, dc * TT:(dc + 1) * TT],
                          wqkv_d[dc * 128:(dc + 1) * 128, 2 * D + vh * TT: 2 * D + (vh + 1) * TT],
                          eng="gpsimd")
                for tc in range(NTC):
                    for dc in range(NCH):
                        k.mm(psQ, fmv(xn, dc, tc * 128, 128), wv[:, dc * TT:(dc + 1) * TT],
                             start=(dc == 0), stop=False)
                    k.mm(psQ, c["ones_row"], bvrow[:, vh * TT:(vh + 1) * TT],
                         start=False, stop=True)
                    base = tc * NHEAD * 65 + vh * 8 * 65
                    k.copy(vtm[:, base: base + 8 * 65]
                           .rearrange("p (h e) -> p h e", e=65)[:, :, 0:64],
                           psQ.rearrange("p (h e) -> p h e", e=64))
            kaug = sc.sbuf("kaug", [128, T], BF16)
            qaug = sc.sbuf("qaug", [128, T], BF16)
            wqh = sc.sbuf("wqh", [128, NCH * 64], BF16)
            wkh = sc.sbuf("wkh", [128, NCH * 64], BF16)
            sqsc = sc.sbuf("sqsc", [64, TT])
            rowm = sc.sbuf("rowm", [1, T])
            sc1 = sc.sbuf("sc1", [1, 1])
            sc2 = sc.sbuf("sc2", [1, 1])
            mcol = sc.sbuf("mcol", [128, 1])
            srow = sc.sbuf("srow", [128, TT])
            srow0 = sc.sbuf("srow0", [1, TT])
            bcs = sc.sbuf("bcs", [64, TT])
            pt = sc.sbuf("pt", [128, TT], BF16)
            psS = sc.psum("psS", [128, TT])
            psA = sc.psum("psA", [128, TT])
            psR = sc.psum("psR", [1, TT])
            psB = sc.psum("psB", [128, 1])
            psH = sc.psum("psHh", [64, TT])
            k.memset(kaug[64:66, :], 0.0)
            k.memset(qaug[64:66, :], 1.0)
            for h in range(NHEAD):
                for dc in range(NCH):
                    k.dma(wqh[:, dc * 64:(dc + 1) * 64],
                          wqkv_d[dc * 128:(dc + 1) * 128, h * 64:(h + 1) * 64],
                          eng="gpsimd")
                    k.dma(wkh[:, dc * 64:(dc + 1) * 64],
                          wqkv_d[dc * 128:(dc + 1) * 128, D + h * 64: D + (h + 1) * 64],
                          eng="gpsimd")
                for tt_ in range(NTT):
                    t0 = tt_ * TT
                    for dc in range(NCH):
                        k.mm(psH, wqh[:, dc * 64:(dc + 1) * 64], fmv(xn, dc, t0, TT),
                             start=(dc == 0), stop=(dc == NCH - 1))
                    k.ts(qaug[0:64, t0:t0 + TT], psH, bqh[:, h:h + 1], 0.125,
                         ALU.add, ALU.mult)
                    for dc in range(NCH):
                        k.mm(psH, wkh[:, dc * 64:(dc + 1) * 64], fmv(xn, dc, t0, TT),
                             start=(dc == 0), stop=(dc == NCH - 1))
                    k.ts(kaug[0:64, t0:t0 + TT], psH, bkh[:, h:h + 1], None, ALU.add)
                for aug, scx in ((qaug, sc1), (kaug, sc2)):
                    for tt_ in range(NTT):
                        k.act(sqsc, aug[0:64, tt_ * TT:(tt_ + 1) * TT], AF.Square)
                        k.mm(psR, c["ones_col"][0:64, :], sqsc)
                        k.copy(rowm[:, tt_ * TT:(tt_ + 1) * TT], psR)
                    k.red(scx, rowm, ALU.max)
                k.act(sc1, sc1, AF.Sqrt)
                k.act(sc2, sc2, AF.Sqrt)
                k.tt(sc1, sc1, sc2, ALU.mult)
                k.ts(sc1, sc1, -1.0, None, ALU.mult)
                k.mm(psB, c["ones_row"], sc1)
                k.copy(mcol, psB)
                for qt in range(NTT):
                    q0 = qt * TT
                    for kc in range(NTC):
                        k.mm(psS, kaug[:, kc * 128:(kc + 1) * 128][0:65, :],
                             qaug[0:65, q0:q0 + TT])
                        k.act(pt, psS, AF.Exp, bias=mcol)
                        k.mm(psA[0:65, :], vtm[:, (kc * NHEAD + h) * 65:(kc * NHEAD + h) * 65 + 65],
                             pt, start=(kc == 0), stop=(kc == NTC - 1))
                    k.copy(srow[64:65, :], psA[64:65, :])
                    k.dma(srow0, srow[64:65, :])
                    k.recip(srow0, srow0)
                    k.mm(psS[0:64, :], c["ones_row"][:, 0:64], srow0)
                    k.copy(bcs, psS[0:64, :])
                    k.tt(bcs, psA[0:64, :], bcs, ALU.mult)
                    ci, off = h // 2, (h % 2) * 64
                    k.copy(fmv(avall, ci, q0, TT)[off:off + 64, :], bcs)
            for oc in range(NCH):
                for dc in range(NCH):
                    k.dma(wqs[:, dc * 128:(dc + 1) * 128],
                          wo_d[dc * 128:(dc + 1) * 128, oc * 128:(oc + 1) * 128],
                          eng="gpsimd")
                for tt_ in range(NTT):
                    t0 = tt_ * TT
                    for dc in range(NCH):
                        k.mm(psQ, wqs[:, dc * 128:(dc + 1) * 128], fmv(avall, dc, t0, TT),
                             start=(dc == 0), stop=(dc == NCH - 1))
                    k.ts(srow, psQ, boc[:, oc:oc + 1], None, ALU.add)
                    k.dma(s_dram[s, oc * 128:(oc + 1) * 128, t0:t0 + TT], srow,
                          eng="gpsimd")

        with k.scope() as sc:
            rq = sc.sbuf("rq", [128, NCH * T], BF16)
            emit_actquant_fm(k, c, sc, xn, rq)
            w1b = sc.sbuf("w1b", [128, NCH * 128], FP8)
            y1 = sc.sbuf("y1", [128, T + 4])
            y2 = sc.sbuf("y2", [128, T])
            y3 = sc.sbuf("y3", [128, T])
            h2b = sc.sbuf("h2b", [128, T], BF16)
            psF = sc.psum("psF", [128, TT])
            k.memset(y1, 0.0)
            for hc in range(NCHH):
                for dc in range(NCH):
                    k.dma(w1b[:, dc * 128:(dc + 1) * 128],
                          w1q_d[dc * 128:(dc + 1) * 128, hc * 128:(hc + 1) * 128])
                for tt_ in range(NTT):
                    t0 = tt_ * TT
                    for dc in range(NCH):
                        k.mm(psF, w1b[:, dc * 128:(dc + 1) * 128], fmv(rq, dc, t0, TT),
                             start=(dc == 0), stop=(dc == NCH - 1))
                    k.copy(y1[:, 1 + t0: 1 + t0 + TT], psF)
                k.ts(y2, y1[:, 0:T], cwc[:, hc * 3:hc * 3 + 1], None, ALU.mult)
                k.stt(y3, y1[:, 1:T + 1], cwc[:, hc * 3 + 1:hc * 3 + 2], y2, ALU.mult, ALU.add)
                k.stt(y2, y1[:, 2:T + 2], cwc[:, hc * 3 + 2:hc * 3 + 3], y3, ALU.mult, ALU.add)
                k.act(h2b, y2, AF.Silu, bias=cbc[:, hc:hc + 1])
                k.dma(h2_dram[hc * 128:(hc + 1) * 128, :], h2b, eng="gpsimd")

        with k.scope() as sc:
            pstr = sc.psum("ptr", [128, 128], BF16)
            h2tm = sc.sbuf("h2tm", [128, DH], BF16)
            hts = sc.sbuf("hts", [128, DH // 2])
            hqfm = sc.sbuf("hqfm", [128, NCHH * TT], BF16)
            hsl = sc.sbuf("hsl", [128, 128], BF16)
            w2q8 = sc.sbuf("w2q8", [128, NCHH * 1024], FP8)
            gt = sc.sbuf("gt", [128, TT], BF16)
            st2 = sc.sbuf("st2", [128, TT])
            xt2 = sc.sbuf("xt2", [128, TT])
            hcm = sc.sbuf("hcm", [128, 1])
            hcm2 = sc.sbuf("hcm2", [128, 1])
            hcc = sc.sbuf("hcc", [128, 1])
            hcd = sc.sbuf("hcd", [128, 1])
            ps1 = sc.psum("ps1", [128, TT])
            ps2 = sc.psum("ps2", [128, TT])
            for hc in range(NCHH):
                k.dma(w2q8[:, hc * D:(hc + 1) * D], w2q_d[hc * 128:(hc + 1) * 128, :])
            for tt_ in range(NTT):
                t0 = tt_ * TT
                for tci in range(4):
                    tc = tt_ * 4 + tci
                    for hc in range(NCHH):
                        k.dma(hsl, h2_dram[hc * 128:(hc + 1) * 128, tc * 128:(tc + 1) * 128])
                        k.tr(pstr, hsl, c["identb"])
                        k.copy(h2tm[:, hc * 128:(hc + 1) * 128], pstr)
                    k.copy(hts, h2tm[:, 0:DH // 2])
                    k.red(hcm, hts, ALU.max, absval=True)
                    k.copy(hts, h2tm[:, DH // 2:DH])
                    k.red(hcm2, hts, ALU.max, absval=True)
                    k.tt(hcm, hcm, hcm2, ALU.max)
                    k.ts(hcm, hcm, 1e-5, None, ALU.max)
                    k.recip(hcc, hcm)
                    k.ts(hcc, hcc, 127.0, None, ALU.mult)
                    k.ts(hcd, hcm, 1.0 / 127.0, None, ALU.mult)
                    for hf in range(2):
                        hx = h2tm[:, hf * (DH // 2):(hf + 1) * (DH // 2)]
                        k.ts(hts, hx, hcc, RTM, ALU.mult, ALU.add)
                        k.ts(hts, hts, RTM, None, ALU.subtract)
                        k.ts(hx, hts, hcd, None, ALU.mult)
                    for hc in range(NCHH):
                        k.tr(pstr, h2tm[:, hc * 128:(hc + 1) * 128], c["identb"])
                        k.copy(hqfm[:, hc * TT + tci * 128: hc * TT + tci * 128 + 128], pstr)
                for dc in range(NCH):
                    for hc in range(NCHH):
                        k.mm(ps2, w2q8[:, hc * D + dc * 128: hc * D + (dc + 1) * 128],
                             hqfm[:, hc * TT:(hc + 1) * TT],
                             start=(hc == 0), stop=(hc == NCHH - 1))
                    k.act(gt, ps2, AF.Silu, scale=sw2)
                    k.dma(st2, s_dram[s, dc * 128:(dc + 1) * 128, t0:t0 + TT], eng="gpsimd")
                    k.dma(xt2, x_in[s, dc * 128:(dc + 1) * 128, t0:t0 + TT])
                    k.tt(st2, st2, gt, ALU.mult)
                    k.tt(xt2, xt2, st2, ALU.add)
                    k.dma(x_out[s, dc * 128:(dc + 1) * 128, t0:t0 + TT], xt2)

    k.p.finalize()
    return k



def build_L2():
    k = K()
    c = load_consts(k)
    x_in = k.inp("x_in", [NSEQ, D, T])
    rmsw_d = k.inp("l2_rmsw_c", [128, NCH])
    wA_d = k.inp("l2_wA", [D, D])
    bA_d = k.inp("l2_bA_c", [128, NCH])
    wx_d = k.inp("l2_wx", [D, D])
    wsg_d = k.inp("l2_wsg", [D, D])
    wout_d = k.inp("l2_wout", [D, D])
    wo_d = k.inp("l2_wo", [D, D])
    w1T = k.inp("l2_w1T", [D, DH])
    b1c_d = k.inp("l2_b1_c", [128, NCHH])
    w2T = k.inp("l2_w2T", [DH, D])
    b2c_d = k.inp("l2_b2_c", [128, NCH])
    drop_d = k.inp("drop_fm", [NSEQ, D, T], BF16)
    x_out = k.out("x_out", [NSEQ, D, T])
    s_dram = k.out("s_scr2", [NSEQ, D, T], BF16) if DBG \
        else k.dram("s_scr", [NSEQ, D, T], BF16)

    xn = k.sbuf("xn", [128, NCH * T], BF16)
    rows_r = k.sbuf("rows_r", [1, T])
    rows_m = k.sbuf("rows_m", [1, T])
    rmsw = k.sbuf("rmsw", [128, NCH])
    bAc = k.sbuf("bAc", [128, NCH])
    b1q = k.sbuf("b1q", [128, NCHH])
    b2q = k.sbuf("b2q", [128, NCH])
    b8s = k.sbuf("b8s", [128, NCHH], FP8)
    k.dma(rmsw, rmsw_d)
    k.dma(bAc, bA_d)
    k.dma(b1q, b1c_d)
    k.dma(b2q, b2c_d)
    k.act(b8s, b1q, AF.Copy)
    k.act(b1q, b8s, AF.Copy)
    k.act(b8s[:, 0:NCH], b2q, AF.Copy)
    k.act(b2q, b8s[:, 0:NCH], AF.Copy)

    for s in range(NSEQ):
        emit_ln(k, c, s, x_in, xn, rows_keep=(rows_r, rows_m))

        with k.scope() as sc:
            xi = sc.sbuf("xi", [128, NCH * T], BF16)
            yb = sc.sbuf("yb", [128, NCH * T], BF16)
            wsl = sc.sbuf("wsl", [128, 1024], BF16)
            Ac = sc.sbuf("Ac", [128, T])
            Bc = sc.sbuf("Bc", [128, T])
            sidc = sc.sbuf("sidc", [128, T])
            sq2 = sc.sbuf("sq2", [128, TT])
            rrow = sc.sbuf("rrow", [1, TT])
            psP = sc.psum("psP", [128, TT])
            psR2 = sc.psum("psR2", [1, TT])
            psBB = sc.psum("psBB", [128, TT])
            for tt_ in range(NTT):
                t0 = tt_ * TT
                for ci in range(NCH):
                    k.act(sq2, fmv(xn, ci, t0, TT), AF.Square)
                    k.mm(psR2, c["ones_col"], sq2, start=(ci == 0), stop=(ci == NCH - 1))
                k.ts(rrow, psR2, 1.0 / D, 1e-5, ALU.mult, ALU.add)
                k.act(rrow, rrow, AF.Sqrt)
                k.recip(rrow, rrow)
                k.mm(psBB, c["ones_row"], rrow)
                for ci in range(NCH):
                    k.tt(sq2, fmv(xn, ci, t0, TT), psBB, ALU.mult)
                    k.ts(fmv(xi, ci, t0, TT), sq2, rmsw[:, ci:ci + 1], None, ALU.mult)
            for oc in range(NCH):
                for wd, dst, biased in ((wA_d, Ac, True), (wout_d, sidc, False)):
                    for dc in range(NCH):
                        k.dma(wsl[:, dc * 128:(dc + 1) * 128],
                              wd[dc * 128:(dc + 1) * 128, oc * 128:(oc + 1) * 128],
                              eng="gpsimd")
                    for tt_ in range(NTT):
                        t0 = tt_ * TT
                        for dc in range(NCH):
                            k.mm(psP, wsl[:, dc * 128:(dc + 1) * 128], fmv(xi, dc, t0, TT),
                                 start=(dc == 0), stop=(dc == NCH - 1))
                        if biased:
                            k.act(dst[:, t0:t0 + TT], psP, AF.Sigmoid, bias=bAc[:, oc:oc + 1])
                        else:
                            k.act(dst[:, t0:t0 + TT], psP, AF.Sigmoid)
                for dc in range(NCH):
                    k.dma(wsl[:, dc * 128:(dc + 1) * 128],
                          wx_d[dc * 128:(dc + 1) * 128, oc * 128:(oc + 1) * 128],
                          eng="gpsimd")
                for tt_ in range(NTT):
                    t0 = tt_ * TT
                    for dc in range(NCH):
                        k.mm(psP, wsl[:, dc * 128:(dc + 1) * 128], fmv(xi, dc, t0, TT),
                             start=(dc == 0), stop=(dc == NCH - 1))
                    k.copy(Bc[:, t0:t0 + TT], psP)
                k.tt(Bc, Bc, sidc, ALU.mult)
                k.scan(Bc, Ac, Bc, 0.0)
                for dc in range(NCH):
                    k.dma(wsl[:, dc * 128:(dc + 1) * 128],
                          wsg_d[dc * 128:(dc + 1) * 128, oc * 128:(oc + 1) * 128],
                          eng="gpsimd")
                for tt_ in range(NTT):
                    t0 = tt_ * TT
                    for dc in range(NCH):
                        k.mm(psP, wsl[:, dc * 128:(dc + 1) * 128], fmv(xi, dc, t0, TT),
                             start=(dc == 0), stop=(dc == NCH - 1))
                    k.act(sidc[:, t0:t0 + TT], psP, AF.Sigmoid)
                k.tt(fmv(yb, oc), sidc, Bc, ALU.mult)
            for oc in range(NCH):
                for dc in range(NCH):
                    k.dma(wsl[:, dc * 128:(dc + 1) * 128],
                          wo_d[dc * 128:(dc + 1) * 128, oc * 128:(oc + 1) * 128],
                          eng="gpsimd")
                for tt_ in range(NTT):
                    t0 = tt_ * TT
                    for dc in range(NCH):
                        k.mm(psP, wsl[:, dc * 128:(dc + 1) * 128], fmv(yb, dc, t0, TT),
                             start=(dc == 0), stop=(dc == NCH - 1))
                    k.tt(sq2, psP, fmv(xn, oc, t0, TT), ALU.add)
                    k.dma(s_dram[s, oc * 128:(oc + 1) * 128, t0:t0 + TT], sq2,
                          eng="gpsimd")

        with k.scope() as sc:
            w1q8 = sc.sbuf("w1q8", [128, NCH * DH], FP8)
            w2q8 = sc.sbuf("w2q8", [128, NCHH * D], FP8)
            for dc in range(NCH):
                k.dma(w1q8[:, dc * DH:(dc + 1) * DH], w1T[dc * 128:(dc + 1) * 128, :],
                      eng="gpsimd")
            for hc in range(NCHH):
                k.dma(w2q8[:, hc * D:(hc + 1) * D], w2T[hc * 128:(hc + 1) * 128, :],
                      eng="gpsimd")
            xq8 = sc.sbuf("xq8", [128, NCH * TT], FP8)
            xf = sc.sbuf("xf", [128, TT])
            hb = sc.sbuf("hb", [128, NCHH * TT], BF16)
            hq8 = sc.sbuf("hq8", [128, NCHH * TT], FP8)
            gt = sc.sbuf("gt", [128, TT], BF16)
            st2 = sc.sbuf("st2", [128, TT])
            xt2 = sc.sbuf("xt2", [128, TT])
            dt2 = sc.sbuf("dt2", [128, TT], BF16)
            dpt = sc.sbuf("dpt", [128, TT], BF16)
            psR3 = sc.psum("psR3", [128, TT])
            psM3 = sc.psum("psM3", [128, TT])
            ps1 = sc.psum("ps1", [128, TT])
            ps2 = sc.psum("ps2", [128, TT])
            for tt_ in range(NTT):
                t0 = tt_ * TT
                k.mm(psR3, c["ones_row"], rows_r[:, t0:t0 + TT])
                k.mm(psM3, c["ones_row"], rows_m[:, t0:t0 + TT])
                for ci in range(NCH):
                    k.dma(xt2, x_in[s, ci * 128:(ci + 1) * 128, t0:t0 + TT])
                    k.tt(xf, xt2, psR3, ALU.mult)
                    k.tt(xf, xf, psM3, ALU.subtract)
                    k.act(xq8[:, ci * TT:(ci + 1) * TT], xf, AF.Copy)
                for hc in range(NCHH):
                    for dc in range(NCH):
                        k.mm(ps1, w1q8[:, dc * DH + hc * 128: dc * DH + (hc + 1) * 128],
                             xq8[:, dc * TT:(dc + 1) * TT],
                             start=(dc == 0), stop=(dc == NCH - 1))
                    k.act(hb[:, hc * TT:(hc + 1) * TT], ps1, AF.Relu,
                          bias=b1q[:, hc:hc + 1])
                    k.act(hq8[:, hc * TT:(hc + 1) * TT], hb[:, hc * TT:(hc + 1) * TT],
                          AF.Copy)
                for dc in range(NCH):
                    for hc in range(NCHH):
                        k.mm(ps2, w2q8[:, hc * D + dc * 128: hc * D + (dc + 1) * 128],
                             hq8[:, hc * TT:(hc + 1) * TT],
                             start=(hc == 0), stop=(hc == NCHH - 1))
                    k.ts(dt2, ps2, b2q[:, dc:dc + 1], None, ALU.add)
                    k.dma(dpt, drop_d[s, dc * 128:(dc + 1) * 128, t0:t0 + TT], eng="sync")
                    k.tt(xf, dt2, dpt, ALU.mult)
                    k.act(gt, xf, AF.Silu)
                    k.dma(st2, s_dram[s, dc * 128:(dc + 1) * 128, t0:t0 + TT], eng="gpsimd")
                    k.dma(xt2, x_in[s, dc * 128:(dc + 1) * 128, t0:t0 + TT])
                    k.tt(st2, st2, gt, ALU.mult)
                    k.tt(xt2, xt2, st2, ALU.add)
                    k.dma(x_out[s, dc * 128:(dc + 1) * 128, t0:t0 + TT], xt2)

    k.p.finalize()
    return k



def emit_ln_inner(k, c, xn, dst, gcol, sc_unused=None):
    sc = k.scope()
    sc.__enter__()
    sq = sc.sbuf("li_sq", [128, TT])
    rows = sc.sbuf("li_rows", [1, 4 * TT])
    psSx = sc.psum("li_psx", [1, TT])
    psSq = sc.psum("li_psq", [1, TT])
    psR = sc.psum("li_psr", [128, TT])
    psM = sc.psum("li_psm", [128, TT])
    for tt_ in range(NTT):
        t0 = tt_ * TT
        for ci in range(NCH):
            k.mm(psSx, c["ones_col_b"], fmv(xn, ci, t0, TT),
                 start=(ci == 0), stop=(ci == NCH - 1))
        for ci in range(NCH):
            k.act(sq, fmv(xn, ci, t0, TT), AF.Square)
            k.mm(psSq, c["ones_col"], sq, start=(ci == 0), stop=(ci == NCH - 1))
        rmu = rows[:, 0:TT]
        rvar = rows[:, TT:2 * TT]
        rr = rows[:, 2 * TT:3 * TT]
        rm = rows[:, 3 * TT:4 * TT]
        k.ts(rmu, psSx, 1.0 / D, None, ALU.mult)
        k.ts(rvar, psSq, 1.0 / D, None, ALU.mult)
        k.tt(rr, rmu, rmu, ALU.mult)
        k.tt(rvar, rvar, rr, ALU.subtract)
        k.act(rr, rvar, AF.Sqrt, bias=c["eps_ln"])
        k.recip(rr, rr)
        k.tt(rm, rmu, rr, ALU.mult)
        k.mm(psR, c["ones_row"], rr)
        k.mm(psM, c["ones_row"], rm)
        for ci in range(NCH):
            k.tt(sq, fmv(xn, ci, t0, TT), psR, ALU.mult)
            k.tt(sq, sq, psM, ALU.subtract)
            k.ts(fmv(dst, ci, t0, TT), sq, gcol[:, ci:ci + 1], None, ALU.mult)
    sc.__exit__(None, None, None)


def emit_headln(k, c, pin, pout, gcol, bufs, scale_col=None):
    (sq2, rows4, psR1, psRb) = bufs
    for tt_ in range(NTT):
        t0 = tt_ * TT
        k.mm(psR1, c["ones_col"][0:64, :], pin[:, t0:t0 + TT])
        rmu = rows4[:, 0:TT]
        k.ts(rmu, psR1, 1.0 / HD, None, ALU.mult)
        k.act(sq2, pin[:, t0:t0 + TT], AF.Square)
        k.mm(psR1, c["ones_col"][0:64, :], sq2)
        rvar = rows4[:, TT:2 * TT]
        rr = rows4[:, 2 * TT:3 * TT]
        rm = rows4[:, 3 * TT:4 * TT]
        k.ts(rvar, psR1, 1.0 / HD, None, ALU.mult)
        k.tt(rr, rmu, rmu, ALU.mult)
        k.tt(rvar, rvar, rr, ALU.subtract)
        k.act(rr, rvar, AF.Sqrt, bias=c["eps_ln"])
        k.recip(rr, rr)
        k.tt(rm, rmu, rr, ALU.mult)
        k.mm(psRb[0:64, :], c["ones_row"][:, 0:64], rr)
        k.tt(sq2, pin[:, t0:t0 + TT], psRb[0:64, :], ALU.mult)
        k.mm(psRb[0:64, :], c["ones_row"][:, 0:64], rm)
        k.tt(sq2, sq2, psRb[0:64, :], ALU.subtract)
        k.ts(sq2, sq2, gcol, None, ALU.mult)
        if scale_col is not None:
            k.ts(pout[:, t0:t0 + TT], sq2, scale_col, None, ALU.mult)
        else:
            k.copy(pout[:, t0:t0 + TT], sq2)


def emit_l3_head_attn(k, c, bufs, kaug, qaug, vslot, outdst, qmask_on):
    (sqsc, rowm, sc1, sc2, mcol, srow, srow0, bcs, pt, psS, psA, psR, psB, qmb) = bufs
    for aug, scx in ((qaug, sc1), (kaug, sc2)):
        for tt_ in range(NTT):
            k.act(sqsc, aug[0:64, tt_ * TT:(tt_ + 1) * TT], AF.Square)
            k.mm(psR, c["ones_col"][0:64, :], sqsc)
            k.copy(rowm[:, tt_ * TT:(tt_ + 1) * TT], psR)
        k.red(scx, rowm, ALU.max)
    k.act(sc1, sc1, AF.Sqrt)
    k.act(sc2, sc2, AF.Sqrt)
    k.tt(sc1, sc1, sc2, ALU.mult)
    k.ts(sc1, sc1, -1.0, None, ALU.mult)
    k.mm(psB, c["ones_row"], sc1)
    k.copy(mcol, psB)
    for qt in range(NTT):
        q0 = qt * TT
        for kc in range(NTC):
            k.mm(psS, kaug[:, kc * 128:(kc + 1) * 128][0:65, :],
                 qaug[0:65, q0:q0 + TT])
            k.act(pt, psS, AF.Exp, bias=mcol)
            k.mm(psA[0:65, :], vslot(kc), pt,
                 start=(kc == 0), stop=(kc == NTC - 1))
        k.copy(srow[64:65, :], psA[64:65, :])
        k.dma(srow0, srow[64:65, :])
        k.recip(srow0, srow0)
        k.mm(psS[0:64, :], c["ones_row"][:, 0:64], srow0)
        k.copy(bcs, psS[0:64, :])
        k.tt(bcs, psA[0:64, :], bcs, ALU.mult)
        if qmask_on is not None:
            k.mm(psS[0:64, :], c["ones_row"][:, 0:64], qmask_on[:, q0:q0 + TT])
            k.tt(bcs, bcs, psS[0:64, :], ALU.mult)
        outdst(qt, bcs)


def l3a_common_inputs(k):
    d = {}
    d["x_in"] = k.inp("x_in", [NSEQ, D, T])
    d["wqT"] = k.inp("l3_wqT", [D, D])
    d["bq_c"] = k.inp("l3_bq_c", [128, NCH])
    d["wkT"] = k.inp("l3_wkT", [D, D])
    d["wvT"] = k.inp("l3_wvT", [D, D])
    d["bv_c"] = k.inp("l3_bv_c", [128, NCH])
    d["lna_c"] = k.inp("l3_lna_c", [128, NCH])
    d["lnb_c"] = k.inp("l3_lnb_c", [128, NCH])
    d["lnc"] = k.inp("l3_lnc", [HD, 1])
    d["lnd"] = k.inp("l3_lnd", [HD, 1])
    d["lqT"] = k.inp("l3_lqT", [HD, HD])
    d["lqb"] = k.inp("l3_lqb", [HD, 1])
    d["lkT"] = k.inp("l3_lkT", [HD, HD])
    d["lkb"] = k.inp("l3_lkb", [HD, 1])
    d["lvT"] = k.inp("l3_lvT", [HD, HD])
    d["lvb"] = k.inp("l3_lvb", [1, HD])
    d["sc3"] = k.inp("l3_sc3", [1, 3])
    return d


SC64 = float(64.0 ** -0.25)
CNT_L3 = float(16 * NHEAD * T * HD)


def emit_tscale(k, c, temp, tsc, t1, t2):
    k.ts(t1, temp, 1e-30, None, ALU.max)
    k.act(t1, t1, AF.Sqrt)
    k.recip(t1, t1)
    k.ts(t2, temp, 0.0, None, ALU.is_gt)
    k.tt(t1, t1, t2, ALU.mult)
    k.ts(t2, t2, -1.0, 1.0, ALU.mult, ALU.add)
    k.tt(tsc, t1, t2, ALU.add)


def build_L3a():
    k = K()
    c = load_consts(k)
    d = l3a_common_inputs(k)
    bqh_d = k.inp("l3_bq_h", [HD, NHEAD])
    bvh_d = k.inp("l3_bv_h", [HD, NHEAD])
    xn3_o = k.out("xn3", [NSEQ, D, T], BF16)
    sk_o = k.out("sk3", [NSEQ, NHEAD, HD, T], FP16)
    vtm_o = k.out("vtm3", [NSEQ, 128, NTC * NHEAD * 65], BF16)
    out0_o = k.out("out0", [NSEQ, D, T], BF16)
    sq_o = k.out("sq3", [NSEQ, NHEAD, HD, T], FP16) if DBG else None
    qc1_o = k.out("qc1", [NSEQ, D, T], BF16)
    d0_o = k.out("d0_part", [1, 1])

    xn = k.sbuf("xn", [128, NCH * T], BF16)
    lna_c = k.sbuf("lna_c", [128, NCH])
    lnb_c = k.sbuf("lnb_c", [128, NCH])
    bqh = k.sbuf("bqh", [HD, NHEAD])
    bvh = k.sbuf("bvh", [HD, NHEAD])
    lncc = k.sbuf("lncc", [HD, 1])
    lndc = k.sbuf("lndc", [HD, 1])
    lqTb = k.sbuf("lqTb", [HD, HD], BF16)
    lkTb = k.sbuf("lkTb", [HD, HD], BF16)
    lvTb = k.sbuf("lvTb", [HD, HD], BF16)
    lqbc = k.sbuf("lqbc", [HD, 1])
    lkbc = k.sbuf("lkbc", [HD, 1])
    lvbb = k.sbuf("lvbb", [128, HD])
    sc3s = k.sbuf("sc3s", [1, 3])
    tsc0 = k.sbuf("tsc0", [1, 1])
    tsa = k.sbuf("tsa", [1, 1])
    tsb = k.sbuf("tsb", [1, 1])
    sqsc_col = k.sbuf("sqsc_col", [HD, 1])
    acc64 = k.sbuf("acc64", [HD, 1])
    d0sb = k.sbuf("d0sb", [1, 1])
    k.dma(lna_c, d["lna_c"])
    k.dma(lnb_c, d["lnb_c"])
    k.dma(bqh, bqh_d)
    k.dma(bvh, bvh_d)
    k.dma(lncc, d["lnc"])
    k.dma(lndc, d["lnd"])
    k.dma(lqTb, d["lqT"], eng="gpsimd")
    k.dma(lkTb, d["lkT"], eng="gpsimd")
    k.dma(lvTb, d["lvT"], eng="gpsimd")
    k.dma(lqbc, d["lqb"])
    k.dma(lkbc, d["lkb"])
    k.dma(sc3s, d["sc3"])
    psI = k.psum("psI", [128, HD])
    lvbrow = k.sbuf("lvbrow", [1, HD])
    k.dma(lvbrow, d["lvb"])
    k.mm(psI, c["ones_row"], lvbrow)
    k.copy(lvbb, psI)
    emit_tscale(k, c, sc3s[:, 1:2], tsc0, tsa, tsb)
    k.mm(psI[0:HD, 0:1], c["ones_row"][:, 0:HD], tsc0)
    k.copy(sqsc_col, psI[0:HD, 0:1])
    k.ts(sqsc_col, sqsc_col, 0.125, None, ALU.mult)
    k.memset(acc64, 0.0)

    for s in range(NSEQ):
        emit_ln(k, c, s, d["x_in"], xn)
        for ci in range(NCH):
            k.dma(xn3_o[s, ci * 128:(ci + 1) * 128, :], fmv(xn, ci))
        with k.scope() as sc:
            xna = sc.sbuf("xna", [128, NCH * T], BF16)
            xnb = sc.sbuf("xnb", [128, NCH * T], BF16)
            emit_ln_inner(k, c, xn, xna, lna_c, sc)
            emit_ln_inner(k, c, xn, xnb, lnb_c, sc)
            vtm = sc.sbuf("vtm", [128, NTC * NHEAD * 65], BF16)
            k.memset(vtm, 1.0)
            wqh = sc.sbuf("wqh", [128, NCH * HD], BF16)
            wkh = sc.sbuf("wkh", [128, NCH * HD], BF16)
            wvh = sc.sbuf("wvh", [128, NCH * HD], BF16)
            qhb = sc.sbuf("qhb", [HD, T], BF16)
            khb = sc.sbuf("khb", [HD, T], BF16)
            vhb = sc.sbuf("vhb", [HD, T], BF16)
            kpf = sc.sbuf("kpf", [HD, T])
            kaug = sc.sbuf("kaug", [128, T], FP16)
            qaug = sc.sbuf("qaug", [128, T], FP16)
            sqsc = sc.sbuf("sqsc", [64, TT])
            rows4 = sc.sbuf("rows4", [1, 4 * TT])
            rowm = sc.sbuf("rowm", [1, T])
            sc1 = sc.sbuf("sc1", [1, 1])
            sc2 = sc.sbuf("sc2", [1, 1])
            mcol = sc.sbuf("mcol", [128, 1])
            srow = sc.sbuf("srow", [128, TT])
            srow0 = sc.sbuf("srow0", [1, TT])
            bcs = sc.sbuf("bcs", [64, TT])
            o0b = sc.sbuf("o0b", [64, TT], BF16)
            accq = sc.sbuf("accq", [64, 1])
            pt = sc.sbuf("pt", [128, TT], BF16)
            psS = sc.psum("psS", [128, TT])
            psA = sc.psum("psA", [128, TT])
            psR = sc.psum("psR", [1, TT])
            psB = sc.psum("psB", [128, 1])
            psH = sc.psum("psH", [64, TT])
            psV = sc.psum("psV", [128, HD])
            k.memset(kaug[64:66, :], 0.0)
            k.memset(qaug[64:66, :], 1.0)
            attn_bufs = (sqsc, rowm, sc1, sc2, mcol, srow, srow0, bcs, pt,
                         psS, psA, psR, psB, None)
            hl_bufs = (sqsc, rows4, psR, psS)
            for h in range(NHEAD):
                for dc in range(NCH):
                    k.dma(wqh[:, dc * HD:(dc + 1) * HD],
                          d["wqT"][dc * 128:(dc + 1) * 128, h * HD:(h + 1) * HD],
                          eng="gpsimd")
                    k.dma(wkh[:, dc * HD:(dc + 1) * HD],
                          d["wkT"][dc * 128:(dc + 1) * 128, h * HD:(h + 1) * HD],
                          eng="gpsimd")
                    k.dma(wvh[:, dc * HD:(dc + 1) * HD],
                          d["wvT"][dc * 128:(dc + 1) * 128, h * HD:(h + 1) * HD],
                          eng="gpsimd")
                for tt_ in range(NTT):
                    t0 = tt_ * TT
                    for dc in range(NCH):
                        k.mm(psH, wqh[:, dc * HD:(dc + 1) * HD], fmv(xna, dc, t0, TT),
                             start=(dc == 0), stop=(dc == NCH - 1))
                    k.ts(qhb[:, t0:t0 + TT], psH, bqh[:, h:h + 1], SC64, ALU.add, ALU.mult)
                    for dc in range(NCH):
                        k.mm(psH, wkh[:, dc * HD:(dc + 1) * HD], fmv(xnb, dc, t0, TT),
                             start=(dc == 0), stop=(dc == NCH - 1))
                    k.ts(khb[:, t0:t0 + TT], psH, SC64, None, ALU.mult)
                    for dc in range(NCH):
                        k.mm(psH, wvh[:, dc * HD:(dc + 1) * HD], fmv(xnb, dc, t0, TT),
                             start=(dc == 0), stop=(dc == NCH - 1))
                    k.ts(vhb[:, t0:t0 + TT], psH, bvh[:, h:h + 1], None, ALU.add)
                for tc in range(NTC):
                    k.mm(psV, vhb[:, tc * 128:(tc + 1) * 128], lvTb)
                    k.tt(srow[:, 0:HD], psV, lvbb, ALU.add)
                    k.copy(vtm[:, (tc * NHEAD + h) * 65:(tc * NHEAD + h) * 65 + 64],
                           srow[:, 0:HD])
                for tt_ in range(NTT):
                    t0 = tt_ * TT
                    k.mm(psH, lkTb, khb[:, t0:t0 + TT])
                    k.ts(kpf[:, t0:t0 + TT], psH, lkbc, None, ALU.add)
                emit_headln(k, c, kpf, kaug[0:64, :], lndc, hl_bufs)
                k.dma(sk_o[s, h], kaug[0:64, :])
                for tt_ in range(NTT):
                    t0 = tt_ * TT
                    k.mm(psH, lqTb, qhb[:, t0:t0 + TT])
                    k.ts(kpf[:, t0:t0 + TT], psH, lqbc, None, ALU.add)
                emit_headln(k, c, kpf, qaug[0:64, :], lncc, hl_bufs, scale_col=sqsc_col)
                if DBG:
                    k.dma(sq_o[s, h], qaug[0:64, :])

                def dst(qt, bcs_ap, h=h, s=s):
                    q0 = qt * TT
                    r0 = h * HD
                    k.copy(o0b, bcs_ap)
                    k.dma(out0_o[s, r0:r0 + HD, q0:q0 + TT], o0b)
                    k.tt(o0b, bcs_ap, qhb[:, q0:q0 + TT], ALU.add)
                    k.dma(qc1_o[s, r0:r0 + HD, q0:q0 + TT], o0b)
                    k.red(accq, bcs_ap, ALU.add, absval=True)
                    k.tt(acc64, acc64, accq, ALU.add)
                emit_l3_head_attn(k, c, attn_bufs, kaug, qaug,
                                  lambda kc, h=h: vtm[:, (kc * NHEAD + h) * 65:
                                                      (kc * NHEAD + h) * 65 + 65],
                                  dst, None)
            k.dma(vtm_o[s], vtm)
    psd = k.psum("psd", [1, 1])
    k.mm(psd, c["ones_col"][0:64, :], acc64)
    k.ts(d0sb, psd, 1.0 / CNT_L3, None, ALU.mult)
    k.dma(d0_o, d0sb)
    k.p.finalize()
    return k





def emit_l3_scalars_cw1(k, c, d0row, sc3s, cw1, tmp):
    k.red(tmp["d0"], d0row, ALU.add)
    m1 = tmp["a"]
    ge = tmp["b"]
    k.ts(m1, tmp["d0"], 0.05, None, ALU.is_gt)
    k.ts(ge, tmp["d0"], 0.001, None, ALU.is_ge)
    k.ts(tmp["c"], ge, 205.0, 1843.0, ALU.mult, ALU.add)
    k.ts(tmp["d"], m1, -1.0, 1.0, ALU.mult, ALU.add)
    k.tt(tmp["c"], tmp["c"], tmp["d"], ALU.mult)
    k.ts(cw1, m1, 2048.0, None, ALU.mult)
    k.tt(cw1, cw1, tmp["c"], ALU.add)


def emit_l3_iter_prep(k, c, cwv, tscv, iota, sqsc_col, mb_row, qm_row, psI):
    k.ts(mb_row, iota, cwv, None, ALU.is_ge)
    k.ts(qm_row, mb_row, -1.0, 1.0, ALU.mult, ALU.add)
    k.ts(mb_row, mb_row, -60000.0, None, ALU.mult)
    k.mm(psI[0:HD, 0:1], c["ones_row"][:, 0:HD], tscv)
    k.copy(sqsc_col, psI[0:HD, 0:1])
    k.ts(sqsc_col, sqsc_col, 0.125, None, ALU.mult)


def build_L3b():
    k = K()
    c = load_consts(k)
    d0r_d = k.inp("d0_all", [1, 8])
    qc1_d = k.inp("qc1", [NSEQ, D, T], BF16)
    sk_d = k.inp("sk3", [NSEQ, NHEAD, HD, T], FP16)
    vtm_d = k.inp("vtm3", [NSEQ, 128, NTC * NHEAD * 65], BF16)
    out0_d = k.inp("out0", [NSEQ, D, T], BF16)
    lqT_d = k.inp("l3_lqT", [HD, HD])
    lqb_d = k.inp("l3_lqb", [HD, 1])
    lnc_d = k.inp("l3_lnc", [HD, 1])
    sc3_d = k.inp("l3_sc3", [1, 3])
    iota_d = k.inp("iota_row", [1, T])
    out1_o = k.out("out1", [NSEQ, D, T], BF16)
    d1_o = k.out("d1_part", [1, 1])

    lqTb = k.sbuf("lqTb", [HD, HD], BF16)
    lqbc = k.sbuf("lqbc", [HD, 1])
    lncc = k.sbuf("lncc", [HD, 1])
    sc3s = k.sbuf("sc3s", [1, 3])
    d0row = k.sbuf("d0row", [1, 8])
    iota = k.sbuf("iota", [1, T])
    mb_row = k.sbuf("mb_row", [1, T])
    qm_row = k.sbuf("qm_row", [1, T])
    cw1 = k.sbuf("cw1", [1, 1])
    tsc1 = k.sbuf("tsc1", [1, 1])
    sqsc_col = k.sbuf("sqsc_col", [HD, 1])
    acc64 = k.sbuf("acc64", [HD, 1])
    d1sb = k.sbuf("d1sb", [1, 1])
    tmp = {n: k.sbuf(f"tmp_{n}", [1, 1]) for n in ("d0", "a", "b", "c", "d", "e")}
    k.dma(lqTb, lqT_d, eng="gpsimd")
    k.dma(lqbc, lqb_d)
    k.dma(lncc, lnc_d)
    k.dma(sc3s, sc3_d)
    k.dma(d0row, d0r_d)
    k.dma(iota, iota_d)
    psI = k.psum("psI", [128, HD])
    emit_l3_scalars_cw1(k, c, d0row, sc3s, cw1, tmp)
    k.ts(tmp["e"], sc3s[:, 1:2], 0.005, None, ALU.add)
    emit_tscale(k, c, tmp["e"], tsc1, tmp["a"], tmp["b"])
    emit_l3_iter_prep(k, c, cw1, tsc1, iota, sqsc_col, mb_row, qm_row, psI)
    k.memset(acc64, 0.0)

    for s in range(NSEQ):
        with k.scope() as sc:
            vtm = sc.sbuf("vtm", [128, NTC * NHEAD * 65], BF16)
            k.dma(vtm, vtm_d[s])
            qhb = sc.sbuf("qhb", [HD, T], BF16)
            kpf = sc.sbuf("kpf", [HD, T])
            kaug = sc.sbuf("kaug", [128, T], FP16)
            qaug = sc.sbuf("qaug", [128, T], FP16)
            o0t = sc.sbuf("o0t", [HD, TT], BF16)
            o1b = sc.sbuf("o1b", [HD, TT], BF16)
            dtl = sc.sbuf("dtl", [HD, TT])
            sqsc = sc.sbuf("sqsc", [64, TT])
            rows4 = sc.sbuf("rows4", [1, 4 * TT])
            rowm = sc.sbuf("rowm", [1, T])
            sc1 = sc.sbuf("sc1", [1, 1])
            sc2 = sc.sbuf("sc2", [1, 1])
            mcol = sc.sbuf("mcol", [128, 1])
            srow = sc.sbuf("srow", [128, TT])
            srow0 = sc.sbuf("srow0", [1, TT])
            bcs = sc.sbuf("bcs", [64, TT])
            accq = sc.sbuf("accq", [64, 1])
            pt = sc.sbuf("pt", [128, TT], BF16)
            psS = sc.psum("psS", [128, TT])
            psA = sc.psum("psA", [128, TT])
            psR = sc.psum("psR", [1, TT])
            psB = sc.psum("psB", [128, 1])
            psH = sc.psum("psH", [64, TT])
            k.memset(qaug[64:66, :], 1.0)
            attn_bufs = (sqsc, rowm, sc1, sc2, mcol, srow, srow0, bcs, pt,
                         psS, psA, psR, psB, None)
            hl_bufs = (sqsc, rows4, psR, psS)
            for h in range(NHEAD):
                k.dma(kaug[0:64, :], sk_d[s, h])
                k.copy(kaug[64:65, :], mb_row)
                k.dma(qhb, qc1_d[s, h * HD:(h + 1) * HD, :])
                for tt_ in range(NTT):
                    t0 = tt_ * TT
                    k.mm(psH, lqTb, qhb[:, t0:t0 + TT])
                    k.ts(kpf[:, t0:t0 + TT], psH, lqbc, None, ALU.add)
                emit_headln(k, c, kpf, qaug[0:64, :], lncc, hl_bufs, scale_col=sqsc_col)

                def dst(qt, bcs_ap, h=h, s=s):
                    q0 = qt * TT
                    r0 = h * HD
                    k.copy(o1b, bcs_ap)
                    k.dma(out1_o[s, r0:r0 + HD, q0:q0 + TT], o1b)
                    k.dma(o0t, out0_d[s, r0:r0 + HD, q0:q0 + TT])
                    k.tt(dtl, bcs_ap, o0t, ALU.subtract)
                    k.red(accq, dtl, ALU.add, absval=True)
                    k.tt(acc64, acc64, accq, ALU.add)
                emit_l3_head_attn(k, c, attn_bufs, kaug, qaug,
                                  lambda kc, h=h: vtm[:, (kc * NHEAD + h) * 65:
                                                      (kc * NHEAD + h) * 65 + 65],
                                  dst, qm_row)
    psd = k.psum("psd", [1, 1])
    k.mm(psd, c["ones_col"][0:64, :], acc64)
    k.ts(d1sb, psd, 1.0 / CNT_L3, None, ALU.mult)
    k.dma(d1_o, d1sb)
    k.p.finalize()
    return k


def build_L3c():
    k = K()
    c = load_consts(k)
    d0r_d = k.inp("d0_all", [1, 8])
    d1r_d = k.inp("d1_all", [1, 8])
    x_in = k.inp("x_in", [NSEQ, D, T])
    qc1_d = k.inp("qc1", [NSEQ, D, T], BF16)
    out1_d = k.inp("out1", [NSEQ, D, T], BF16)
    sk_d = k.inp("sk3", [NSEQ, NHEAD, HD, T], FP16)
    vtm_d = k.inp("vtm3", [NSEQ, 128, NTC * NHEAD * 65], BF16)
    xn3_d = k.inp("xn3", [NSEQ, D, T], BF16)
    lqT_d = k.inp("l3_lqT", [HD, HD])
    lqb_d = k.inp("l3_lqb", [HD, 1])
    lnc_d = k.inp("l3_lnc", [HD, 1])
    sc3_d = k.inp("l3_sc3", [1, 3])
    iota_d = k.inp("iota_row", [1, T])
    woT_d = k.inp("l3_woT", [D, D])
    boc_d = k.inp("l3_bo_c", [128, NCH])
    wg_d = k.inp("l3_wgate", [D, NEXP])
    bg_d = k.inp("l3_bgate", [NEXP, 1])
    W1_d = k.inp("l3_W1", [NEXP, D, DH])
    b1c_d = k.inp("l3_b1_cols", [NEXP, 128, NCHH])
    W2_d = k.inp("l3_W2", [NEXP, DH, D])
    b2c_d = k.inp("l3_b2_cols", [NEXP, 128, NCH])
    triu_d = k.inp("triu", [128, 128])
    iota8_d = k.inp("iota8", [128, NEXP])
    x_out = k.out("x_out", [NSEQ, D, T])
    s_dram = k.dram("s_scr", [NSEQ, D, T], BF16)
    packx_d = k.dram("packx", [NEXP * CAP + 128, D], BF16)
    packy_d = k.dram("packy", [NEXP * CAP + 128, D], BF16)

    lqTb = k.sbuf("lqTb", [HD, HD], BF16)
    lqbc = k.sbuf("lqbc", [HD, 1])
    lncc = k.sbuf("lncc", [HD, 1])
    sc3s = k.sbuf("sc3s", [1, 3])
    d0row = k.sbuf("d0row", [1, 8])
    d1row = k.sbuf("d1row", [1, 8])
    iota = k.sbuf("iota", [1, T])
    mb_row = k.sbuf("mb_row", [1, T])
    qm_row = k.sbuf("qm_row", [1, T])
    cw1 = k.sbuf("cw1", [1, 1])
    cw2 = k.sbuf("cw2", [1, 1])
    tsc2 = k.sbuf("tsc2", [1, 1])
    sqsc_col = k.sbuf("sqsc_col", [HD, 1])
    c1s = k.sbuf("c1s", [1, 1])
    u1s = k.sbuf("u1s", [1, 1])
    u1col = k.sbuf("u1col", [HD, 1])
    c1col = k.sbuf("c1col", [HD, 1])
    boc = k.sbuf("boc", [128, NCH])
    tmp = {n: k.sbuf(f"tmp_{n}", [1, 1]) for n in ("d0", "a", "b", "c", "d", "e", "f", "g")}
    k.dma(lqTb, lqT_d, eng="gpsimd")
    k.dma(lqbc, lqb_d)
    k.dma(lncc, lnc_d)
    k.dma(sc3s, sc3_d)
    k.dma(d0row, d0r_d)
    k.dma(d1row, d1r_d)
    k.dma(iota, iota_d)
    k.dma(boc, boc_d)
    psI = k.psum("psI", [128, HD])
    emit_l3_scalars_cw1(k, c, d0row, sc3s, cw1, tmp)
    d1 = tmp["f"]
    k.red(d1, d1row, ALU.add)
    k.ts(tmp["a"], d1, sc3s[:, 2:3], None, ALU.mult)
    k.tt(tmp["a"], tmp["a"], sc3s[:, 0:1], ALU.add)
    k.ts(u1s, d1, tmp["a"], None, ALU.is_ge)
    k.ts(c1s, u1s, -1.0, 1.0, ALU.mult, ALU.add)
    k.ts(tmp["a"], cw1, 1.1, -0.5, ALU.mult, ALU.add)
    k.ts(tmp["a"], tmp["a"], RTM, None, ALU.add)
    k.ts(tmp["a"], tmp["a"], RTM, None, ALU.subtract)
    k.ts(tmp["b"], cw1, 0.9, -0.5, ALU.mult, ALU.add)
    k.ts(tmp["b"], tmp["b"], RTM, None, ALU.add)
    k.ts(tmp["b"], tmp["b"], RTM, None, ALU.subtract)
    k.ts(tmp["c"], d1, 0.05, None, ALU.is_gt)
    k.ts(tmp["d"], d1, 0.001, None, ALU.is_ge)
    k.ts(tmp["e"], cw1, 64.0, None, ALU.is_gt)
    k.ts(tmp["d"], tmp["d"], -1.0, 1.0, ALU.mult, ALU.add)
    k.tt(tmp["d"], tmp["d"], tmp["e"], ALU.mult)
    k.tt(tmp["b"], tmp["b"], tmp["d"], ALU.mult)
    k.ts(tmp["d"], tmp["d"], -1.0, 1.0, ALU.mult, ALU.add)
    k.ts(tmp["g"], cw1, 1.0, None, ALU.mult)
    k.tt(tmp["g"], tmp["g"], tmp["d"], ALU.mult)
    k.tt(tmp["b"], tmp["b"], tmp["g"], ALU.add)
    k.tt(tmp["a"], tmp["a"], tmp["c"], ALU.mult)
    k.ts(tmp["c"], tmp["c"], -1.0, 1.0, ALU.mult, ALU.add)
    k.tt(tmp["b"], tmp["b"], tmp["c"], ALU.mult)
    k.tt(tmp["a"], tmp["a"], tmp["b"], ALU.add)
    k.ts(tmp["a"], tmp["a"], 64.0, 2048.0, ALU.max, ALU.min)
    k.tt(tmp["a"], tmp["a"], u1s, ALU.mult)
    k.ts(tmp["b"], cw1, 1.0, None, ALU.mult)
    k.tt(tmp["b"], tmp["b"], c1s, ALU.mult)
    k.tt(cw2, tmp["a"], tmp["b"], ALU.add)
    k.ts(tmp["e"], u1s, 0.005, None, ALU.mult)
    k.ts(tmp["g"], sc3s[:, 1:2], 0.005, None, ALU.add)
    k.tt(tmp["g"], tmp["g"], tmp["e"], ALU.add)
    emit_tscale(k, c, tmp["g"], tsc2, tmp["a"], tmp["b"])
    emit_l3_iter_prep(k, c, cw2, tsc2, iota, sqsc_col, mb_row, qm_row, psI)
    k.mm(psI[0:HD, 0:1], c["ones_row"][:, 0:HD], u1s)
    k.copy(u1col, psI[0:HD, 0:1])
    k.mm(psI[0:HD, 0:1], c["ones_row"][:, 0:HD], c1s)
    k.copy(c1col, psI[0:HD, 0:1])

    for s in range(NSEQ):
        with k.scope() as sc:
            vtm = sc.sbuf("vtm", [128, NTC * NHEAD * 65], BF16)
            k.dma(vtm, vtm_d[s])
            attnall = sc.sbuf("attnall", [128, NCH * T], BF16)
            qhb = sc.sbuf("qhb", [HD, T], BF16)
            kpf = sc.sbuf("kpf", [HD, T])
            kaug = sc.sbuf("kaug", [128, T], FP16)
            qaug = sc.sbuf("qaug", [128, T], FP16)
            o1t = sc.sbuf("o1t", [HD, T], BF16)
            sqsc = sc.sbuf("sqsc", [64, TT])
            rows4 = sc.sbuf("rows4", [1, 4 * TT])
            rowm = sc.sbuf("rowm", [1, T])
            sc1 = sc.sbuf("sc1", [1, 1])
            sc2 = sc.sbuf("sc2", [1, 1])
            mcol = sc.sbuf("mcol", [128, 1])
            srow = sc.sbuf("srow", [128, TT])
            srow0 = sc.sbuf("srow0", [1, TT])
            bcs = sc.sbuf("bcs", [64, TT])
            att = sc.sbuf("att", [64, TT], BF16)
            pt = sc.sbuf("pt", [128, TT], BF16)
            wqs = sc.sbuf("wqs", [128, 1024], BF16)
            psS = sc.psum("psS", [128, TT])
            psA = sc.psum("psA", [128, TT])
            psR = sc.psum("psR", [1, TT])
            psB = sc.psum("psB", [128, 1])
            psH = sc.psum("psH", [64, TT])
            psQ = sc.psum("psQ", [128, TT])
            k.memset(qaug[64:66, :], 1.0)
            attn_bufs = (sqsc, rowm, sc1, sc2, mcol, srow, srow0, bcs, pt,
                         psS, psA, psR, psB, None)
            hl_bufs = (sqsc, rows4, psR, psS)
            for h in range(NHEAD):
                k.dma(kaug[0:64, :], sk_d[s, h])
                k.copy(kaug[64:65, :], mb_row)
                k.dma(qhb, qc1_d[s, h * HD:(h + 1) * HD, :])
                k.dma(o1t, out1_d[s, h * HD:(h + 1) * HD, :])
                k.stt(qhb, o1t, u1col, qhb, ALU.mult, ALU.add)
                for tt_ in range(NTT):
                    t0 = tt_ * TT
                    k.mm(psH, lqTb, qhb[:, t0:t0 + TT])
                    k.ts(kpf[:, t0:t0 + TT], psH, lqbc, None, ALU.add)
                emit_headln(k, c, kpf, qaug[0:64, :], lncc, hl_bufs, scale_col=sqsc_col)

                def dst(qt, bcs_ap, h=h, s=s):
                    q0 = qt * TT
                    k.ts(bcs_ap, bcs_ap, u1col, None, ALU.mult)
                    k.stt(att, o1t[:, q0:q0 + TT], c1col, bcs_ap, ALU.mult, ALU.add)
                    ci, off = h // 2, (h % 2) * 64
                    k.copy(fmv(attnall, ci, q0, TT)[off:off + 64, :], att)
                emit_l3_head_attn(k, c, attn_bufs, kaug, qaug,
                                  lambda kc, h=h: vtm[:, (kc * NHEAD + h) * 65:
                                                      (kc * NHEAD + h) * 65 + 65],
                                  dst, qm_row)
            for oc in range(NCH):
                for dc in range(NCH):
                    k.dma(wqs[:, dc * 128:(dc + 1) * 128],
                          woT_d[dc * 128:(dc + 1) * 128, oc * 128:(oc + 1) * 128],
                          eng="gpsimd")
                for tt_ in range(NTT):
                    t0 = tt_ * TT
                    for dc in range(NCH):
                        k.mm(psQ, wqs[:, dc * 128:(dc + 1) * 128], fmv(attnall, dc, t0, TT),
                             start=(dc == 0), stop=(dc == NCH - 1))
                    k.ts(srow, psQ, boc[:, oc:oc + 1], None, ALU.add)
                    k.dma(s_dram[s, oc * 128:(oc + 1) * 128, t0:t0 + TT], srow,
                          eng="gpsimd")

    idx_all = k.sbuf("idx_all", [128, NSEQ * NTC])
    rp_all = k.sbuf("rp_all", [128, NSEQ * NTC])
    desti = k.sbuf("desti", [128, NSEQ * NTC], I32)
    identf = k.sbuf("identf", [128, 128])
    idf_d = k.inp("ident", [128, 128])
    k.dma(identf, idf_d)
    onesr32 = k.sbuf("onesr32", [1, NSEQ * NTC])
    k.memset(onesr32, 1.0)
    NT2 = NSEQ * NTC
    DUMP = float(NEXP * CAP)

    with k.scope() as sc:
        xn = sc.sbuf("xnm", [128, NCH * T], BF16)
        wgb = sc.sbuf("wgb", [128, NCH * NEXP], BF16)
        bgc = sc.sbuf("bgc", [NEXP, 1])
        lsb = sc.sbuf("lsb", [NEXP, T])
        ltm = sc.sbuf("ltm", [128, NEXP])
        iota8c = sc.sbuf("iota8c", [128, NEXP])
        triu = sc.sbuf("triu", [128, 128])
        rmx = sc.sbuf("rmx", [128, 1])
        nmx = sc.sbuf("nmx", [128, 1])
        exps = sc.sbuf("exps", [128, NEXP])
        sume = sc.sbuf("sume", [128, 1])
        mio = sc.sbuf("mio", [128, NEXP])
        mask_e = sc.sbuf("mask_e", [128, NT2])
        trow = sc.sbuf("trow", [1, NT2])
        prow = sc.sbuf("prow", [1, NT2])
        t1m = sc.sbuf("t1m", [128, NT2])
        t2m = sc.sbuf("t2m", [128, NT2])
        dacc = sc.sbuf("dacc", [128, NT2])
        xtmo = sc.sbuf("xtmo", [128, D], BF16)
        psG = sc.psum("psG", [NEXP, TT])
        psT = sc.psum("psT2", [128, NEXP])
        psP2 = sc.psum("psP2", [128, NT2])
        psTt = sc.psum("psTt", [1, NT2])
        pstr = sc.psum("pstr", [128, 128], BF16)
        k.dma(bgc, bg_d)
        k.dma(iota8c, iota8_d)
        k.dma(triu, triu_d)
        for dc in range(NCH):
            k.dma(wgb[:, dc * NEXP:(dc + 1) * NEXP], wg_d[dc * 128:(dc + 1) * 128, :],
                  eng="gpsimd")
        for s in range(NSEQ):
            for ci in range(NCH):
                k.dma(fmv(xn, ci), xn3_d[s, ci * 128:(ci + 1) * 128, :])
            for tt_ in range(NTT):
                t0 = tt_ * TT
                for dc in range(NCH):
                    k.mm(psG, wgb[:, dc * NEXP:(dc + 1) * NEXP], fmv(xn, dc, t0, TT),
                         start=(dc == 0), stop=(dc == NCH - 1))
                k.ts(lsb[:, t0:t0 + TT], psG, bgc, None, ALU.add)
            for tc in range(NTC):
                col = s * NTC + tc
                k.tr(psT, lsb[:, tc * 128:(tc + 1) * 128], identf[0:NEXP, 0:NEXP])
                k.copy(ltm, psT)
                k.red(rmx, ltm, ALU.max)
                k.ts(nmx, rmx, -1.0, None, ALU.mult)
                k.act(exps, ltm, AF.Exp, bias=nmx)
                k.red(sume, exps, ALU.add)
                k.recip(sume, sume)
                k.copy(rp_all[:, col:col + 1], sume)
                k.ts(mio, ltm, rmx, None, ALU.is_ge)
                k.stt(mio, mio, -1e6, iota8c, ALU.mult, ALU.add)
                k.red(idx_all[:, col:col + 1], mio, ALU.min)
        k.memset(dacc, 0.0)
        for e in range(NEXP):
            k.ts(mask_e, idx_all, float(e), None, ALU.is_equal)
            k.mm(psP2, triu, mask_e)
            k.mm(psTt, c["ones_col"], mask_e)
            k.copy(trow, psTt)
            k.scan(prow, onesr32, trow, 0.0)
            k.tt(prow, prow, trow, ALU.subtract)
            k.copy(t1m, psP2)
            k.mm(psP2, c["ones_row"], prow)
            k.tt(t1m, t1m, psP2, ALU.add)
            k.ts(t1m, t1m, -1.0, None, ALU.add)
            k.ts(t2m, t1m, float(CAP), None, ALU.is_ge)
            k.ts(t1m, t1m, float(e * CAP), None, ALU.add)
            k.ts(t2m, t2m, -1.0, 1.0, ALU.mult, ALU.add)
            k.tt(t1m, t1m, t2m, ALU.mult)
            k.ts(t2m, t2m, -DUMP, DUMP, ALU.mult, ALU.add)
            k.tt(t1m, t1m, t2m, ALU.add)
            k.tt(t1m, t1m, mask_e, ALU.mult)
            k.tt(dacc, dacc, t1m, ALU.add)
        k.copy(desti, dacc)
        for s in range(NSEQ):
            for ci in range(NCH):
                k.dma(fmv(xn, ci), xn3_d[s, ci * 128:(ci + 1) * 128, :])
            for tc in range(NTC):
                for ci in range(NCH):
                    k.tr(pstr, fmv(xn, ci, tc * 128, 128), c["identb"])
                    k.copy(xtmo[:, ci * 128:(ci + 1) * 128], pstr)
                k.idma(packx_d, (desti[:, s * NTC + tc: s * NTC + tc + 1], 0),
                       xtmo, None)

    with k.scope() as sc:
        xpf = sc.sbuf("xpf", [128, NCH * CAP], BF16)
        xptm = sc.sbuf("xptm", [128, D], BF16)
        hb = sc.sbuf("hb", [128, NCHH * TT], BF16)
        w1sl = sc.sbuf("w1sl", [128, NCH * 128], BF16)
        w2sl = sc.sbuf("w2sl", [128, 4 * 128], BF16)
        b1ce = sc.sbuf("b1ce", [128, NCHH])
        b2ce = sc.sbuf("b2ce", [128, NCH])
        ytile = sc.sbuf("ytile", [128, NCH * TT], BF16)
        ytmo = sc.sbuf("ytmo", [128, D], BF16)
        psF = sc.psum("psF", [128, TT])
        psE = [sc.psum(f"psE{j}", [128, TT]) for j in range(4)]
        pstr2 = sc.psum("pstr2", [128, 128], BF16)
        for e in range(NEXP):
            k.dma(b1ce, b1c_d[e])
            k.dma(b2ce, b2c_d[e])
            for it in range(CAP // 128):
                k.dma(xptm, packx_d[e * CAP + it * 128: e * CAP + (it + 1) * 128, :])
                for ci in range(NCH):
                    k.tr(pstr2, xptm[:, ci * 128:(ci + 1) * 128], c["identb"])
                    k.copy(xpf[:, ci * CAP + it * 128: ci * CAP + it * 128 + 128], pstr2)
            for ct in range(CAP // TT):
                c0 = ct * TT
                for hc in range(NCHH):
                    for dc in range(NCH):
                        k.dma(w1sl[:, dc * 128:(dc + 1) * 128],
                              W1_d[e, dc * 128:(dc + 1) * 128, hc * 128:(hc + 1) * 128],
                              eng="gpsimd")
                    for dc in range(NCH):
                        k.mm(psF, w1sl[:, dc * 128:(dc + 1) * 128],
                             xpf[:, dc * CAP + c0: dc * CAP + c0 + TT],
                             start=(dc == 0), stop=(dc == NCH - 1))
                    k.act(hb[:, hc * TT:(hc + 1) * TT], psF, AF.Relu,
                          bias=b1ce[:, hc:hc + 1])
                for g2 in range(2):
                    for hc in range(NCHH):
                        k.dma(w2sl, W2_d[e, hc * 128:(hc + 1) * 128,
                                         g2 * 512:(g2 + 1) * 512], eng="gpsimd")
                        for j in range(4):
                            k.mm(psE[j], w2sl[:, j * 128:(j + 1) * 128],
                                 hb[:, hc * TT:(hc + 1) * TT],
                                 start=(hc == 0), stop=(hc == NCHH - 1))
                    for j in range(4):
                        dc = g2 * 4 + j
                        k.ts(ytile[:, dc * TT:(dc + 1) * TT], psE[j],
                             b2ce[:, dc:dc + 1], None, ALU.add)
                for tc2 in range(TT // 128):
                    for ci in range(NCH):
                        k.tr(pstr2, ytile[:, ci * TT + tc2 * 128: ci * TT + tc2 * 128 + 128],
                             c["identb"])
                        k.copy(ytmo[:, ci * 128:(ci + 1) * 128], pstr2)
                    r0 = e * CAP + c0 + tc2 * 128
                    k.dma(packy_d[r0:r0 + 128, :], ytmo)

    with k.scope() as sc:
        ytm = sc.sbuf("ytm", [128, D], BF16)
        ytms = sc.sbuf("ytms", [128, D], BF16)
        gt2 = sc.sbuf("gt2", [128, 128], BF16)
        st3 = sc.sbuf("st3", [128, 128])
        xt3 = sc.sbuf("xt3", [128, 128])
        pstr3 = sc.psum("pstr3", [128, 128], BF16)
        for s in range(NSEQ):
            for tc in range(NTC):
                col = s * NTC + tc
                k.idma(ytm, None, packy_d, (desti[:, col:col + 1], 0))
                k.ts(ytms, ytm, rp_all[:, col:col + 1], None, ALU.mult)
                k.act(ytm, ytms, AF.Silu)
                for ci in range(NCH):
                    k.tr(pstr3, ytm[:, ci * 128:(ci + 1) * 128], c["identb"])
                    k.copy(gt2, pstr3)
                    k.dma(st3, s_dram[s, ci * 128:(ci + 1) * 128, tc * 128:(tc + 1) * 128],
                          eng="gpsimd")
                    k.dma(xt3, x_in[s, ci * 128:(ci + 1) * 128, tc * 128:(tc + 1) * 128])
                    k.tt(st3, st3, gt2, ALU.mult)
                    k.tt(xt3, xt3, st3, ALU.add)
                    k.dma(x_out[s, ci * 128:(ci + 1) * 128, tc * 128:(tc + 1) * 128], xt3)

    k.p.finalize()
    return k

def _f32c(a):
    return np.ascontiguousarray(np.asarray(a, dtype=np.float32))


def tf_mask_bits(n, k1=0, k2=42):
    old = np.seterr(over="ignore")
    i = np.arange(n, dtype=np.uint64)
    x0 = (i >> np.uint64(32)).astype(np.uint32)
    x1 = i.astype(np.uint32)
    ks0 = np.uint32(k1)
    ks1 = np.uint32(k2)
    ks2 = np.uint32(0x1BD11BDA) ^ ks0 ^ ks1

    def rotl(x, d):
        return (x << np.uint32(d)) | (x >> np.uint32(32 - d))
    x0 = x0 + ks0
    x1 = x1 + ks1
    ks = (ks0, ks1, ks2)
    for r in range(5):
        rots = (13, 15, 26, 6) if r % 2 == 0 else (17, 29, 16, 24)
        for d in rots:
            x0 = x0 + x1
            x1 = rotl(x1, d) ^ x0
        x0 = x0 + ks[(r + 1) % 3]
        x1 = x1 + ks[(r + 2) % 3] + np.uint32(r + 1)
    np.seterr(**old)
    return x0 ^ x1


def dropout_mask_scaled():
    n = 16 * T * D
    bits = tf_mask_bits(n)
    fb = (bits >> np.uint32(9)) | np.uint32(0x3F800000)
    u = fb.view(np.float32) - 1.0
    keep = (u < np.float32(0.8)).reshape(16, T, D)
    return keep.astype(np.float32) * np.float32(1.25)


def host_prepare(inputs):
    g = {}
    g["l0_whgT"] = _f32c(np.asarray(inputs["L0_attn_W_hg"]).T)
    g["l0_woutT"] = _f32c(np.asarray(inputs["L0_attn_W_out"]).T)
    g["l0_w1T"] = _f32c(np.asarray(inputs["L0_ffn_W1"]).T)
    g["l0_b1"] = _f32c(inputs["L0_ffn_b1"]).reshape(DH, 1)
    g["l0_w2T"] = _f32c(np.asarray(inputs["L0_ffn_W2"]).T)
    g["l0_b2"] = _f32c(inputs["L0_ffn_b2"]).reshape(D, 1)
    g["l0_b2c"] = _f32c(np.asarray(inputs["L0_ffn_b2"]).reshape(NCH, 128).T)
    g["l1_wqkv"] = _f32c(inputs["L1_attn_W_qkv"])
    g["l1_bqkv"] = _f32c(inputs["L1_attn_b_qkv"]).reshape(3 * D, 1)
    g["l1_wo"] = _f32c(inputs["L1_attn_W_o"])
    g["l1_bo"] = _f32c(inputs["L1_attn_b_o"]).reshape(D, 1)
    g["l1_w1T"] = _f32c(np.asarray(inputs["L1_ffn_w1"]).T)
    g["l1_convw"] = _f32c(np.asarray(inputs["L1_ffn_conv_w"]).reshape(DH, 3))
    g["l1_convb"] = _f32c(inputs["L1_ffn_conv_b"]).reshape(DH, 1)
    g["l1_w2T"] = _f32c(np.asarray(inputs["L1_ffn_w2"]).T)
    bqkv = np.asarray(inputs["L1_attn_b_qkv"], np.float32)
    g["l1_bqkv_c"] = _f32c(bqkv[0:2048].reshape(16, 128).T)
    g["l1_bq_h"] = _f32c(bqkv[0:1024].reshape(16, 64).T)
    g["l1_bk_h"] = _f32c(bqkv[1024:2048].reshape(16, 64).T)
    g["l1_bv_row"] = _f32c(bqkv[2048:3072].reshape(1, D))
    g["l1_bo_c"] = _f32c(np.asarray(inputs["L1_attn_b_o"], np.float32).reshape(NCH, 128).T)
    g["l1_convw_c"] = _f32c(np.asarray(inputs["L1_ffn_conv_w"], np.float32)
                            .reshape(NCHH, 128, 3).transpose(1, 0, 2).reshape(128, NCHH * 3))
    g["l1_convb_c"] = _f32c(np.asarray(inputs["L1_ffn_conv_b"], np.float32)
                            .reshape(NCHH, 128).T)
    g["l2_rmsw"] = _f32c(inputs["L2_attn_rms_weight"]).reshape(D, 1)
    g["l2_rmsw_c"] = _f32c(np.asarray(inputs["L2_attn_rms_weight"], np.float32).reshape(NCH, 128).T)
    g["l2_bA_c"] = _f32c(np.asarray(inputs["L2_attn_b_A"], np.float32).reshape(NCH, 128).T)
    g["l2_b1_c"] = _f32c(np.asarray(inputs["L2_ffn_b1"], np.float32).reshape(NCHH, 128).T)
    g["l2_b2_c"] = _f32c(np.asarray(inputs["L2_ffn_b2"], np.float32).reshape(NCH, 128).T)
    g["l2_wA"] = _f32c(inputs["L2_attn_w_A"])
    g["l2_bA"] = _f32c(inputs["L2_attn_b_A"]).reshape(D, 1)
    g["l2_wx"] = _f32c(inputs["L2_attn_w_x"])
    g["l2_wsg"] = _f32c(inputs["L2_attn_w_sidegate"])
    g["l2_wout"] = _f32c(inputs["L2_attn_w_output"])
    g["l2_wo"] = _f32c(inputs["L2_attn_w_o"])
    g["l2_w1T"] = _f32c(np.asarray(inputs["L2_ffn_W1"]).T)
    g["l2_b1"] = _f32c(inputs["L2_ffn_b1"]).reshape(DH, 1)
    g["l2_w2T"] = _f32c(np.asarray(inputs["L2_ffn_W2"]).T)
    g["l2_b2"] = _f32c(inputs["L2_ffn_b2"]).reshape(D, 1)
    g["l3_wqT"] = _f32c(np.asarray(inputs["L3_attn_Wq"]).T)
    g["l3_bq"] = _f32c(inputs["L3_attn_bq"]).reshape(D, 1)
    g["l3_wkT"] = _f32c(np.asarray(inputs["L3_attn_Wk"]).T)
    g["l3_wvT"] = _f32c(np.asarray(inputs["L3_attn_Wv"]).T)
    g["l3_bv"] = _f32c(inputs["L3_attn_bv"]).reshape(D, 1)
    g["l3_woT"] = _f32c(np.asarray(inputs["L3_attn_Wo"]).T)
    g["l3_bo"] = _f32c(inputs["L3_attn_bo"]).reshape(D, 1)
    g["l3_lna"] = _f32c(inputs["L3_attn_lna_w"]).reshape(D, 1)
    g["l3_lnb"] = _f32c(inputs["L3_attn_lnb_w"]).reshape(D, 1)
    g["l3_lnc"] = _f32c(inputs["L3_attn_lnc_w"]).reshape(HD, 1)
    g["l3_lnd"] = _f32c(inputs["L3_attn_lnd_w"]).reshape(HD, 1)
    g["l3_lqT"] = _f32c(np.asarray(inputs["L3_attn_lq_w"]).T)
    g["l3_lqb"] = _f32c(inputs["L3_attn_lq_b"]).reshape(HD, 1)
    g["l3_lkT"] = _f32c(np.asarray(inputs["L3_attn_lk_w"]).T)
    g["l3_lkb"] = _f32c(inputs["L3_attn_lk_b"]).reshape(HD, 1)
    g["l3_lvT"] = _f32c(np.asarray(inputs["L3_attn_lv_w"]).T)
    g["l3_lvb"] = _f32c(inputs["L3_attn_lv_b"]).reshape(1, HD)
    g["l3_bq_h"] = _f32c(np.asarray(inputs["L3_attn_bq"], np.float32).reshape(NHEAD, HD).T)
    g["l3_bv_h"] = _f32c(np.asarray(inputs["L3_attn_bv"], np.float32).reshape(NHEAD, HD).T)
    g["l3_lna_c"] = _f32c(np.asarray(inputs["L3_attn_lna_w"], np.float32).reshape(NCH, 128).T)
    g["l3_lnb_c"] = _f32c(np.asarray(inputs["L3_attn_lnb_w"], np.float32).reshape(NCH, 128).T)
    g["l3_bq_c"] = _f32c(np.asarray(inputs["L3_attn_bq"], np.float32).reshape(NCH, 128).T)
    g["l3_bv_c"] = _f32c(np.asarray(inputs["L3_attn_bv"], np.float32).reshape(NCH, 128).T)
    g["l3_bo_c"] = _f32c(np.asarray(inputs["L3_attn_bo"], np.float32).reshape(NCH, 128).T)
    g["l3_sc3"] = np.array([[np.float32(inputs["L3_attn_threshold"]),
                             np.float32(inputs["L3_attn_temp_base"]),
                             np.float32(inputs["L3_attn_factor"])]], dtype=np.float32)
    g["l3_wgate"] = _f32c(inputs["L3_ffn_w_gate"])
    g["l3_bgate"] = _f32c(inputs["L3_ffn_b_gate"]).reshape(NEXP, 1)
    g["l3_W1"] = _f32c(inputs["L3_ffn_W1"])
    g["l3_b1"] = _f32c(inputs["L3_ffn_b1"]).reshape(NEXP, DH)
    g["l3_b1_cols"] = _f32c(np.asarray(inputs["L3_ffn_b1"], np.float32)
                            .reshape(NEXP, NCHH, 128).transpose(0, 2, 1))
    g["l3_b2_cols"] = _f32c(np.asarray(inputs["L3_ffn_b2"], np.float32)
                            .reshape(NEXP, NCH, 128).transpose(0, 2, 1))
    g["l3_W2"] = _f32c(inputs["L3_ffn_W2"])
    g["l3_b2"] = _f32c(inputs["L3_ffn_b2"]).reshape(NEXP, D)

    g["ones_col"] = np.ones((128, 1), np.float32)
    g["ones_row"] = np.ones((1, 128), np.float32)
    g["ident"] = np.eye(128, dtype=np.float32)
    g["identb"] = np.eye(128, dtype=np.float32).astype(ml_dtypes.bfloat16)
    g["triu"] = np.triu(np.ones((128, 128), np.float32))
    g["iota_row"] = np.arange(T, dtype=np.float32).reshape(1, T)
    g["iota8"] = np.tile(np.arange(8, dtype=np.float32) + 1e6, (128, 1))

    maskv = dropout_mask_scaled()
    x = np.asarray(inputs["L0_attn_x"], dtype=np.float32)
    per_core = []
    for ci in range(8):
        m = dict(g)
        m["x_in"] = _f32c(x[2 * ci: 2 * ci + 2].transpose(0, 2, 1))
        m["drop_fm"] = np.ascontiguousarray(
            maskv[2 * ci: 2 * ci + 2].transpose(0, 2, 1)).astype(ml_dtypes.bfloat16)
        per_core.append(m)
    return per_core


def _run(builder, per_core, keys, extra=None):
    k = builder()
    maps = []
    for ci in range(8):
        m = {kk: per_core[ci][kk] for kk in keys}
        if extra is not None:
            m.update(extra[ci])
        maps.append(m)
    res = run_bass_kernel_spmd(k.nc, maps, core_ids=list(range(8)))
    k.p.close()
    return res


L0_KEYS = ["x_in", "l0_whgT", "l0_woutT", "l0_w1T", "l0_b1", "l0_w2T", "l0_b2",
           "l0_b2c", "ones_col", "ones_row", "identb"]


L3B_KEYS = ["d0_all", "qc1", "sk3", "vtm3", "out0", "l3_lqT", "l3_lqb",
            "l3_lnc", "l3_sc3", "iota_row", "ones_col", "ones_row", "identb"]

L3C_KEYS = ["d0_all", "d1_all", "x_in", "qc1", "out1", "sk3", "vtm3", "xn3",
            "l3_lqT", "l3_lqb", "l3_lnc", "l3_sc3", "iota_row", "l3_woT",
            "l3_bo_c", "l3_wgate", "l3_bgate", "l3_W1", "l3_b1_cols", "l3_W2",
            "l3_b2_cols", "triu", "iota8", "ident",
            "ones_col", "ones_row", "identb"]

L3A_KEYS = ["x_in", "l3_wqT", "l3_bq_c", "l3_wkT", "l3_wvT", "l3_bv_c",
            "l3_lna_c", "l3_lnb_c", "l3_lnc", "l3_lnd", "l3_lqT", "l3_lqb",
            "l3_lkT", "l3_lkb", "l3_lvT", "l3_lvb", "l3_sc3",
            "l3_bq_h", "l3_bv_h", "ones_col", "ones_row", "identb"]

L2_KEYS = ["x_in", "l2_rmsw_c", "l2_wA", "l2_bA_c", "l2_wx", "l2_wsg", "l2_wout",
           "l2_wo", "l2_w1T", "l2_b1_c", "l2_w2T", "l2_b2_c", "drop_fm",
           "ones_col", "ones_row", "identb"]

L1_KEYS = ["x_in", "l1_wqkv", "l1_bqkv_c", "l1_bq_h", "l1_bk_h", "l1_bv_row", "l1_wo", "l1_bo_c",
           "l1_w1T", "l1_convw_c", "l1_convb_c", "l1_w2T",
           "ones_col", "ones_row", "identb"]


def kernel(**inputs):
    per_core = host_prepare(inputs)
    r = _run(build_L0, per_core, L0_KEYS)
    for ci in range(8):
        per_core[ci]["x_in"] = np.asarray(r.results[ci]["x_out"])
    r = _run(build_L1, per_core, L1_KEYS)
    for ci in range(8):
        per_core[ci]["x_in"] = np.asarray(r.results[ci]["x_out"])
    r = _run(build_L2, per_core, L2_KEYS)
    for ci in range(8):
        per_core[ci]["x_in"] = np.asarray(r.results[ci]["x_out"])
    ra = _run(build_L3a, per_core, L3A_KEYS)
    d0 = np.concatenate([np.asarray(ra.results[ci]["d0_part"]).reshape(1)
                         for ci in range(8)]).reshape(1, 8)
    extra_b = []
    for ci in range(8):
        extra_b.append({
            "d0_all": np.ascontiguousarray(d0, np.float32),
            "qc1": np.asarray(ra.results[ci]["qc1"]),
            "sk3": np.asarray(ra.results[ci]["sk3"]),
            "vtm3": np.asarray(ra.results[ci]["vtm3"]),
            "out0": np.asarray(ra.results[ci]["out0"]),
        })
    rb = _run(build_L3b, per_core, [kk for kk in L3B_KEYS if kk not in
                                    ("d0_all", "qc1", "sk3", "vtm3", "out0")],
              extra=extra_b)
    d1 = np.concatenate([np.asarray(rb.results[ci]["d1_part"]).reshape(1)
                         for ci in range(8)]).reshape(1, 8)
    extra_c = []
    for ci in range(8):
        extra_c.append({
            "d0_all": np.ascontiguousarray(d0, np.float32),
            "d1_all": np.ascontiguousarray(d1, np.float32),
            "qc1": np.asarray(ra.results[ci]["qc1"]),
            "sk3": np.asarray(ra.results[ci]["sk3"]),
            "vtm3": np.asarray(ra.results[ci]["vtm3"]),
            "xn3": np.asarray(ra.results[ci]["xn3"]),
            "out1": np.asarray(rb.results[ci]["out1"]),
        })
    rc = _run(build_L3c, per_core, [kk for kk in L3C_KEYS if kk not in
                                    ("d0_all", "d1_all", "qc1", "sk3", "vtm3",
                                     "xn3", "out1")],
              extra=extra_c)
    out = np.empty((16, T, D), np.float32)
    for ci in range(8):
        xf = np.asarray(rc.results[ci]["x_out"])
        out[2 * ci: 2 * ci + 2] = xf.transpose(0, 2, 1)
    return out


# version 04 (final)
# speedup vs baseline: 1.2132x; 1.2132x over previous
import math
import contextlib
import numpy as np
import ml_dtypes
import concourse.bass as bass
import concourse.mybir as mybir
from concourse.bass import IndirectOffsetOnAxis
from concourse.bass_utils import run_bass_kernel_spmd

F32 = mybir.dt.float32
BF16 = mybir.dt.bfloat16
FP16 = mybir.dt.float16
FP8 = mybir.dt.float8e5
I32 = mybir.dt.int32
AF = mybir.ActivationFunctionType
ALU = mybir.AluOpType
AX = mybir.AxisListType

ENG = ("tensor", "vector", "scalar", "gpsimd", "sync")
import os
DBG = bool(int(os.environ.get("KDBG", "0")))
NSEQ = 2
T = 2048
D = 1024
NCH = D // 128
TT = 512
NTT = T // TT
NTC = T // 128
NHEAD = 16
HD = 64
CAP = 1024
NEXP = 8
DH = 4096
NCHH = DH // 128
RTM = 12582912.0


class Prog:
    def __init__(self, nc):
        self.nc = nc
        self.stack = contextlib.ExitStack()
        self.cs = {e: self.stack.enter_context(nc.semaphore(f"cs_{e}")) for e in ENG}
        self.ds = {e: self.stack.enter_context(nc.semaphore(f"ds_{e}")) for e in ENG}
        self.cc = {e: 0 for e in ENG}
        self.dc = {e: 0 for e in ENG}
        self.clock = {e: {} for e in ENG}
        self.last_w = {}
        self.readers = {}
        self.nops = 0

    def _wait(self, eng, kind, src, cnt, snap):
        if cnt <= 0:
            return
        if self.clock[eng].get((kind, src), 0) >= cnt:
            return
        sem = (self.cs if kind == "c" else self.ds)[src]
        getattr(self.nc, eng).wait_ge(sem, cnt)
        self.clock[eng][(kind, src)] = cnt
        if snap:
            for kk, v in snap.items():
                if self.clock[eng].get(kk, 0) < v:
                    self.clock[eng][kk] = v

    def do(self, eng, emit, reads=(), writes=(), dma=False):
        for b in reads:
            lw = self.last_w.get(b)
            if lw is not None:
                self._wait(eng, lw[0], lw[1], lw[2], lw[3])
        for b in writes:
            lw = self.last_w.get(b)
            if lw is not None:
                self._wait(eng, lw[0], lw[1], lw[2], lw[3])
            for rr in self.readers.get(b, ()):
                self._wait(eng, rr[0], rr[1], rr[2], rr[3])
        inst = emit()
        kind = "d" if dma else "c"
        if dma:
            self.dc[eng] += 16
            inst.then_inc(self.ds[eng], 16)
            cnt = self.dc[eng]
        else:
            self.cc[eng] += 1
            inst.then_inc(self.cs[eng], 1)
            cnt = self.cc[eng]
        snap = dict(self.clock[eng])
        snap[(kind, eng)] = cnt
        rec = (kind, eng, cnt, snap)
        for b in writes:
            self.last_w[b] = rec
            self.readers[b] = []
        for b in reads:
            self.readers.setdefault(b, []).append(rec)
        self.nops += 1
        return inst

    def finalize(self, eng="sync"):
        for e in ENG:
            self._wait(eng, "c", e, self.cc[e], None)
            self._wait(eng, "d", e, self.dc[e], None)

    def barrier(self):
        for eng in ENG:
            for e in ENG:
                self._wait(eng, "c", e, self.cc[e], None)
                self._wait(eng, "d", e, self.dc[e], None)

    def close(self):
        self.stack.close()


def names(*aps):
    out = []
    for a in aps:
        if a is None or isinstance(a, (int, float)):
            continue
        out.append(a if isinstance(a, str) else a.tensor.name)
    return out


class K:
    def __init__(self):
        self.nc = bass.Bass("TRN2", target_bir_lowering=False)
        self.p = Prog(self.nc)

    def inp(self, name, shape, dtype=F32):
        return self.nc.dram_tensor(name, shape, dtype, kind="ExternalInput").ap()

    def out(self, name, shape, dtype=F32):
        return self.nc.dram_tensor(name, shape, dtype, kind="ExternalOutput").ap()

    def dram(self, name, shape, dtype=F32):
        return self.nc.dram_tensor(name, shape, dtype, kind="Internal").ap()

    def sbuf(self, name, shape, dtype=F32):
        return self.nc.alloc_sbuf_tensor(name, shape, dtype).ap()

    def psum(self, name, shape, dtype=F32):
        return self.nc.alloc_psum_tensor(name, shape, dtype).ap()

    def scope(self):
        return _Scope(self)

    def barrier(self):
        self.p.barrier()

    def mm(self, out, lhsT, rhs, start=True, stop=True):
        self.p.do("tensor",
                  lambda: self.nc.tensor.matmul(out, lhsT, rhs, start=start, stop=stop),
                  reads=names(lhsT, rhs), writes=names(out))

    def tr(self, out, in_, ident):
        self.p.do("tensor", lambda: self.nc.tensor.transpose(out, in_, ident),
                  reads=names(in_, ident), writes=names(out))

    def act(self, out, in_, func, bias=None, scale=1.0, accum_out=None):
        kw = {}
        if bias is not None:
            kw["bias"] = bias
        if accum_out is not None:
            kw["accum_out"] = accum_out
        self.p.do("scalar",
                  lambda: self.nc.scalar.activation(out, in_, func, scale=scale, **kw),
                  reads=names(in_, bias if not isinstance(bias, (float, type(None))) else None,
                              scale if not isinstance(scale, (float, int)) else None),
                  writes=names(out, accum_out))

    def tt(self, out, a, b, op, eng="vector"):
        self.p.do(eng, lambda: getattr(self.nc, eng).tensor_tensor(out, a, b, op),
                  reads=names(a, b), writes=names(out))

    def ts(self, out, in_, s1, s2, op0, op1=None, eng="vector"):
        def e():
            if op1 is None:
                return getattr(self.nc, eng).tensor_single_scalar(out, in_, s1, op0)
            return getattr(self.nc, eng).tensor_scalar(out, in_, s1, s2, op0, op1)
        self.p.do(eng, e, reads=names(in_, s1, s2), writes=names(out))

    def stt(self, out, in0, scalar, in1, op0, op1, eng="vector"):
        self.p.do(eng,
                  lambda: getattr(self.nc, eng).scalar_tensor_tensor(out, in0, scalar, in1, op0, op1),
                  reads=names(in0, in1, scalar), writes=names(out))

    def red(self, out, in_, op, absval=False, eng="vector"):
        self.p.do(eng,
                  lambda: getattr(self.nc, eng).tensor_reduce(
                      out, in_, AX.X, op, apply_absolute_value=absval),
                  reads=names(in_), writes=names(out))

    def scan(self, out, a, b, init, eng="vector"):
        self.p.do(eng,
                  lambda: getattr(self.nc, eng).tensor_tensor_scan(out, a, b, init, ALU.mult, ALU.add),
                  reads=names(a, b, init), writes=names(out))

    def copy(self, out, in_, eng="vector"):
        self.p.do(eng, lambda: getattr(self.nc, eng).tensor_copy(out, in_),
                  reads=names(in_), writes=names(out))

    def memset(self, ap, v, eng="vector"):
        self.p.do(eng, lambda: getattr(self.nc, eng).memset(ap, v), writes=names(ap))

    def recip(self, out, in_):
        self.p.do("vector", lambda: self.nc.vector.reciprocal(out, in_),
                  reads=names(in_), writes=names(out))

    def dma(self, out, in_, eng="sync", **kw):
        self.p.do(eng, lambda: getattr(self.nc, eng).dma_start(out, in_, **kw),
                  reads=names(in_), writes=names(out), dma=True)

    def idma(self, out, out_off, in_, in_off):
        rd = names(in_)
        wr = names(out)
        if in_off is not None:
            rd += names(in_off[0])
        if out_off is not None:
            rd += names(out_off[0])

        def e():
            oo = IndirectOffsetOnAxis(out_off[0], out_off[1]) if out_off else None
            io = IndirectOffsetOnAxis(in_off[0], in_off[1]) if in_off else None
            return self.nc.gpsimd.indirect_dma_start(out, oo, in_, io)
        self.p.do("gpsimd", e, reads=rd, writes=wr, dma=True)


class _Scope:
    _uid = [0]

    def __init__(self, k):
        self.k = k
        self.stack = contextlib.ExitStack()
        _Scope._uid[0] += 1
        self.uid = _Scope._uid[0]

    def sbuf(self, name, shape, dtype=F32):
        return self.stack.enter_context(
            self.k.nc.sbuf_tensor(f"{name}__{self.uid}", shape, dtype)).ap()

    def psum(self, name, shape, dtype=F32):
        return self.stack.enter_context(
            self.k.nc.psum_tensor(f"{name}__{self.uid}", shape, dtype)).ap()

    def __enter__(self):
        return self

    def __exit__(self, *a):
        self.k.p.barrier()
        self.stack.close()
        return False


def fmv(X, c, t0=0, tlen=T, stride=T):
    return X[:, c * stride + t0: c * stride + t0 + tlen]


def load_consts(k):
    c = {}
    c["ones_col"] = k.sbuf("c_ones_col", [128, 1])
    c["ones_row"] = k.sbuf("c_ones_row", [1, 128])
    c["identb"] = k.sbuf("c_identb", [128, 128], BF16)
    k.dma(c["ones_col"], k.inp("ones_col", [128, 1]))
    k.dma(c["ones_row"], k.inp("ones_row", [1, 128]))
    k.dma(c["identb"], k.inp("identb", [128, 128], BF16))
    c["eps_ln"] = k.sbuf("c_epsln", [1, 1])
    k.memset(c["eps_ln"], 1e-5)
    c["ones_col_b"] = k.sbuf("c_ones_col_b", [128, 1], BF16)
    k.memset(c["ones_col_b"], 1.0)
    return c


def emit_ln(k, c, s, x_dram, xn, rows_keep=None):
    with k.scope() as sc:
        xt = sc.sbuf("ln_xt", [128, NCH * TT])
        sq = sc.sbuf("ln_sq", [128, TT])
        rows = sc.sbuf("ln_rows", [1, 4 * TT])
        psSx = sc.psum("ln_psx", [1, TT])
        psSq = sc.psum("ln_psq", [1, TT])
        psR = sc.psum("ln_psr", [128, TT])
        psM = sc.psum("ln_psm", [128, TT])
        for tt_ in range(NTT):
            t0 = tt_ * TT
            for ci in range(NCH):
                k.dma(xt[:, ci * TT:(ci + 1) * TT],
                      x_dram[s, ci * 128:(ci + 1) * 128, t0:t0 + TT])
            for ci in range(NCH):
                k.mm(psSx, c["ones_col"], xt[:, ci * TT:(ci + 1) * TT],
                     start=(ci == 0), stop=(ci == NCH - 1))
            for ci in range(NCH):
                k.act(sq, xt[:, ci * TT:(ci + 1) * TT], AF.Square)
                k.mm(psSq, c["ones_col"], sq, start=(ci == 0), stop=(ci == NCH - 1))
            rmu = rows[:, 0:TT]
            rvar = rows[:, TT:2 * TT]
            rr = rows[:, 2 * TT:3 * TT]
            rm = rows[:, 3 * TT:4 * TT]
            k.ts(rmu, psSx, 1.0 / D, None, ALU.mult)
            k.ts(rvar, psSq, 1.0 / D, None, ALU.mult)
            k.tt(rr, rmu, rmu, ALU.mult)
            k.tt(rvar, rvar, rr, ALU.subtract)
            k.act(rr, rvar, AF.Sqrt, bias=c["eps_ln"])
            k.recip(rr, rr)
            k.tt(rm, rmu, rr, ALU.mult)
            if rows_keep is not None:
                k.copy(rows_keep[0][:, t0:t0 + TT], rr)
                k.copy(rows_keep[1][:, t0:t0 + TT], rm)
            k.mm(psR, c["ones_row"], rr)
            k.mm(psM, c["ones_row"], rm)
            for ci in range(NCH):
                k.tt(sq, xt[:, ci * TT:(ci + 1) * TT], psR, ALU.mult)
                k.tt(fmv(xn, ci, t0, TT), sq, psM, ALU.subtract)


def emit_wscale_ternary(k, c, wT_dram, rows_total, q_dram, sname):
    cols = wT_dram.shape[1]
    nrch = rows_total // 128
    swrow = k.sbuf(f"qsw_{sname}", [1, 1])
    rsrow = k.sbuf(f"qrs_{sname}", [1, 1])
    swcol = k.sbuf(f"qswc_{sname}", [128, 1])
    rscol = k.sbuf(f"qrsc_{sname}", [128, 1])
    with k.scope() as sc:
        wbuf = sc.sbuf(f"qw_{sname}", [128, cols])
        q8 = sc.sbuf(f"qw8_{sname}", [128, cols], FP8)
        acc = sc.sbuf(f"qa_{sname}", [128, 1])
        tmp1 = sc.sbuf(f"qt_{sname}", [128, 1])
        psA = sc.psum(f"qp_{sname}", [1, 1])
        psB = sc.psum(f"qb_{sname}", [128, 1])
        k.memset(acc, 0.0)
        for r in range(nrch):
            k.dma(wbuf, wT_dram[r * 128:(r + 1) * 128, :])
            k.red(tmp1, wbuf, ALU.add, absval=True)
            k.tt(acc, acc, tmp1, ALU.add)
        k.mm(psA, c["ones_col"], acc)
        k.ts(swrow, psA, 1.0 / (rows_total * cols), None, ALU.mult)
        k.ts(swrow, swrow, 1e-5, None, ALU.max)
        k.recip(rsrow, swrow)
        k.mm(psB, c["ones_row"], swrow)
        k.copy(swcol, psB)
        k.mm(psB, c["ones_row"], rsrow)
        k.copy(rscol, psB)
        for r in range(nrch):
            k.dma(wbuf, wT_dram[r * 128:(r + 1) * 128, :])
            k.ts(wbuf, wbuf, rscol, RTM, ALU.mult, ALU.add)
            k.ts(wbuf, wbuf, RTM, None, ALU.subtract)
            k.ts(wbuf, wbuf, -1.0, 1.0, ALU.max, ALU.min)
            k.copy(q8, wbuf)
            k.dma(q_dram[r * 128:(r + 1) * 128, :], q8)
    return swcol, rscol, rsrow


def emit_actquant_fm(k, c, sc, xn, rq, nch_in=NCH, eps=1e-5):
    pstr_p = [sc.psum("aq_ptr0", [128, 128], BF16), sc.psum("aq_ptr1", [128, 128], BF16)]
    xtm = sc.sbuf("aq_xtm", [128, nch_in * 128], BF16)
    str_ = sc.sbuf("aq_scr", [128, nch_in * 128])
    mcol = sc.sbuf("aq_m", [128, 1])
    ccol = sc.sbuf("aq_c", [128, 1])
    dcol = sc.sbuf("aq_d", [128, 1])
    for tc in range(NTC):
        for ci in range(nch_in):
            pstr = pstr_p[ci % 2]
            k.tr(pstr, fmv(xn, ci, tc * 128, 128), c["identb"])
            k.copy(xtm[:, ci * 128:(ci + 1) * 128], pstr)
        k.copy(str_, xtm)
        k.red(mcol, str_, ALU.max, absval=True)
        k.ts(mcol, mcol, eps, None, ALU.max)
        k.recip(ccol, mcol)
        k.ts(ccol, ccol, 127.0, None, ALU.mult)
        k.ts(dcol, mcol, 1.0 / 127.0, None, ALU.mult)
        k.ts(str_, xtm, ccol, RTM, ALU.mult, ALU.add)
        k.ts(str_, str_, RTM, None, ALU.subtract)
        k.ts(xtm, str_, dcol, None, ALU.mult)
        for ci in range(nch_in):
            pstr = pstr_p[ci % 2]
            k.tr(pstr, xtm[:, ci * 128:(ci + 1) * 128], c["identb"])
            k.copy(fmv(rq, ci, tc * 128, 128), pstr)


def build_L0():
    k = K()
    c = load_consts(k)
    x_in = k.inp("x_in", [NSEQ, D, T])
    whgT = k.inp("l0_whgT", [D, 3072])
    woutT_d = k.inp("l0_woutT", [1536, D])
    w1T = k.inp("l0_w1T", [D, DH])
    b1_d = k.inp("l0_b1", [DH, 1])
    w2T = k.inp("l0_w2T", [DH, D])
    b2_d = k.inp("l0_b2", [D, 1])
    x_out = k.out("x_out", [NSEQ, D, T])
    if DBG:
        s_dram = k.out("s_scr", [NSEQ, D, T], BF16)
        xn_dbg = k.out("xn_dbg", [NSEQ, D, T], BF16)
        g_dbg = k.out("g_dbg", [NSEQ, D, T], BF16)
        rq_dbg = k.out("rq_dbg", [NSEQ, D, T], BF16)
        h_dbg = k.out("h_dbg", [NSEQ, T, DH], BF16)
    else:
        s_dram = k.dram("s_scr", [NSEQ, D, T], BF16)
    w1q_d = k.dram("w1q8d", [D, DH], FP8)
    w2q_d = k.dram("w2q8d", [DH, D], FP8)

    xn = k.sbuf("xn", [128, NCH * T], BF16)

    sw1, rs1, rs1row = emit_wscale_ternary(k, c, w1T, D, w1q_d, "w1")
    sw2, rs2, rs2row = emit_wscale_ternary(k, c, w2T, DH, w2q_d, "w2")
    b1p = k.sbuf("b1p", [1, DH])
    b2col = k.sbuf("b2col", [128, NCH])
    k.dma(b1p, b1_d.rearrange("h o -> o h"))
    k.ts(b1p, b1p, rs1row, None, ALU.mult)
    b2c_d = k.inp("l0_b2c", [128, NCH])
    k.dma(b2col, b2c_d)
    for s in range(NSEQ):
        emit_ln(k, c, s, x_in, xn)
        if DBG:
            for ci in range(NCH):
                k.dma(xn_dbg[s, ci * 128:(ci + 1) * 128, :], fmv(xn, ci), eng="gpsimd")

        with k.scope() as sc:
            woutT = sc.sbuf("woutT", [128, 12 * D], BF16)
            for ec in range(12):
                k.dma(woutT[:, ec * D:(ec + 1) * D], woutT_d[ec * 128:(ec + 1) * 128, :],
                      eng="gpsimd")
            wh = sc.sbuf("wh", [128, 1024], BF16)
            wg = sc.sbuf("wg", [128, 1024], BF16)
            Hc = sc.sbuf("Hc", [128, T])
            Gc = sc.sbuf("Gc", [128, T])
            sa = sc.sbuf("sa", [128, T])
            sz = sc.sbuf("sz", [128, T])
            sh = sc.sbuf("sh", [128, T])
            SF = sc.sbuf("SF", [128, T])
            SB = sc.sbuf("SB", [128, T])
            Ssum = sc.sbuf("Ssum", [128, 12 * T], BF16)
            psH = sc.psum("psH", [128, TT])
            psG = sc.psum("psG", [128, TT])
            for cc_ in range(12):
                for wbuf, col0 in ((wh, cc_ * 128), (wg, 1536 + cc_ * 128)):
                    for dc in range(NCH):
                        k.dma(wbuf[:, dc * 128:(dc + 1) * 128],
                              whgT[dc * 128:(dc + 1) * 128, col0:col0 + 128],
                              eng="gpsimd")
                for tt_ in range(NTT):
                    t0 = tt_ * TT
                    for dc in range(NCH):
                        k.mm(psH, wh[:, dc * 128:(dc + 1) * 128], fmv(xn, dc, t0, TT),
                             start=(dc == 0), stop=(dc == NCH - 1))
                    k.copy(Hc[:, t0:t0 + TT], psH)
                    for dc in range(NCH):
                        k.mm(psG, wg[:, dc * 128:(dc + 1) * 128], fmv(xn, dc, t0, TT),
                             start=(dc == 0), stop=(dc == NCH - 1))
                    k.copy(Gc[:, t0:t0 + TT], psG)
                k.act(sa, Gc, AF.Sigmoid, scale=-1.0)
                k.act(sz, Gc, AF.Sigmoid)
                k.act(sh, Hc, AF.Sigmoid)
                k.ts(Gc, Hc, 0.0, None, ALU.is_ge)
                k.ts(Hc, Hc, 0.5, None, ALU.add)
                k.tt(Hc, Hc, sh, ALU.subtract)
                k.tt(Gc, Gc, Hc, ALU.mult)
                k.tt(Gc, Gc, sh, ALU.add)
                k.tt(sz, sz, Gc, ALU.mult)
                k.scan(SF, sa, sz, 0.0)
                k.scan(SB, sa[:, ::-1], sz[:, ::-1], 0.0)
                k.tt(Ssum[:, cc_ * T:(cc_ + 1) * T], SF, SB, ALU.add)
            for dc in range(NCH):
                for tt_ in range(NTT):
                    t0 = tt_ * TT
                    for ec in range(12):
                        k.mm(psH, woutT[:, ec * D + dc * 128: ec * D + (dc + 1) * 128],
                             Ssum[:, ec * T + t0: ec * T + t0 + TT],
                             start=(ec == 0), stop=(ec == 11))
                    k.copy(SF[:, 0:TT], psH)
                    k.dma(s_dram[s, dc * 128:(dc + 1) * 128, t0:t0 + TT], SF[:, 0:TT],
                          eng="gpsimd")

        with k.scope() as sc:
            rq = sc.sbuf("rq", [128, NCH * T], BF16)
            emit_actquant_fm(k, c, sc, xn, rq)
            if DBG:
                for ci in range(NCH):
                    k.dma(rq_dbg[s, ci * 128:(ci + 1) * 128, :], fmv(rq, ci), eng="gpsimd")
            w2q8 = sc.sbuf("w2q8", [128, NCHH * 1024], FP8)
            for hc in range(NCHH):
                k.dma(w2q8[:, hc * D:(hc + 1) * D], w2q_d[hc * 128:(hc + 1) * 128, :])
            w1b = sc.sbuf("w1b", [128, NCH * DH], FP8)
            for dc in range(NCH):
                k.dma(w1b[:, dc * DH:(dc + 1) * DH], w1q_d[dc * 128:(dc + 1) * 128, :])
            htm = sc.sbuf("htm", [128, DH], BF16)
            hts = sc.sbuf("hts", [128, DH // 2])
            hqfm = sc.sbuf("hqfm", [128, NCHH * TT], BF16)
            gt = sc.sbuf("gt", [128, TT], BF16)
            st2 = sc.sbuf("st2", [128, TT])
            xt2 = sc.sbuf("xt2", [128, TT])
            hcm = sc.sbuf("hcm", [128, 1])
            hcm2 = sc.sbuf("hcm2", [128, 1])
            hcc = sc.sbuf("hcc", [128, 1])
            hcd = sc.sbuf("hcd", [128, 1])
            ps1_p = [sc.psum("ps1a", [128, TT]), sc.psum("ps1b", [128, TT])]
            ps2_p = [sc.psum("ps2a", [128, TT]), sc.psum("ps2b", [128, TT])]
            pstr2_p = [sc.psum("pstr2a", [128, 128], BF16), sc.psum("pstr2b", [128, 128], BF16)]
            for tt_ in range(NTT):
                t0 = tt_ * TT
                for tci in range(4):
                    tc = tt_ * 4 + tci
                    for hh in range(NCH):
                        ps1 = ps1_p[hh % 2]
                        for dc in range(NCH):
                            k.mm(ps1, rq[:, dc * T + tc * 128: dc * T + (tc + 1) * 128],
                                 w1b[:, dc * DH + hh * TT: dc * DH + hh * TT + TT],
                                 start=(dc == 0), stop=False)
                        k.mm(ps1, c["ones_row"], b1p[:, hh * TT:(hh + 1) * TT],
                             start=False, stop=True)
                        k.act(htm[:, hh * TT:(hh + 1) * TT], ps1, AF.Gelu, scale=sw1)
                    if DBG:
                        k.dma(h_dbg[s, tc * 128:(tc + 1) * 128, :], htm, eng="gpsimd")
                    k.copy(hts, htm[:, 0:DH // 2])
                    k.red(hcm, hts, ALU.max, absval=True)
                    k.copy(hts, htm[:, DH // 2:DH])
                    k.red(hcm2, hts, ALU.max, absval=True)
                    k.tt(hcm, hcm, hcm2, ALU.max)
                    k.ts(hcm, hcm, 1e-5, None, ALU.max)
                    k.recip(hcc, hcm)
                    k.ts(hcc, hcc, 127.0, None, ALU.mult)
                    k.ts(hcd, hcm, 1.0 / 127.0, None, ALU.mult)
                    for hf in range(2):
                        hsl = htm[:, hf * (DH // 2):(hf + 1) * (DH // 2)]
                        k.ts(hts, hsl, hcc, RTM, ALU.mult, ALU.add)
                        k.ts(hts, hts, RTM, None, ALU.subtract)
                        k.ts(hsl, hts, hcd, None, ALU.mult)
                    for hc in range(NCHH):
                        pstr2 = pstr2_p[hc % 2]
                        k.tr(pstr2, htm[:, hc * 128:(hc + 1) * 128], c["identb"])
                        k.copy(hqfm[:, hc * TT + tci * 128: hc * TT + tci * 128 + 128], pstr2)
                for dc in range(NCH):
                    ps2 = ps2_p[dc % 2]
                    for hc in range(NCHH):
                        k.mm(ps2, w2q8[:, hc * D + dc * 128: hc * D + (dc + 1) * 128],
                             hqfm[:, hc * TT:(hc + 1) * TT],
                             start=(hc == 0), stop=(hc == NCHH - 1))
                    k.act(gt, ps2, AF.Silu, bias=b2col[:, dc:dc + 1], scale=sw2)
                    if DBG:
                        k.dma(g_dbg[s, dc * 128:(dc + 1) * 128, t0:t0 + TT], gt, eng="gpsimd")
                    k.dma(st2, s_dram[s, dc * 128:(dc + 1) * 128, t0:t0 + TT],
                          eng="gpsimd")
                    k.dma(xt2, x_in[s, dc * 128:(dc + 1) * 128, t0:t0 + TT])
                    k.tt(st2, st2, gt, ALU.mult)
                    k.tt(xt2, xt2, st2, ALU.add)
                    k.dma(x_out[s, dc * 128:(dc + 1) * 128, t0:t0 + TT], xt2)

    k.p.finalize()
    return k




def build_L1():
    k = K()
    c = load_consts(k)
    x_in = k.inp("x_in", [NSEQ, D, T])
    wqkv_d = k.inp("l1_wqkv", [D, 3 * D])
    bqkc_d = k.inp("l1_bqkv_c", [128, 16])
    bvrow_d = k.inp("l1_bv_row", [1, D])
    wo_d = k.inp("l1_wo", [D, D])
    boc_d = k.inp("l1_bo_c", [128, NCH])
    w1T = k.inp("l1_w1T", [D, DH])
    cwc_d = k.inp("l1_convw_c", [128, NCHH * 3])
    cbc_d = k.inp("l1_convb_c", [128, NCHH])
    w2T = k.inp("l1_w2T", [DH, D])
    x_out = k.out("x_out", [NSEQ, D, T])
    s_dram = k.dram("s_scr", [NSEQ, D, T], BF16)
    h2_dram = k.dram("h2_scr", [DH, T], BF16)
    w1q_d = k.dram("w1q8d", [D, DH], FP8)
    w2q_d = k.dram("w2q8d", [DH, D], FP8)

    xn = k.sbuf("xn", [128, NCH * T], BF16)
    sw1, rs1, rs1row = emit_wscale_ternary(k, c, w1T, D, w1q_d, "w1")
    sw2, rs2, rs2row = emit_wscale_ternary(k, c, w2T, DH, w2q_d, "w2")
    bqkc = k.sbuf("bqkc", [128, 16])
    bvrow = k.sbuf("bvrow", [1, D])
    boc = k.sbuf("boc", [128, NCH])
    cwc = k.sbuf("cwc", [128, NCHH * 3])
    cbc = k.sbuf("cbc", [128, NCHH])
    bqh = k.sbuf("bqh", [64, NHEAD])
    bkh = k.sbuf("bkh", [64, NHEAD])
    bqh_d = k.inp("l1_bq_h", [64, NHEAD])
    bkh_d = k.inp("l1_bk_h", [64, NHEAD])
    k.dma(bqh, bqh_d)
    k.dma(bkh, bkh_d)
    k.ts(bqh, bqh, 0.125, None, ALU.mult)
    k.dma(bqkc, bqkc_d)
    k.dma(bvrow, bvrow_d)
    k.dma(boc, boc_d)
    k.dma(cwc, cwc_d)
    k.dma(cbc, cbc_d)
    k.ts(bqkc[:, 0:8], bqkc[:, 0:8], 0.125, None, ALU.mult)
    k.ts(cwc, cwc, sw1, None, ALU.mult)

    if DBG:
        s_dram = k.out("s_scr2", [NSEQ, D, T], BF16)

    for s in range(NSEQ):
        emit_ln(k, c, s, x_in, xn)

        with k.scope() as sc:
            wqs = sc.sbuf("wqs", [128, 1024], BF16)
            vtm = sc.sbuf("vtm", [128, NTC * NHEAD * 65], BF16)
            avall = sc.sbuf("avall", [128, NCH * T], BF16)
            wv = sc.sbuf("wv", [128, NCH * TT], BF16)
            psQ = sc.psum("psQ", [128, TT])
            k.memset(vtm, 1.0)
            for vh in range(2):
                for dc in range(NCH):
                    k.dma(wv[:, dc * TT:(dc + 1) * TT],
                          wqkv_d[dc * 128:(dc + 1) * 128, 2 * D + vh * TT: 2 * D + (vh + 1) * TT],
                          eng="gpsimd")
                for tc in range(NTC):
                    for dc in range(NCH):
                        k.mm(psQ, fmv(xn, dc, tc * 128, 128), wv[:, dc * TT:(dc + 1) * TT],
                             start=(dc == 0), stop=False)
                    k.mm(psQ, c["ones_row"], bvrow[:, vh * TT:(vh + 1) * TT],
                         start=False, stop=True)
                    base = tc * NHEAD * 65 + vh * 8 * 65
                    k.copy(vtm[:, base: base + 8 * 65]
                           .rearrange("p (h e) -> p h e", e=65)[:, :, 0:64],
                           psQ.rearrange("p (h e) -> p h e", e=64))
            kaug = sc.sbuf("kaug", [128, T], BF16)
            qaug = sc.sbuf("qaug", [128, T], BF16)
            wqh = sc.sbuf("wqh", [128, NCH * 64], BF16)
            wkh = sc.sbuf("wkh", [128, NCH * 64], BF16)
            sqsc = sc.sbuf("sqsc", [64, TT])
            rowm = sc.sbuf("rowm", [1, T])
            sc1 = sc.sbuf("sc1", [1, 1])
            sc2 = sc.sbuf("sc2", [1, 1])
            mcol = sc.sbuf("mcol", [128, 1])
            srow = sc.sbuf("srow", [128, TT])
            srow0 = sc.sbuf("srow0", [1, TT])
            bcs = sc.sbuf("bcs", [64, TT])
            pt = sc.sbuf("pt", [128, TT], BF16)
            pt2 = sc.sbuf("pt2", [128, TT], BF16)
            psS = sc.psum("psS", [128, TT])
            psS2 = sc.psum("psS2", [128, TT])
            psA = sc.psum("psA", [128, TT])
            psR = sc.psum("psR", [1, TT])
            psB = sc.psum("psB", [128, 1])
            psH = sc.psum("psHh", [64, TT])
            k.memset(kaug[64:66, :], 0.0)
            k.memset(qaug[64:66, :], 1.0)
            for h in range(NHEAD):
                for dc in range(NCH):
                    k.dma(wqh[:, dc * 64:(dc + 1) * 64],
                          wqkv_d[dc * 128:(dc + 1) * 128, h * 64:(h + 1) * 64],
                          eng="gpsimd")
                    k.dma(wkh[:, dc * 64:(dc + 1) * 64],
                          wqkv_d[dc * 128:(dc + 1) * 128, D + h * 64: D + (h + 1) * 64],
                          eng="gpsimd")
                for tt_ in range(NTT):
                    t0 = tt_ * TT
                    for dc in range(NCH):
                        k.mm(psH, wqh[:, dc * 64:(dc + 1) * 64], fmv(xn, dc, t0, TT),
                             start=(dc == 0), stop=(dc == NCH - 1))
                    k.ts(qaug[0:64, t0:t0 + TT], psH, bqh[:, h:h + 1], 0.125,
                         ALU.add, ALU.mult)
                    for dc in range(NCH):
                        k.mm(psH, wkh[:, dc * 64:(dc + 1) * 64], fmv(xn, dc, t0, TT),
                             start=(dc == 0), stop=(dc == NCH - 1))
                    k.ts(kaug[0:64, t0:t0 + TT], psH, bkh[:, h:h + 1], None, ALU.add)
                for aug, scx in ((qaug, sc1), (kaug, sc2)):
                    for tt_ in range(NTT):
                        k.act(sqsc, aug[0:64, tt_ * TT:(tt_ + 1) * TT], AF.Square)
                        k.mm(psR, c["ones_col"][0:64, :], sqsc)
                        k.copy(rowm[:, tt_ * TT:(tt_ + 1) * TT], psR)
                    k.red(scx, rowm, ALU.max)
                k.act(sc1, sc1, AF.Sqrt)
                k.act(sc2, sc2, AF.Sqrt)
                k.tt(sc1, sc1, sc2, ALU.mult)
                k.ts(sc1, sc1, -1.0, None, ALU.mult)
                k.mm(psB, c["ones_row"], sc1)
                k.copy(mcol, psB)
                for qt in range(NTT):
                    q0 = qt * TT
                    for kc in range(NTC):
                        pS = psS if kc % 2 == 0 else psS2
                        pT = pt if kc % 2 == 0 else pt2
                        k.mm(pS, kaug[:, kc * 128:(kc + 1) * 128][0:65, :],
                             qaug[0:65, q0:q0 + TT])
                        k.act(pT, pS, AF.Exp, bias=mcol)
                        k.mm(psA[0:65, :], vtm[:, (kc * NHEAD + h) * 65:(kc * NHEAD + h) * 65 + 65],
                             pT, start=(kc == 0), stop=(kc == NTC - 1))
                    k.copy(srow[64:65, :], psA[64:65, :])
                    k.dma(srow0, srow[64:65, :])
                    k.recip(srow0, srow0)
                    k.mm(psS[0:64, :], c["ones_row"][:, 0:64], srow0)
                    k.copy(bcs, psS[0:64, :])
                    k.tt(bcs, psA[0:64, :], bcs, ALU.mult)
                    ci, off = h // 2, (h % 2) * 64
                    k.copy(fmv(avall, ci, q0, TT)[off:off + 64, :], bcs)
            for oc in range(NCH):
                for dc in range(NCH):
                    k.dma(wqs[:, dc * 128:(dc + 1) * 128],
                          wo_d[dc * 128:(dc + 1) * 128, oc * 128:(oc + 1) * 128],
                          eng="gpsimd")
                for tt_ in range(NTT):
                    t0 = tt_ * TT
                    for dc in range(NCH):
                        k.mm(psQ, wqs[:, dc * 128:(dc + 1) * 128], fmv(avall, dc, t0, TT),
                             start=(dc == 0), stop=(dc == NCH - 1))
                    k.ts(srow, psQ, boc[:, oc:oc + 1], None, ALU.add)
                    k.dma(s_dram[s, oc * 128:(oc + 1) * 128, t0:t0 + TT], srow,
                          eng="gpsimd")

        with k.scope() as sc:
            rq = sc.sbuf("rq", [128, NCH * T], BF16)
            emit_actquant_fm(k, c, sc, xn, rq)
            w1b = sc.sbuf("w1b", [128, NCH * 128], FP8)
            y1 = sc.sbuf("y1", [128, T + 4])
            y2 = sc.sbuf("y2", [128, T])
            y3 = sc.sbuf("y3", [128, T])
            h2b = sc.sbuf("h2b", [128, T], BF16)
            psF = sc.psum("psF", [128, TT])
            k.memset(y1, 0.0)
            for hc in range(NCHH):
                for dc in range(NCH):
                    k.dma(w1b[:, dc * 128:(dc + 1) * 128],
                          w1q_d[dc * 128:(dc + 1) * 128, hc * 128:(hc + 1) * 128])
                for tt_ in range(NTT):
                    t0 = tt_ * TT
                    for dc in range(NCH):
                        k.mm(psF, w1b[:, dc * 128:(dc + 1) * 128], fmv(rq, dc, t0, TT),
                             start=(dc == 0), stop=(dc == NCH - 1))
                    k.copy(y1[:, 1 + t0: 1 + t0 + TT], psF)
                k.ts(y2, y1[:, 0:T], cwc[:, hc * 3:hc * 3 + 1], None, ALU.mult)
                k.stt(y3, y1[:, 1:T + 1], cwc[:, hc * 3 + 1:hc * 3 + 2], y2, ALU.mult, ALU.add)
                k.stt(y2, y1[:, 2:T + 2], cwc[:, hc * 3 + 2:hc * 3 + 3], y3, ALU.mult, ALU.add)
                k.act(h2b, y2, AF.Silu, bias=cbc[:, hc:hc + 1])
                k.dma(h2_dram[hc * 128:(hc + 1) * 128, :], h2b, eng="gpsimd")

        with k.scope() as sc:
            pstr_q = [sc.psum("ptr0", [128, 128], BF16), sc.psum("ptr1", [128, 128], BF16)]
            h2tm = sc.sbuf("h2tm", [128, DH], BF16)
            hts = sc.sbuf("hts", [128, DH // 2])
            hqfm = sc.sbuf("hqfm", [128, NCHH * TT], BF16)
            hsl_p = [sc.sbuf("hsl0", [128, 128], BF16), sc.sbuf("hsl1", [128, 128], BF16)]
            w2q8 = sc.sbuf("w2q8", [128, NCHH * 1024], FP8)
            gt = sc.sbuf("gt", [128, TT], BF16)
            st2 = sc.sbuf("st2", [128, TT])
            xt2 = sc.sbuf("xt2", [128, TT])
            hcm = sc.sbuf("hcm", [128, 1])
            hcm2 = sc.sbuf("hcm2", [128, 1])
            hcc = sc.sbuf("hcc", [128, 1])
            hcd = sc.sbuf("hcd", [128, 1])
            ps1 = sc.psum("ps1", [128, TT])
            ps2 = sc.psum("ps2", [128, TT])
            for hc in range(NCHH):
                k.dma(w2q8[:, hc * D:(hc + 1) * D], w2q_d[hc * 128:(hc + 1) * 128, :])
            for tt_ in range(NTT):
                t0 = tt_ * TT
                for tci in range(4):
                    tc = tt_ * 4 + tci
                    for hc in range(NCHH):
                        pstr = pstr_q[hc % 2]
                        hsl = hsl_p[hc % 2]
                        k.dma(hsl, h2_dram[hc * 128:(hc + 1) * 128, tc * 128:(tc + 1) * 128])
                        k.tr(pstr, hsl, c["identb"])
                        k.copy(h2tm[:, hc * 128:(hc + 1) * 128], pstr)
                    k.copy(hts, h2tm[:, 0:DH // 2])
                    k.red(hcm, hts, ALU.max, absval=True)
                    k.copy(hts, h2tm[:, DH // 2:DH])
                    k.red(hcm2, hts, ALU.max, absval=True)
                    k.tt(hcm, hcm, hcm2, ALU.max)
                    k.ts(hcm, hcm, 1e-5, None, ALU.max)
                    k.recip(hcc, hcm)
                    k.ts(hcc, hcc, 127.0, None, ALU.mult)
                    k.ts(hcd, hcm, 1.0 / 127.0, None, ALU.mult)
                    for hf in range(2):
                        hx = h2tm[:, hf * (DH // 2):(hf + 1) * (DH // 2)]
                        k.ts(hts, hx, hcc, RTM, ALU.mult, ALU.add)
                        k.ts(hts, hts, RTM, None, ALU.subtract)
                        k.ts(hx, hts, hcd, None, ALU.mult)
                    for hc in range(NCHH):
                        pstr = pstr_q[hc % 2]
                        k.tr(pstr, h2tm[:, hc * 128:(hc + 1) * 128], c["identb"])
                        k.copy(hqfm[:, hc * TT + tci * 128: hc * TT + tci * 128 + 128], pstr)
                for dc in range(NCH):
                    for hc in range(NCHH):
                        k.mm(ps2, w2q8[:, hc * D + dc * 128: hc * D + (dc + 1) * 128],
                             hqfm[:, hc * TT:(hc + 1) * TT],
                             start=(hc == 0), stop=(hc == NCHH - 1))
                    k.act(gt, ps2, AF.Silu, scale=sw2)
                    k.dma(st2, s_dram[s, dc * 128:(dc + 1) * 128, t0:t0 + TT], eng="gpsimd")
                    k.dma(xt2, x_in[s, dc * 128:(dc + 1) * 128, t0:t0 + TT])
                    k.tt(st2, st2, gt, ALU.mult)
                    k.tt(xt2, xt2, st2, ALU.add)
                    k.dma(x_out[s, dc * 128:(dc + 1) * 128, t0:t0 + TT], xt2)

    k.p.finalize()
    return k



def build_L2():
    k = K()
    c = load_consts(k)
    x_in = k.inp("x_in", [NSEQ, D, T])
    rmsw_d = k.inp("l2_rmsw_c", [128, NCH])
    wA_d = k.inp("l2_wA", [D, D])
    bA_d = k.inp("l2_bA_c", [128, NCH])
    wx_d = k.inp("l2_wx", [D, D])
    wsg_d = k.inp("l2_wsg", [D, D])
    wout_d = k.inp("l2_wout", [D, D])
    wo_d = k.inp("l2_wo", [D, D])
    w1T = k.inp("l2_w1T", [D, DH])
    b1c_d = k.inp("l2_b1_c", [128, NCHH])
    w2T = k.inp("l2_w2T", [DH, D])
    b2c_d = k.inp("l2_b2_c", [128, NCH])
    drop_d = k.inp("drop_fm", [NSEQ, D, T], BF16)
    x_out = k.out("x_out", [NSEQ, D, T])
    s_dram = k.out("s_scr2", [NSEQ, D, T], BF16) if DBG \
        else k.dram("s_scr", [NSEQ, D, T], BF16)

    xn = k.sbuf("xn", [128, NCH * T], BF16)
    rows_r = k.sbuf("rows_r", [1, T])
    rows_m = k.sbuf("rows_m", [1, T])
    rmsw = k.sbuf("rmsw", [128, NCH])
    bAc = k.sbuf("bAc", [128, NCH])
    b1q = k.sbuf("b1q", [128, NCHH])
    b2q = k.sbuf("b2q", [128, NCH])
    b8s = k.sbuf("b8s", [128, NCHH], FP8)
    k.dma(rmsw, rmsw_d)
    k.dma(bAc, bA_d)
    k.dma(b1q, b1c_d)
    k.dma(b2q, b2c_d)
    k.act(b8s, b1q, AF.Copy)
    k.act(b1q, b8s, AF.Copy)
    k.act(b8s[:, 0:NCH], b2q, AF.Copy)
    k.act(b2q, b8s[:, 0:NCH], AF.Copy)

    for s in range(NSEQ):
        emit_ln(k, c, s, x_in, xn, rows_keep=(rows_r, rows_m))

        with k.scope() as sc:
            xi = sc.sbuf("xi", [128, NCH * T], BF16)
            yb = sc.sbuf("yb", [128, NCH * T], BF16)
            wsl = sc.sbuf("wsl", [128, 1024], BF16)
            Ac = sc.sbuf("Ac", [128, T])
            Bc = sc.sbuf("Bc", [128, T])
            sidc = sc.sbuf("sidc", [128, T])
            sq2 = sc.sbuf("sq2", [128, TT])
            rrow = sc.sbuf("rrow", [1, TT])
            psP = sc.psum("psP", [128, TT])
            psR2 = sc.psum("psR2", [1, TT])
            psBB = sc.psum("psBB", [128, TT])
            for tt_ in range(NTT):
                t0 = tt_ * TT
                for ci in range(NCH):
                    k.act(sq2, fmv(xn, ci, t0, TT), AF.Square)
                    k.mm(psR2, c["ones_col"], sq2, start=(ci == 0), stop=(ci == NCH - 1))
                k.ts(rrow, psR2, 1.0 / D, 1e-5, ALU.mult, ALU.add)
                k.act(rrow, rrow, AF.Sqrt)
                k.recip(rrow, rrow)
                k.mm(psBB, c["ones_row"], rrow)
                for ci in range(NCH):
                    k.tt(sq2, fmv(xn, ci, t0, TT), psBB, ALU.mult)
                    k.ts(fmv(xi, ci, t0, TT), sq2, rmsw[:, ci:ci + 1], None, ALU.mult)
            for oc in range(NCH):
                for wd, dst, biased in ((wA_d, Ac, True), (wout_d, sidc, False)):
                    for dc in range(NCH):
                        k.dma(wsl[:, dc * 128:(dc + 1) * 128],
                              wd[dc * 128:(dc + 1) * 128, oc * 128:(oc + 1) * 128],
                              eng="gpsimd")
                    for tt_ in range(NTT):
                        t0 = tt_ * TT
                        for dc in range(NCH):
                            k.mm(psP, wsl[:, dc * 128:(dc + 1) * 128], fmv(xi, dc, t0, TT),
                                 start=(dc == 0), stop=(dc == NCH - 1))
                        if biased:
                            k.act(dst[:, t0:t0 + TT], psP, AF.Sigmoid, bias=bAc[:, oc:oc + 1])
                        else:
                            k.act(dst[:, t0:t0 + TT], psP, AF.Sigmoid)
                for dc in range(NCH):
                    k.dma(wsl[:, dc * 128:(dc + 1) * 128],
                          wx_d[dc * 128:(dc + 1) * 128, oc * 128:(oc + 1) * 128],
                          eng="gpsimd")
                for tt_ in range(NTT):
                    t0 = tt_ * TT
                    for dc in range(NCH):
                        k.mm(psP, wsl[:, dc * 128:(dc + 1) * 128], fmv(xi, dc, t0, TT),
                             start=(dc == 0), stop=(dc == NCH - 1))
                    k.copy(Bc[:, t0:t0 + TT], psP)
                k.tt(Bc, Bc, sidc, ALU.mult)
                k.scan(Bc, Ac, Bc, 0.0)
                for dc in range(NCH):
                    k.dma(wsl[:, dc * 128:(dc + 1) * 128],
                          wsg_d[dc * 128:(dc + 1) * 128, oc * 128:(oc + 1) * 128],
                          eng="gpsimd")
                for tt_ in range(NTT):
                    t0 = tt_ * TT
                    for dc in range(NCH):
                        k.mm(psP, wsl[:, dc * 128:(dc + 1) * 128], fmv(xi, dc, t0, TT),
                             start=(dc == 0), stop=(dc == NCH - 1))
                    k.act(sidc[:, t0:t0 + TT], psP, AF.Sigmoid)
                k.tt(fmv(yb, oc), sidc, Bc, ALU.mult)
            for oc in range(NCH):
                for dc in range(NCH):
                    k.dma(wsl[:, dc * 128:(dc + 1) * 128],
                          wo_d[dc * 128:(dc + 1) * 128, oc * 128:(oc + 1) * 128],
                          eng="gpsimd")
                for tt_ in range(NTT):
                    t0 = tt_ * TT
                    for dc in range(NCH):
                        k.mm(psP, wsl[:, dc * 128:(dc + 1) * 128], fmv(yb, dc, t0, TT),
                             start=(dc == 0), stop=(dc == NCH - 1))
                    k.tt(sq2, psP, fmv(xn, oc, t0, TT), ALU.add)
                    k.dma(s_dram[s, oc * 128:(oc + 1) * 128, t0:t0 + TT], sq2,
                          eng="gpsimd")

        with k.scope() as sc:
            w1q8 = sc.sbuf("w1q8", [128, NCH * DH], FP8)
            w2q8 = sc.sbuf("w2q8", [128, NCHH * D], FP8)
            for dc in range(NCH):
                k.dma(w1q8[:, dc * DH:(dc + 1) * DH], w1T[dc * 128:(dc + 1) * 128, :],
                      eng="gpsimd")
            for hc in range(NCHH):
                k.dma(w2q8[:, hc * D:(hc + 1) * D], w2T[hc * 128:(hc + 1) * 128, :],
                      eng="gpsimd")
            xq8 = sc.sbuf("xq8", [128, NCH * TT], FP8)
            xf = sc.sbuf("xf", [128, TT])
            hb = sc.sbuf("hb", [128, NCHH * TT], BF16)
            hq8 = sc.sbuf("hq8", [128, NCHH * TT], FP8)
            gt = sc.sbuf("gt", [128, TT], BF16)
            st2 = sc.sbuf("st2", [128, TT])
            xt2 = sc.sbuf("xt2", [128, TT])
            dt2 = sc.sbuf("dt2", [128, TT], BF16)
            dpt = sc.sbuf("dpt", [128, TT], BF16)
            psR3 = sc.psum("psR3", [128, TT])
            psM3 = sc.psum("psM3", [128, TT])
            ps1 = sc.psum("ps1", [128, TT])
            ps2 = sc.psum("ps2", [128, TT])
            for tt_ in range(NTT):
                t0 = tt_ * TT
                k.mm(psR3, c["ones_row"], rows_r[:, t0:t0 + TT])
                k.mm(psM3, c["ones_row"], rows_m[:, t0:t0 + TT])
                for ci in range(NCH):
                    k.dma(xt2, x_in[s, ci * 128:(ci + 1) * 128, t0:t0 + TT])
                    k.tt(xf, xt2, psR3, ALU.mult)
                    k.tt(xf, xf, psM3, ALU.subtract)
                    k.act(xq8[:, ci * TT:(ci + 1) * TT], xf, AF.Copy)
                for hc in range(NCHH):
                    for dc in range(NCH):
                        k.mm(ps1, w1q8[:, dc * DH + hc * 128: dc * DH + (hc + 1) * 128],
                             xq8[:, dc * TT:(dc + 1) * TT],
                             start=(dc == 0), stop=(dc == NCH - 1))
                    k.act(hb[:, hc * TT:(hc + 1) * TT], ps1, AF.Relu,
                          bias=b1q[:, hc:hc + 1])
                    k.act(hq8[:, hc * TT:(hc + 1) * TT], hb[:, hc * TT:(hc + 1) * TT],
                          AF.Copy)
                for dc in range(NCH):
                    for hc in range(NCHH):
                        k.mm(ps2, w2q8[:, hc * D + dc * 128: hc * D + (dc + 1) * 128],
                             hq8[:, hc * TT:(hc + 1) * TT],
                             start=(hc == 0), stop=(hc == NCHH - 1))
                    k.ts(dt2, ps2, b2q[:, dc:dc + 1], None, ALU.add)
                    k.dma(dpt, drop_d[s, dc * 128:(dc + 1) * 128, t0:t0 + TT], eng="sync")
                    k.tt(xf, dt2, dpt, ALU.mult)
                    k.act(gt, xf, AF.Silu)
                    k.dma(st2, s_dram[s, dc * 128:(dc + 1) * 128, t0:t0 + TT], eng="gpsimd")
                    k.dma(xt2, x_in[s, dc * 128:(dc + 1) * 128, t0:t0 + TT])
                    k.tt(st2, st2, gt, ALU.mult)
                    k.tt(xt2, xt2, st2, ALU.add)
                    k.dma(x_out[s, dc * 128:(dc + 1) * 128, t0:t0 + TT], xt2)

    k.p.finalize()
    return k



def emit_ln_inner(k, c, xn, dst, gcol, sc_unused=None):
    sc = k.scope()
    sc.__enter__()
    sq = sc.sbuf("li_sq", [128, TT])
    rows = sc.sbuf("li_rows", [1, 4 * TT])
    psSx = sc.psum("li_psx", [1, TT])
    psSq = sc.psum("li_psq", [1, TT])
    psR = sc.psum("li_psr", [128, TT])
    psM = sc.psum("li_psm", [128, TT])
    for tt_ in range(NTT):
        t0 = tt_ * TT
        for ci in range(NCH):
            k.mm(psSx, c["ones_col_b"], fmv(xn, ci, t0, TT),
                 start=(ci == 0), stop=(ci == NCH - 1))
        for ci in range(NCH):
            k.act(sq, fmv(xn, ci, t0, TT), AF.Square)
            k.mm(psSq, c["ones_col"], sq, start=(ci == 0), stop=(ci == NCH - 1))
        rmu = rows[:, 0:TT]
        rvar = rows[:, TT:2 * TT]
        rr = rows[:, 2 * TT:3 * TT]
        rm = rows[:, 3 * TT:4 * TT]
        k.ts(rmu, psSx, 1.0 / D, None, ALU.mult)
        k.ts(rvar, psSq, 1.0 / D, None, ALU.mult)
        k.tt(rr, rmu, rmu, ALU.mult)
        k.tt(rvar, rvar, rr, ALU.subtract)
        k.act(rr, rvar, AF.Sqrt, bias=c["eps_ln"])
        k.recip(rr, rr)
        k.tt(rm, rmu, rr, ALU.mult)
        k.mm(psR, c["ones_row"], rr)
        k.mm(psM, c["ones_row"], rm)
        for ci in range(NCH):
            k.tt(sq, fmv(xn, ci, t0, TT), psR, ALU.mult)
            k.tt(sq, sq, psM, ALU.subtract)
            k.ts(fmv(dst, ci, t0, TT), sq, gcol[:, ci:ci + 1], None, ALU.mult)
    sc.__exit__(None, None, None)


def emit_headln(k, c, pin, pout, gcol, bufs, scale_col=None):
    (sq2, rows4, psR1, psRb) = bufs
    for tt_ in range(NTT):
        t0 = tt_ * TT
        k.mm(psR1, c["ones_col"][0:64, :], pin[:, t0:t0 + TT])
        rmu = rows4[:, 0:TT]
        k.ts(rmu, psR1, 1.0 / HD, None, ALU.mult)
        k.act(sq2, pin[:, t0:t0 + TT], AF.Square)
        k.mm(psR1, c["ones_col"][0:64, :], sq2)
        rvar = rows4[:, TT:2 * TT]
        rr = rows4[:, 2 * TT:3 * TT]
        rm = rows4[:, 3 * TT:4 * TT]
        k.ts(rvar, psR1, 1.0 / HD, None, ALU.mult)
        k.tt(rr, rmu, rmu, ALU.mult)
        k.tt(rvar, rvar, rr, ALU.subtract)
        k.act(rr, rvar, AF.Sqrt, bias=c["eps_ln"])
        k.recip(rr, rr)
        k.tt(rm, rmu, rr, ALU.mult)
        k.mm(psRb[0:64, :], c["ones_row"][:, 0:64], rr)
        k.tt(sq2, pin[:, t0:t0 + TT], psRb[0:64, :], ALU.mult)
        k.mm(psRb[0:64, :], c["ones_row"][:, 0:64], rm)
        k.tt(sq2, sq2, psRb[0:64, :], ALU.subtract)
        k.ts(sq2, sq2, gcol, None, ALU.mult)
        if scale_col is not None:
            k.ts(pout[:, t0:t0 + TT], sq2, scale_col, None, ALU.mult)
        else:
            k.copy(pout[:, t0:t0 + TT], sq2)


def emit_l3_head_attn(k, c, bufs, kaug, qaug, vslot, outdst, qmask_on):
    (sqsc, rowm, sc1, sc2, mcol, srow, srow0, bcs, pt, psS, psA, psR, psB, qmb,
     psS2, pt2) = bufs
    for aug, scx in ((qaug, sc1), (kaug, sc2)):
        for tt_ in range(NTT):
            k.act(sqsc, aug[0:64, tt_ * TT:(tt_ + 1) * TT], AF.Square)
            k.mm(psR, c["ones_col"][0:64, :], sqsc)
            k.copy(rowm[:, tt_ * TT:(tt_ + 1) * TT], psR)
        k.red(scx, rowm, ALU.max)
    k.act(sc1, sc1, AF.Sqrt)
    k.act(sc2, sc2, AF.Sqrt)
    k.tt(sc1, sc1, sc2, ALU.mult)
    k.ts(sc1, sc1, -1.0, None, ALU.mult)
    k.mm(psB, c["ones_row"], sc1)
    k.copy(mcol, psB)
    for qt in range(NTT):
        q0 = qt * TT
        for kc in range(NTC):
            pS = psS if kc % 2 == 0 else psS2
            pT = pt if kc % 2 == 0 else pt2
            k.mm(pS, kaug[:, kc * 128:(kc + 1) * 128][0:65, :],
                 qaug[0:65, q0:q0 + TT])
            k.act(pT, pS, AF.Exp, bias=mcol)
            k.mm(psA[0:65, :], vslot(kc), pT,
                 start=(kc == 0), stop=(kc == NTC - 1))
        k.copy(srow[64:65, :], psA[64:65, :])
        k.dma(srow0, srow[64:65, :])
        k.recip(srow0, srow0)
        k.mm(psS[0:64, :], c["ones_row"][:, 0:64], srow0)
        k.copy(bcs, psS[0:64, :])
        k.tt(bcs, psA[0:64, :], bcs, ALU.mult)
        if qmask_on is not None:
            k.mm(psS[0:64, :], c["ones_row"][:, 0:64], qmask_on[:, q0:q0 + TT])
            k.tt(bcs, bcs, psS[0:64, :], ALU.mult)
        outdst(qt, bcs)


def l3a_common_inputs(k):
    d = {}
    d["x_in"] = k.inp("x_in", [NSEQ, D, T])
    d["wqT"] = k.inp("l3_wqT", [D, D])
    d["bq_c"] = k.inp("l3_bq_c", [128, NCH])
    d["wkT"] = k.inp("l3_wkT", [D, D])
    d["wvT"] = k.inp("l3_wvT", [D, D])
    d["bv_c"] = k.inp("l3_bv_c", [128, NCH])
    d["lna_c"] = k.inp("l3_lna_c", [128, NCH])
    d["lnb_c"] = k.inp("l3_lnb_c", [128, NCH])
    d["lnc"] = k.inp("l3_lnc", [HD, 1])
    d["lnd"] = k.inp("l3_lnd", [HD, 1])
    d["lqT"] = k.inp("l3_lqT", [HD, HD])
    d["lqb"] = k.inp("l3_lqb", [HD, 1])
    d["lkT"] = k.inp("l3_lkT", [HD, HD])
    d["lkb"] = k.inp("l3_lkb", [HD, 1])
    d["lvT"] = k.inp("l3_lvT", [HD, HD])
    d["lvb"] = k.inp("l3_lvb", [1, HD])
    d["sc3"] = k.inp("l3_sc3", [1, 3])
    return d


SC64 = float(64.0 ** -0.25)
CNT_L3 = float(16 * NHEAD * T * HD)


def emit_tscale(k, c, temp, tsc, t1, t2):
    k.ts(t1, temp, 1e-30, None, ALU.max)
    k.act(t1, t1, AF.Sqrt)
    k.recip(t1, t1)
    k.ts(t2, temp, 0.0, None, ALU.is_gt)
    k.tt(t1, t1, t2, ALU.mult)
    k.ts(t2, t2, -1.0, 1.0, ALU.mult, ALU.add)
    k.tt(tsc, t1, t2, ALU.add)


def build_L3a():
    k = K()
    c = load_consts(k)
    d = l3a_common_inputs(k)
    bqh_d = k.inp("l3_bq_h", [HD, NHEAD])
    bvh_d = k.inp("l3_bv_h", [HD, NHEAD])
    xn3_o = k.out("xn3", [NSEQ, D, T], BF16)
    sk_o = k.out("sk3", [NSEQ, NHEAD, HD, T], FP16)
    vtm_o = k.out("vtm3", [NSEQ, 128, NTC * NHEAD * 65], BF16)
    out0_o = k.out("out0", [NSEQ, D, T], BF16)
    sq_o = k.out("sq3", [NSEQ, NHEAD, HD, T], FP16) if DBG else None
    qc1_o = k.out("qc1", [NSEQ, D, T], BF16)
    d0_o = k.out("d0_part", [1, 1])

    xn = k.sbuf("xn", [128, NCH * T], BF16)
    lna_c = k.sbuf("lna_c", [128, NCH])
    lnb_c = k.sbuf("lnb_c", [128, NCH])
    bqh = k.sbuf("bqh", [HD, NHEAD])
    bvh = k.sbuf("bvh", [HD, NHEAD])
    lncc = k.sbuf("lncc", [HD, 1])
    lndc = k.sbuf("lndc", [HD, 1])
    lqTb = k.sbuf("lqTb", [HD, HD], BF16)
    lkTb = k.sbuf("lkTb", [HD, HD], BF16)
    lvTb = k.sbuf("lvTb", [HD, HD], BF16)
    lqbc = k.sbuf("lqbc", [HD, 1])
    lkbc = k.sbuf("lkbc", [HD, 1])
    lvbb = k.sbuf("lvbb", [128, HD])
    sc3s = k.sbuf("sc3s", [1, 3])
    tsc0 = k.sbuf("tsc0", [1, 1])
    tsa = k.sbuf("tsa", [1, 1])
    tsb = k.sbuf("tsb", [1, 1])
    sqsc_col = k.sbuf("sqsc_col", [HD, 1])
    acc64 = k.sbuf("acc64", [HD, 1])
    d0sb = k.sbuf("d0sb", [1, 1])
    k.dma(lna_c, d["lna_c"])
    k.dma(lnb_c, d["lnb_c"])
    k.dma(bqh, bqh_d)
    k.dma(bvh, bvh_d)
    k.dma(lncc, d["lnc"])
    k.dma(lndc, d["lnd"])
    k.dma(lqTb, d["lqT"], eng="gpsimd")
    k.dma(lkTb, d["lkT"], eng="gpsimd")
    k.dma(lvTb, d["lvT"], eng="gpsimd")
    k.dma(lqbc, d["lqb"])
    k.dma(lkbc, d["lkb"])
    k.dma(sc3s, d["sc3"])
    psI = k.psum("psI", [128, HD])
    lvbrow = k.sbuf("lvbrow", [1, HD])
    k.dma(lvbrow, d["lvb"])
    k.mm(psI, c["ones_row"], lvbrow)
    k.copy(lvbb, psI)
    emit_tscale(k, c, sc3s[:, 1:2], tsc0, tsa, tsb)
    k.mm(psI[0:HD, 0:1], c["ones_row"][:, 0:HD], tsc0)
    k.copy(sqsc_col, psI[0:HD, 0:1])
    k.ts(sqsc_col, sqsc_col, 0.125, None, ALU.mult)
    k.memset(acc64, 0.0)

    for s in range(NSEQ):
        emit_ln(k, c, s, d["x_in"], xn)
        for ci in range(NCH):
            k.dma(xn3_o[s, ci * 128:(ci + 1) * 128, :], fmv(xn, ci))
        with k.scope() as sc:
            xna = sc.sbuf("xna", [128, NCH * T], BF16)
            xnb = sc.sbuf("xnb", [128, NCH * T], BF16)
            emit_ln_inner(k, c, xn, xna, lna_c, sc)
            emit_ln_inner(k, c, xn, xnb, lnb_c, sc)
            vtm = sc.sbuf("vtm", [128, NTC * NHEAD * 65], BF16)
            k.memset(vtm, 1.0)
            wqh = sc.sbuf("wqh", [128, NCH * HD], BF16)
            wkh = sc.sbuf("wkh", [128, NCH * HD], BF16)
            wvh = sc.sbuf("wvh", [128, NCH * HD], BF16)
            qhb = sc.sbuf("qhb", [HD, T], BF16)
            khb = sc.sbuf("khb", [HD, T], BF16)
            vhb = sc.sbuf("vhb", [HD, T], BF16)
            kpf = sc.sbuf("kpf", [HD, T])
            kaug = sc.sbuf("kaug", [128, T], FP16)
            qaug = sc.sbuf("qaug", [128, T], FP16)
            sqsc = sc.sbuf("sqsc", [64, TT])
            rows4 = sc.sbuf("rows4", [1, 4 * TT])
            rowm = sc.sbuf("rowm", [1, T])
            sc1 = sc.sbuf("sc1", [1, 1])
            sc2 = sc.sbuf("sc2", [1, 1])
            mcol = sc.sbuf("mcol", [128, 1])
            srow = sc.sbuf("srow", [128, TT])
            srow0 = sc.sbuf("srow0", [1, TT])
            bcs = sc.sbuf("bcs", [64, TT])
            o0b = sc.sbuf("o0b", [64, TT], BF16)
            accq = sc.sbuf("accq", [64, 1])
            pt = sc.sbuf("pt", [128, TT], BF16)
            pt2 = sc.sbuf("pt2", [128, TT], BF16)
            psS = sc.psum("psS", [128, TT])
            psS2 = sc.psum("psS2", [128, TT])
            psA = sc.psum("psA", [128, TT])
            psR = sc.psum("psR", [1, TT])
            psB = sc.psum("psB", [128, 1])
            psH = sc.psum("psH", [64, TT])
            psV = sc.psum("psV", [128, HD])
            k.memset(kaug[64:66, :], 0.0)
            k.memset(qaug[64:66, :], 1.0)
            attn_bufs = (sqsc, rowm, sc1, sc2, mcol, srow, srow0, bcs, pt,
                         psS, psA, psR, psB, None, psS2, pt2)
            hl_bufs = (sqsc, rows4, psR, psS)
            for h in range(NHEAD):
                for dc in range(NCH):
                    k.dma(wqh[:, dc * HD:(dc + 1) * HD],
                          d["wqT"][dc * 128:(dc + 1) * 128, h * HD:(h + 1) * HD],
                          eng="gpsimd")
                    k.dma(wkh[:, dc * HD:(dc + 1) * HD],
                          d["wkT"][dc * 128:(dc + 1) * 128, h * HD:(h + 1) * HD],
                          eng="gpsimd")
                    k.dma(wvh[:, dc * HD:(dc + 1) * HD],
                          d["wvT"][dc * 128:(dc + 1) * 128, h * HD:(h + 1) * HD],
                          eng="gpsimd")
                for tt_ in range(NTT):
                    t0 = tt_ * TT
                    for dc in range(NCH):
                        k.mm(psH, wqh[:, dc * HD:(dc + 1) * HD], fmv(xna, dc, t0, TT),
                             start=(dc == 0), stop=(dc == NCH - 1))
                    k.ts(qhb[:, t0:t0 + TT], psH, bqh[:, h:h + 1], SC64, ALU.add, ALU.mult)
                    for dc in range(NCH):
                        k.mm(psH, wkh[:, dc * HD:(dc + 1) * HD], fmv(xnb, dc, t0, TT),
                             start=(dc == 0), stop=(dc == NCH - 1))
                    k.ts(khb[:, t0:t0 + TT], psH, SC64, None, ALU.mult)
                    for dc in range(NCH):
                        k.mm(psH, wvh[:, dc * HD:(dc + 1) * HD], fmv(xnb, dc, t0, TT),
                             start=(dc == 0), stop=(dc == NCH - 1))
                    k.ts(vhb[:, t0:t0 + TT], psH, bvh[:, h:h + 1], None, ALU.add)
                for tc in range(NTC):
                    k.mm(psV, vhb[:, tc * 128:(tc + 1) * 128], lvTb)
                    k.tt(srow[:, 0:HD], psV, lvbb, ALU.add)
                    k.copy(vtm[:, (tc * NHEAD + h) * 65:(tc * NHEAD + h) * 65 + 64],
                           srow[:, 0:HD])
                for tt_ in range(NTT):
                    t0 = tt_ * TT
                    k.mm(psH, lkTb, khb[:, t0:t0 + TT])
                    k.ts(kpf[:, t0:t0 + TT], psH, lkbc, None, ALU.add)
                emit_headln(k, c, kpf, kaug[0:64, :], lndc, hl_bufs)
                k.dma(sk_o[s, h], kaug[0:64, :])
                for tt_ in range(NTT):
                    t0 = tt_ * TT
                    k.mm(psH, lqTb, qhb[:, t0:t0 + TT])
                    k.ts(kpf[:, t0:t0 + TT], psH, lqbc, None, ALU.add)
                emit_headln(k, c, kpf, qaug[0:64, :], lncc, hl_bufs, scale_col=sqsc_col)
                if DBG:
                    k.dma(sq_o[s, h], qaug[0:64, :])

                def dst(qt, bcs_ap, h=h, s=s):
                    q0 = qt * TT
                    r0 = h * HD
                    k.copy(o0b, bcs_ap)
                    k.dma(out0_o[s, r0:r0 + HD, q0:q0 + TT], o0b)
                    k.tt(o0b, bcs_ap, qhb[:, q0:q0 + TT], ALU.add)
                    k.dma(qc1_o[s, r0:r0 + HD, q0:q0 + TT], o0b)
                    k.red(accq, bcs_ap, ALU.add, absval=True)
                    k.tt(acc64, acc64, accq, ALU.add)
                emit_l3_head_attn(k, c, attn_bufs, kaug, qaug,
                                  lambda kc, h=h: vtm[:, (kc * NHEAD + h) * 65:
                                                      (kc * NHEAD + h) * 65 + 65],
                                  dst, None)
            k.dma(vtm_o[s], vtm)
    psd = k.psum("psd", [1, 1])
    k.mm(psd, c["ones_col"][0:64, :], acc64)
    k.ts(d0sb, psd, 1.0 / CNT_L3, None, ALU.mult)
    k.dma(d0_o, d0sb)
    k.p.finalize()
    return k





def emit_l3_scalars_cw1(k, c, d0row, sc3s, cw1, tmp):
    k.red(tmp["d0"], d0row, ALU.add)
    m1 = tmp["a"]
    ge = tmp["b"]
    k.ts(m1, tmp["d0"], 0.05, None, ALU.is_gt)
    k.ts(ge, tmp["d0"], 0.001, None, ALU.is_ge)
    k.ts(tmp["c"], ge, 205.0, 1843.0, ALU.mult, ALU.add)
    k.ts(tmp["d"], m1, -1.0, 1.0, ALU.mult, ALU.add)
    k.tt(tmp["c"], tmp["c"], tmp["d"], ALU.mult)
    k.ts(cw1, m1, 2048.0, None, ALU.mult)
    k.tt(cw1, cw1, tmp["c"], ALU.add)


def emit_l3_iter_prep(k, c, cwv, tscv, iota, sqsc_col, mb_row, qm_row, psI):
    k.ts(mb_row, iota, cwv, None, ALU.is_ge)
    k.ts(qm_row, mb_row, -1.0, 1.0, ALU.mult, ALU.add)
    k.ts(mb_row, mb_row, -60000.0, None, ALU.mult)
    k.mm(psI[0:HD, 0:1], c["ones_row"][:, 0:HD], tscv)
    k.copy(sqsc_col, psI[0:HD, 0:1])
    k.ts(sqsc_col, sqsc_col, 0.125, None, ALU.mult)


def build_L3b():
    k = K()
    c = load_consts(k)
    d0r_d = k.inp("d0_all", [1, 8])
    qc1_d = k.inp("qc1", [NSEQ, D, T], BF16)
    sk_d = k.inp("sk3", [NSEQ, NHEAD, HD, T], FP16)
    vtm_d = k.inp("vtm3", [NSEQ, 128, NTC * NHEAD * 65], BF16)
    out0_d = k.inp("out0", [NSEQ, D, T], BF16)
    lqT_d = k.inp("l3_lqT", [HD, HD])
    lqb_d = k.inp("l3_lqb", [HD, 1])
    lnc_d = k.inp("l3_lnc", [HD, 1])
    sc3_d = k.inp("l3_sc3", [1, 3])
    iota_d = k.inp("iota_row", [1, T])
    out1_o = k.out("out1", [NSEQ, D, T], BF16)
    d1_o = k.out("d1_part", [1, 1])

    lqTb = k.sbuf("lqTb", [HD, HD], BF16)
    lqbc = k.sbuf("lqbc", [HD, 1])
    lncc = k.sbuf("lncc", [HD, 1])
    sc3s = k.sbuf("sc3s", [1, 3])
    d0row = k.sbuf("d0row", [1, 8])
    iota = k.sbuf("iota", [1, T])
    mb_row = k.sbuf("mb_row", [1, T])
    qm_row = k.sbuf("qm_row", [1, T])
    cw1 = k.sbuf("cw1", [1, 1])
    tsc1 = k.sbuf("tsc1", [1, 1])
    sqsc_col = k.sbuf("sqsc_col", [HD, 1])
    acc64 = k.sbuf("acc64", [HD, 1])
    d1sb = k.sbuf("d1sb", [1, 1])
    tmp = {n: k.sbuf(f"tmp_{n}", [1, 1]) for n in ("d0", "a", "b", "c", "d", "e")}
    k.dma(lqTb, lqT_d, eng="gpsimd")
    k.dma(lqbc, lqb_d)
    k.dma(lncc, lnc_d)
    k.dma(sc3s, sc3_d)
    k.dma(d0row, d0r_d)
    k.dma(iota, iota_d)
    psI = k.psum("psI", [128, HD])
    emit_l3_scalars_cw1(k, c, d0row, sc3s, cw1, tmp)
    k.ts(tmp["e"], sc3s[:, 1:2], 0.005, None, ALU.add)
    emit_tscale(k, c, tmp["e"], tsc1, tmp["a"], tmp["b"])
    emit_l3_iter_prep(k, c, cw1, tsc1, iota, sqsc_col, mb_row, qm_row, psI)
    k.memset(acc64, 0.0)

    for s in range(NSEQ):
        with k.scope() as sc:
            vtm = sc.sbuf("vtm", [128, NTC * NHEAD * 65], BF16)
            k.dma(vtm, vtm_d[s])
            qhb = sc.sbuf("qhb", [HD, T], BF16)
            kpf = sc.sbuf("kpf", [HD, T])
            kaug = sc.sbuf("kaug", [128, T], FP16)
            qaug = sc.sbuf("qaug", [128, T], FP16)
            o0t = sc.sbuf("o0t", [HD, TT], BF16)
            o1b = sc.sbuf("o1b", [HD, TT], BF16)
            dtl = sc.sbuf("dtl", [HD, TT])
            sqsc = sc.sbuf("sqsc", [64, TT])
            rows4 = sc.sbuf("rows4", [1, 4 * TT])
            rowm = sc.sbuf("rowm", [1, T])
            sc1 = sc.sbuf("sc1", [1, 1])
            sc2 = sc.sbuf("sc2", [1, 1])
            mcol = sc.sbuf("mcol", [128, 1])
            srow = sc.sbuf("srow", [128, TT])
            srow0 = sc.sbuf("srow0", [1, TT])
            bcs = sc.sbuf("bcs", [64, TT])
            accq = sc.sbuf("accq", [64, 1])
            pt = sc.sbuf("pt", [128, TT], BF16)
            pt2 = sc.sbuf("pt2", [128, TT], BF16)
            psS = sc.psum("psS", [128, TT])
            psS2 = sc.psum("psS2", [128, TT])
            psA = sc.psum("psA", [128, TT])
            psR = sc.psum("psR", [1, TT])
            psB = sc.psum("psB", [128, 1])
            psH = sc.psum("psH", [64, TT])
            k.memset(qaug[64:66, :], 1.0)
            attn_bufs = (sqsc, rowm, sc1, sc2, mcol, srow, srow0, bcs, pt,
                         psS, psA, psR, psB, None, psS2, pt2)
            hl_bufs = (sqsc, rows4, psR, psS)
            for h in range(NHEAD):
                k.dma(kaug[0:64, :], sk_d[s, h])
                k.copy(kaug[64:65, :], mb_row)
                k.dma(qhb, qc1_d[s, h * HD:(h + 1) * HD, :])
                for tt_ in range(NTT):
                    t0 = tt_ * TT
                    k.mm(psH, lqTb, qhb[:, t0:t0 + TT])
                    k.ts(kpf[:, t0:t0 + TT], psH, lqbc, None, ALU.add)
                emit_headln(k, c, kpf, qaug[0:64, :], lncc, hl_bufs, scale_col=sqsc_col)

                def dst(qt, bcs_ap, h=h, s=s):
                    q0 = qt * TT
                    r0 = h * HD
                    k.copy(o1b, bcs_ap)
                    k.dma(out1_o[s, r0:r0 + HD, q0:q0 + TT], o1b)
                    k.dma(o0t, out0_d[s, r0:r0 + HD, q0:q0 + TT])
                    k.tt(dtl, bcs_ap, o0t, ALU.subtract)
                    k.red(accq, dtl, ALU.add, absval=True)
                    k.tt(acc64, acc64, accq, ALU.add)
                emit_l3_head_attn(k, c, attn_bufs, kaug, qaug,
                                  lambda kc, h=h: vtm[:, (kc * NHEAD + h) * 65:
                                                      (kc * NHEAD + h) * 65 + 65],
                                  dst, qm_row)
    psd = k.psum("psd", [1, 1])
    k.mm(psd, c["ones_col"][0:64, :], acc64)
    k.ts(d1sb, psd, 1.0 / CNT_L3, None, ALU.mult)
    k.dma(d1_o, d1sb)
    k.p.finalize()
    return k


def build_L3c():
    k = K()
    c = load_consts(k)
    d0r_d = k.inp("d0_all", [1, 8])
    d1r_d = k.inp("d1_all", [1, 8])
    x_in = k.inp("x_in", [NSEQ, D, T])
    qc1_d = k.inp("qc1", [NSEQ, D, T], BF16)
    out1_d = k.inp("out1", [NSEQ, D, T], BF16)
    sk_d = k.inp("sk3", [NSEQ, NHEAD, HD, T], FP16)
    vtm_d = k.inp("vtm3", [NSEQ, 128, NTC * NHEAD * 65], BF16)
    xn3_d = k.inp("xn3", [NSEQ, D, T], BF16)
    lqT_d = k.inp("l3_lqT", [HD, HD])
    lqb_d = k.inp("l3_lqb", [HD, 1])
    lnc_d = k.inp("l3_lnc", [HD, 1])
    sc3_d = k.inp("l3_sc3", [1, 3])
    iota_d = k.inp("iota_row", [1, T])
    woT_d = k.inp("l3_woT", [D, D])
    boc_d = k.inp("l3_bo_c", [128, NCH])
    wg_d = k.inp("l3_wgate", [D, NEXP])
    bg_d = k.inp("l3_bgate", [NEXP, 1])
    W1_d = k.inp("l3_W1", [NEXP, D, DH])
    b1c_d = k.inp("l3_b1_cols", [NEXP, 128, NCHH])
    W2_d = k.inp("l3_W2", [NEXP, DH, D])
    b2c_d = k.inp("l3_b2_cols", [NEXP, 128, NCH])
    triu_d = k.inp("triu", [128, 128])
    iota8_d = k.inp("iota8", [128, NEXP])
    x_out = k.out("x_out", [NSEQ, D, T])
    s_dram = k.dram("s_scr", [NSEQ, D, T], BF16)
    packx_d = k.dram("packx", [NEXP * CAP + 128, D], BF16)
    packy_d = k.dram("packy", [NEXP * CAP + 128, D], BF16)

    lqTb = k.sbuf("lqTb", [HD, HD], BF16)
    lqbc = k.sbuf("lqbc", [HD, 1])
    lncc = k.sbuf("lncc", [HD, 1])
    sc3s = k.sbuf("sc3s", [1, 3])
    d0row = k.sbuf("d0row", [1, 8])
    d1row = k.sbuf("d1row", [1, 8])
    iota = k.sbuf("iota", [1, T])
    mb_row = k.sbuf("mb_row", [1, T])
    qm_row = k.sbuf("qm_row", [1, T])
    cw1 = k.sbuf("cw1", [1, 1])
    cw2 = k.sbuf("cw2", [1, 1])
    tsc2 = k.sbuf("tsc2", [1, 1])
    sqsc_col = k.sbuf("sqsc_col", [HD, 1])
    c1s = k.sbuf("c1s", [1, 1])
    u1s = k.sbuf("u1s", [1, 1])
    u1col = k.sbuf("u1col", [HD, 1])
    c1col = k.sbuf("c1col", [HD, 1])
    boc = k.sbuf("boc", [128, NCH])
    tmp = {n: k.sbuf(f"tmp_{n}", [1, 1]) for n in ("d0", "a", "b", "c", "d", "e", "f", "g")}
    k.dma(lqTb, lqT_d, eng="gpsimd")
    k.dma(lqbc, lqb_d)
    k.dma(lncc, lnc_d)
    k.dma(sc3s, sc3_d)
    k.dma(d0row, d0r_d)
    k.dma(d1row, d1r_d)
    k.dma(iota, iota_d)
    k.dma(boc, boc_d)
    psI = k.psum("psI", [128, HD])
    emit_l3_scalars_cw1(k, c, d0row, sc3s, cw1, tmp)
    d1 = tmp["f"]
    k.red(d1, d1row, ALU.add)
    k.ts(tmp["a"], d1, sc3s[:, 2:3], None, ALU.mult)
    k.tt(tmp["a"], tmp["a"], sc3s[:, 0:1], ALU.add)
    k.ts(u1s, d1, tmp["a"], None, ALU.is_ge)
    k.ts(c1s, u1s, -1.0, 1.0, ALU.mult, ALU.add)
    k.ts(tmp["a"], cw1, 1.1, -0.5, ALU.mult, ALU.add)
    k.ts(tmp["a"], tmp["a"], RTM, None, ALU.add)
    k.ts(tmp["a"], tmp["a"], RTM, None, ALU.subtract)
    k.ts(tmp["b"], cw1, 0.9, -0.5, ALU.mult, ALU.add)
    k.ts(tmp["b"], tmp["b"], RTM, None, ALU.add)
    k.ts(tmp["b"], tmp["b"], RTM, None, ALU.subtract)
    k.ts(tmp["c"], d1, 0.05, None, ALU.is_gt)
    k.ts(tmp["d"], d1, 0.001, None, ALU.is_ge)
    k.ts(tmp["e"], cw1, 64.0, None, ALU.is_gt)
    k.ts(tmp["d"], tmp["d"], -1.0, 1.0, ALU.mult, ALU.add)
    k.tt(tmp["d"], tmp["d"], tmp["e"], ALU.mult)
    k.tt(tmp["b"], tmp["b"], tmp["d"], ALU.mult)
    k.ts(tmp["d"], tmp["d"], -1.0, 1.0, ALU.mult, ALU.add)
    k.ts(tmp["g"], cw1, 1.0, None, ALU.mult)
    k.tt(tmp["g"], tmp["g"], tmp["d"], ALU.mult)
    k.tt(tmp["b"], tmp["b"], tmp["g"], ALU.add)
    k.tt(tmp["a"], tmp["a"], tmp["c"], ALU.mult)
    k.ts(tmp["c"], tmp["c"], -1.0, 1.0, ALU.mult, ALU.add)
    k.tt(tmp["b"], tmp["b"], tmp["c"], ALU.mult)
    k.tt(tmp["a"], tmp["a"], tmp["b"], ALU.add)
    k.ts(tmp["a"], tmp["a"], 64.0, 2048.0, ALU.max, ALU.min)
    k.tt(tmp["a"], tmp["a"], u1s, ALU.mult)
    k.ts(tmp["b"], cw1, 1.0, None, ALU.mult)
    k.tt(tmp["b"], tmp["b"], c1s, ALU.mult)
    k.tt(cw2, tmp["a"], tmp["b"], ALU.add)
    k.ts(tmp["e"], u1s, 0.005, None, ALU.mult)
    k.ts(tmp["g"], sc3s[:, 1:2], 0.005, None, ALU.add)
    k.tt(tmp["g"], tmp["g"], tmp["e"], ALU.add)
    emit_tscale(k, c, tmp["g"], tsc2, tmp["a"], tmp["b"])
    emit_l3_iter_prep(k, c, cw2, tsc2, iota, sqsc_col, mb_row, qm_row, psI)
    k.mm(psI[0:HD, 0:1], c["ones_row"][:, 0:HD], u1s)
    k.copy(u1col, psI[0:HD, 0:1])
    k.mm(psI[0:HD, 0:1], c["ones_row"][:, 0:HD], c1s)
    k.copy(c1col, psI[0:HD, 0:1])

    for s in range(NSEQ):
        with k.scope() as sc:
            vtm = sc.sbuf("vtm", [128, NTC * NHEAD * 65], BF16)
            k.dma(vtm, vtm_d[s])
            attnall = sc.sbuf("attnall", [128, NCH * T], BF16)
            qhb = sc.sbuf("qhb", [HD, T], BF16)
            kpf = sc.sbuf("kpf", [HD, T])
            kaug = sc.sbuf("kaug", [128, T], FP16)
            qaug = sc.sbuf("qaug", [128, T], FP16)
            o1t = sc.sbuf("o1t", [HD, T], BF16)
            sqsc = sc.sbuf("sqsc", [64, TT])
            rows4 = sc.sbuf("rows4", [1, 4 * TT])
            rowm = sc.sbuf("rowm", [1, T])
            sc1 = sc.sbuf("sc1", [1, 1])
            sc2 = sc.sbuf("sc2", [1, 1])
            mcol = sc.sbuf("mcol", [128, 1])
            srow = sc.sbuf("srow", [128, TT])
            srow0 = sc.sbuf("srow0", [1, TT])
            bcs = sc.sbuf("bcs", [64, TT])
            att = sc.sbuf("att", [64, TT], BF16)
            pt = sc.sbuf("pt", [128, TT], BF16)
            pt2 = sc.sbuf("pt2", [128, TT], BF16)
            wqs = sc.sbuf("wqs", [128, 1024], BF16)
            psS = sc.psum("psS", [128, TT])
            psS2 = sc.psum("psS2", [128, TT])
            psA = sc.psum("psA", [128, TT])
            psR = sc.psum("psR", [1, TT])
            psB = sc.psum("psB", [128, 1])
            psH = sc.psum("psH", [64, TT])
            psQ = sc.psum("psQ", [128, TT])
            k.memset(qaug[64:66, :], 1.0)
            attn_bufs = (sqsc, rowm, sc1, sc2, mcol, srow, srow0, bcs, pt,
                         psS, psA, psR, psB, None, psS2, pt2)
            hl_bufs = (sqsc, rows4, psR, psS)
            for h in range(NHEAD):
                k.dma(kaug[0:64, :], sk_d[s, h])
                k.copy(kaug[64:65, :], mb_row)
                k.dma(qhb, qc1_d[s, h * HD:(h + 1) * HD, :])
                k.dma(o1t, out1_d[s, h * HD:(h + 1) * HD, :])
                k.stt(qhb, o1t, u1col, qhb, ALU.mult, ALU.add)
                for tt_ in range(NTT):
                    t0 = tt_ * TT
                    k.mm(psH, lqTb, qhb[:, t0:t0 + TT])
                    k.ts(kpf[:, t0:t0 + TT], psH, lqbc, None, ALU.add)
                emit_headln(k, c, kpf, qaug[0:64, :], lncc, hl_bufs, scale_col=sqsc_col)

                def dst(qt, bcs_ap, h=h, s=s):
                    q0 = qt * TT
                    k.ts(bcs_ap, bcs_ap, u1col, None, ALU.mult)
                    k.stt(att, o1t[:, q0:q0 + TT], c1col, bcs_ap, ALU.mult, ALU.add)
                    ci, off = h // 2, (h % 2) * 64
                    k.copy(fmv(attnall, ci, q0, TT)[off:off + 64, :], att)
                emit_l3_head_attn(k, c, attn_bufs, kaug, qaug,
                                  lambda kc, h=h: vtm[:, (kc * NHEAD + h) * 65:
                                                      (kc * NHEAD + h) * 65 + 65],
                                  dst, qm_row)
            for oc in range(NCH):
                for dc in range(NCH):
                    k.dma(wqs[:, dc * 128:(dc + 1) * 128],
                          woT_d[dc * 128:(dc + 1) * 128, oc * 128:(oc + 1) * 128],
                          eng="gpsimd")
                for tt_ in range(NTT):
                    t0 = tt_ * TT
                    for dc in range(NCH):
                        k.mm(psQ, wqs[:, dc * 128:(dc + 1) * 128], fmv(attnall, dc, t0, TT),
                             start=(dc == 0), stop=(dc == NCH - 1))
                    k.ts(srow, psQ, boc[:, oc:oc + 1], None, ALU.add)
                    k.dma(s_dram[s, oc * 128:(oc + 1) * 128, t0:t0 + TT], srow,
                          eng="gpsimd")

    idx_all = k.sbuf("idx_all", [128, NSEQ * NTC])
    rp_all = k.sbuf("rp_all", [128, NSEQ * NTC])
    desti = k.sbuf("desti", [128, NSEQ * NTC], I32)
    identf = k.sbuf("identf", [128, 128])
    idf_d = k.inp("ident", [128, 128])
    k.dma(identf, idf_d)
    onesr32 = k.sbuf("onesr32", [1, NSEQ * NTC])
    k.memset(onesr32, 1.0)
    NT2 = NSEQ * NTC
    DUMP = float(NEXP * CAP)

    with k.scope() as sc:
        xn = sc.sbuf("xnm", [128, NCH * T], BF16)
        wgb = sc.sbuf("wgb", [128, NCH * NEXP], BF16)
        bgc = sc.sbuf("bgc", [NEXP, 1])
        lsb = sc.sbuf("lsb", [NEXP, T])
        ltm = sc.sbuf("ltm", [128, NEXP])
        iota8c = sc.sbuf("iota8c", [128, NEXP])
        triu = sc.sbuf("triu", [128, 128])
        rmx = sc.sbuf("rmx", [128, 1])
        nmx = sc.sbuf("nmx", [128, 1])
        exps = sc.sbuf("exps", [128, NEXP])
        sume = sc.sbuf("sume", [128, 1])
        mio = sc.sbuf("mio", [128, NEXP])
        mask_e = sc.sbuf("mask_e", [128, NT2])
        trow = sc.sbuf("trow", [1, NT2])
        prow = sc.sbuf("prow", [1, NT2])
        t1m = sc.sbuf("t1m", [128, NT2])
        t2m = sc.sbuf("t2m", [128, NT2])
        dacc = sc.sbuf("dacc", [128, NT2])
        xtmo = sc.sbuf("xtmo", [128, D], BF16)
        psG = sc.psum("psG", [NEXP, TT])
        psT = sc.psum("psT2", [128, NEXP])
        psP2 = sc.psum("psP2", [128, NT2])
        psTt = sc.psum("psTt", [1, NT2])
        pstr = sc.psum("pstr", [128, 128], BF16)
        k.dma(bgc, bg_d)
        k.dma(iota8c, iota8_d)
        k.dma(triu, triu_d)
        for dc in range(NCH):
            k.dma(wgb[:, dc * NEXP:(dc + 1) * NEXP], wg_d[dc * 128:(dc + 1) * 128, :],
                  eng="gpsimd")
        for s in range(NSEQ):
            for ci in range(NCH):
                k.dma(fmv(xn, ci), xn3_d[s, ci * 128:(ci + 1) * 128, :])
            for tt_ in range(NTT):
                t0 = tt_ * TT
                for dc in range(NCH):
                    k.mm(psG, wgb[:, dc * NEXP:(dc + 1) * NEXP], fmv(xn, dc, t0, TT),
                         start=(dc == 0), stop=(dc == NCH - 1))
                k.ts(lsb[:, t0:t0 + TT], psG, bgc, None, ALU.add)
            for tc in range(NTC):
                col = s * NTC + tc
                k.tr(psT, lsb[:, tc * 128:(tc + 1) * 128], identf[0:NEXP, 0:NEXP])
                k.copy(ltm, psT)
                k.red(rmx, ltm, ALU.max)
                k.ts(nmx, rmx, -1.0, None, ALU.mult)
                k.act(exps, ltm, AF.Exp, bias=nmx)
                k.red(sume, exps, ALU.add)
                k.recip(sume, sume)
                k.copy(rp_all[:, col:col + 1], sume)
                k.ts(mio, ltm, rmx, None, ALU.is_ge)
                k.stt(mio, mio, -1e6, iota8c, ALU.mult, ALU.add)
                k.red(idx_all[:, col:col + 1], mio, ALU.min)
        k.memset(dacc, 0.0)
        for e in range(NEXP):
            k.ts(mask_e, idx_all, float(e), None, ALU.is_equal)
            k.mm(psP2, triu, mask_e)
            k.mm(psTt, c["ones_col"], mask_e)
            k.copy(trow, psTt)
            k.scan(prow, onesr32, trow, 0.0)
            k.tt(prow, prow, trow, ALU.subtract)
            k.copy(t1m, psP2)
            k.mm(psP2, c["ones_row"], prow)
            k.tt(t1m, t1m, psP2, ALU.add)
            k.ts(t1m, t1m, -1.0, None, ALU.add)
            k.ts(t2m, t1m, float(CAP), None, ALU.is_ge)
            k.ts(t1m, t1m, float(e * CAP), None, ALU.add)
            k.ts(t2m, t2m, -1.0, 1.0, ALU.mult, ALU.add)
            k.tt(t1m, t1m, t2m, ALU.mult)
            k.ts(t2m, t2m, -DUMP, DUMP, ALU.mult, ALU.add)
            k.tt(t1m, t1m, t2m, ALU.add)
            k.tt(t1m, t1m, mask_e, ALU.mult)
            k.tt(dacc, dacc, t1m, ALU.add)
        k.copy(desti, dacc)
        for s in range(NSEQ):
            for ci in range(NCH):
                k.dma(fmv(xn, ci), xn3_d[s, ci * 128:(ci + 1) * 128, :])
            for tc in range(NTC):
                for ci in range(NCH):
                    k.tr(pstr, fmv(xn, ci, tc * 128, 128), c["identb"])
                    k.copy(xtmo[:, ci * 128:(ci + 1) * 128], pstr)
                k.idma(packx_d, (desti[:, s * NTC + tc: s * NTC + tc + 1], 0),
                       xtmo, None)

    with k.scope() as sc:
        xpf = sc.sbuf("xpf", [128, NCH * CAP], BF16)
        xptm = sc.sbuf("xptm", [128, D], BF16)
        hb = sc.sbuf("hb", [128, NCHH * TT], BF16)
        w1sl = sc.sbuf("w1sl", [128, NCH * DH], BF16)
        w2sl = sc.sbuf("w2sl", [128, 4 * 128], BF16)
        b1ce = sc.sbuf("b1ce", [128, NCHH])
        b2ce = sc.sbuf("b2ce", [128, NCH])
        ytile = sc.sbuf("ytile", [128, NCH * TT], BF16)
        ytmo = sc.sbuf("ytmo", [128, D], BF16)
        psF_p = [sc.psum("psFa", [128, TT])]
        psE = [sc.psum(f"psE{j}", [128, TT]) for j in range(4)]
        pstr2_p = [sc.psum("pstr2a", [128, 128], BF16), sc.psum("pstr2b", [128, 128], BF16)]
        for e in range(NEXP):
            k.dma(b1ce, b1c_d[e])
            k.dma(b2ce, b2c_d[e])
            for dc in range(NCH):
                k.dma(w1sl[:, dc * DH:(dc + 1) * DH], W1_d[e, dc * 128:(dc + 1) * 128, :],
                      eng="gpsimd")
            for it in range(CAP // 128):
                k.dma(xptm, packx_d[e * CAP + it * 128: e * CAP + (it + 1) * 128, :])
                for ci in range(NCH):
                    pstr2 = pstr2_p[ci % 2]
                    k.tr(pstr2, xptm[:, ci * 128:(ci + 1) * 128], c["identb"])
                    k.copy(xpf[:, ci * CAP + it * 128: ci * CAP + it * 128 + 128], pstr2)
            for ct in range(CAP // TT):
                c0 = ct * TT
                for hc in range(NCHH):
                    psF = psF_p[0]
                    for dc in range(NCH):
                        k.mm(psF, w1sl[:, dc * DH + hc * 128: dc * DH + (hc + 1) * 128],
                             xpf[:, dc * CAP + c0: dc * CAP + c0 + TT],
                             start=(dc == 0), stop=(dc == NCH - 1))
                    k.act(hb[:, hc * TT:(hc + 1) * TT], psF, AF.Relu,
                          bias=b1ce[:, hc:hc + 1])
                for g2 in range(2):
                    for hc in range(NCHH):
                        k.dma(w2sl, W2_d[e, hc * 128:(hc + 1) * 128,
                                         g2 * 512:(g2 + 1) * 512], eng="gpsimd")
                        for j in range(4):
                            k.mm(psE[j], w2sl[:, j * 128:(j + 1) * 128],
                                 hb[:, hc * TT:(hc + 1) * TT],
                                 start=(hc == 0), stop=(hc == NCHH - 1))
                    for j in range(4):
                        dc = g2 * 4 + j
                        k.ts(ytile[:, dc * TT:(dc + 1) * TT], psE[j],
                             b2ce[:, dc:dc + 1], None, ALU.add)
                for tc2 in range(TT // 128):
                    for ci in range(NCH):
                        pstr2 = pstr2_p[ci % 2]
                        k.tr(pstr2, ytile[:, ci * TT + tc2 * 128: ci * TT + tc2 * 128 + 128],
                             c["identb"])
                        k.copy(ytmo[:, ci * 128:(ci + 1) * 128], pstr2)
                    r0 = e * CAP + c0 + tc2 * 128
                    k.dma(packy_d[r0:r0 + 128, :], ytmo)

    with k.scope() as sc:
        ytm = sc.sbuf("ytm", [128, D], BF16)
        ytms = sc.sbuf("ytms", [128, D], BF16)
        gt2 = sc.sbuf("gt2", [128, 128], BF16)
        st3 = sc.sbuf("st3", [128, 128])
        xt3 = sc.sbuf("xt3", [128, 128])
        pstr3 = sc.psum("pstr3", [128, 128], BF16)
        for s in range(NSEQ):
            for tc in range(NTC):
                col = s * NTC + tc
                k.idma(ytm, None, packy_d, (desti[:, col:col + 1], 0))
                k.ts(ytms, ytm, rp_all[:, col:col + 1], None, ALU.mult)
                k.act(ytm, ytms, AF.Silu)
                for ci in range(NCH):
                    k.tr(pstr3, ytm[:, ci * 128:(ci + 1) * 128], c["identb"])
                    k.copy(gt2, pstr3)
                    k.dma(st3, s_dram[s, ci * 128:(ci + 1) * 128, tc * 128:(tc + 1) * 128],
                          eng="gpsimd")
                    k.dma(xt3, x_in[s, ci * 128:(ci + 1) * 128, tc * 128:(tc + 1) * 128])
                    k.tt(st3, st3, gt2, ALU.mult)
                    k.tt(xt3, xt3, st3, ALU.add)
                    k.dma(x_out[s, ci * 128:(ci + 1) * 128, tc * 128:(tc + 1) * 128], xt3)

    k.p.finalize()
    return k

def _f32c(a):
    return np.ascontiguousarray(np.asarray(a, dtype=np.float32))


def tf_mask_bits(n, k1=0, k2=42):
    old = np.seterr(over="ignore")
    i = np.arange(n, dtype=np.uint64)
    x0 = (i >> np.uint64(32)).astype(np.uint32)
    x1 = i.astype(np.uint32)
    ks0 = np.uint32(k1)
    ks1 = np.uint32(k2)
    ks2 = np.uint32(0x1BD11BDA) ^ ks0 ^ ks1

    def rotl(x, d):
        return (x << np.uint32(d)) | (x >> np.uint32(32 - d))
    x0 = x0 + ks0
    x1 = x1 + ks1
    ks = (ks0, ks1, ks2)
    for r in range(5):
        rots = (13, 15, 26, 6) if r % 2 == 0 else (17, 29, 16, 24)
        for d in rots:
            x0 = x0 + x1
            x1 = rotl(x1, d) ^ x0
        x0 = x0 + ks[(r + 1) % 3]
        x1 = x1 + ks[(r + 2) % 3] + np.uint32(r + 1)
    np.seterr(**old)
    return x0 ^ x1


def dropout_mask_scaled():
    n = 16 * T * D
    bits = tf_mask_bits(n)
    fb = (bits >> np.uint32(9)) | np.uint32(0x3F800000)
    u = fb.view(np.float32) - 1.0
    keep = (u < np.float32(0.8)).reshape(16, T, D)
    return keep.astype(np.float32) * np.float32(1.25)


def host_prepare(inputs):
    g = {}
    g["l0_whgT"] = _f32c(np.asarray(inputs["L0_attn_W_hg"]).T)
    g["l0_woutT"] = _f32c(np.asarray(inputs["L0_attn_W_out"]).T)
    g["l0_w1T"] = _f32c(np.asarray(inputs["L0_ffn_W1"]).T)
    g["l0_b1"] = _f32c(inputs["L0_ffn_b1"]).reshape(DH, 1)
    g["l0_w2T"] = _f32c(np.asarray(inputs["L0_ffn_W2"]).T)
    g["l0_b2"] = _f32c(inputs["L0_ffn_b2"]).reshape(D, 1)
    g["l0_b2c"] = _f32c(np.asarray(inputs["L0_ffn_b2"]).reshape(NCH, 128).T)
    g["l1_wqkv"] = _f32c(inputs["L1_attn_W_qkv"])
    g["l1_bqkv"] = _f32c(inputs["L1_attn_b_qkv"]).reshape(3 * D, 1)
    g["l1_wo"] = _f32c(inputs["L1_attn_W_o"])
    g["l1_bo"] = _f32c(inputs["L1_attn_b_o"]).reshape(D, 1)
    g["l1_w1T"] = _f32c(np.asarray(inputs["L1_ffn_w1"]).T)
    g["l1_convw"] = _f32c(np.asarray(inputs["L1_ffn_conv_w"]).reshape(DH, 3))
    g["l1_convb"] = _f32c(inputs["L1_ffn_conv_b"]).reshape(DH, 1)
    g["l1_w2T"] = _f32c(np.asarray(inputs["L1_ffn_w2"]).T)
    bqkv = np.asarray(inputs["L1_attn_b_qkv"], np.float32)
    g["l1_bqkv_c"] = _f32c(bqkv[0:2048].reshape(16, 128).T)
    g["l1_bq_h"] = _f32c(bqkv[0:1024].reshape(16, 64).T)
    g["l1_bk_h"] = _f32c(bqkv[1024:2048].reshape(16, 64).T)
    g["l1_bv_row"] = _f32c(bqkv[2048:3072].reshape(1, D))
    g["l1_bo_c"] = _f32c(np.asarray(inputs["L1_attn_b_o"], np.float32).reshape(NCH, 128).T)
    g["l1_convw_c"] = _f32c(np.asarray(inputs["L1_ffn_conv_w"], np.float32)
                            .reshape(NCHH, 128, 3).transpose(1, 0, 2).reshape(128, NCHH * 3))
    g["l1_convb_c"] = _f32c(np.asarray(inputs["L1_ffn_conv_b"], np.float32)
                            .reshape(NCHH, 128).T)
    g["l2_rmsw"] = _f32c(inputs["L2_attn_rms_weight"]).reshape(D, 1)
    g["l2_rmsw_c"] = _f32c(np.asarray(inputs["L2_attn_rms_weight"], np.float32).reshape(NCH, 128).T)
    g["l2_bA_c"] = _f32c(np.asarray(inputs["L2_attn_b_A"], np.float32).reshape(NCH, 128).T)
    g["l2_b1_c"] = _f32c(np.asarray(inputs["L2_ffn_b1"], np.float32).reshape(NCHH, 128).T)
    g["l2_b2_c"] = _f32c(np.asarray(inputs["L2_ffn_b2"], np.float32).reshape(NCH, 128).T)
    g["l2_wA"] = _f32c(inputs["L2_attn_w_A"])
    g["l2_bA"] = _f32c(inputs["L2_attn_b_A"]).reshape(D, 1)
    g["l2_wx"] = _f32c(inputs["L2_attn_w_x"])
    g["l2_wsg"] = _f32c(inputs["L2_attn_w_sidegate"])
    g["l2_wout"] = _f32c(inputs["L2_attn_w_output"])
    g["l2_wo"] = _f32c(inputs["L2_attn_w_o"])
    g["l2_w1T"] = _f32c(np.asarray(inputs["L2_ffn_W1"]).T)
    g["l2_b1"] = _f32c(inputs["L2_ffn_b1"]).reshape(DH, 1)
    g["l2_w2T"] = _f32c(np.asarray(inputs["L2_ffn_W2"]).T)
    g["l2_b2"] = _f32c(inputs["L2_ffn_b2"]).reshape(D, 1)
    g["l3_wqT"] = _f32c(np.asarray(inputs["L3_attn_Wq"]).T)
    g["l3_bq"] = _f32c(inputs["L3_attn_bq"]).reshape(D, 1)
    g["l3_wkT"] = _f32c(np.asarray(inputs["L3_attn_Wk"]).T)
    g["l3_wvT"] = _f32c(np.asarray(inputs["L3_attn_Wv"]).T)
    g["l3_bv"] = _f32c(inputs["L3_attn_bv"]).reshape(D, 1)
    g["l3_woT"] = _f32c(np.asarray(inputs["L3_attn_Wo"]).T)
    g["l3_bo"] = _f32c(inputs["L3_attn_bo"]).reshape(D, 1)
    g["l3_lna"] = _f32c(inputs["L3_attn_lna_w"]).reshape(D, 1)
    g["l3_lnb"] = _f32c(inputs["L3_attn_lnb_w"]).reshape(D, 1)
    g["l3_lnc"] = _f32c(inputs["L3_attn_lnc_w"]).reshape(HD, 1)
    g["l3_lnd"] = _f32c(inputs["L3_attn_lnd_w"]).reshape(HD, 1)
    g["l3_lqT"] = _f32c(np.asarray(inputs["L3_attn_lq_w"]).T)
    g["l3_lqb"] = _f32c(inputs["L3_attn_lq_b"]).reshape(HD, 1)
    g["l3_lkT"] = _f32c(np.asarray(inputs["L3_attn_lk_w"]).T)
    g["l3_lkb"] = _f32c(inputs["L3_attn_lk_b"]).reshape(HD, 1)
    g["l3_lvT"] = _f32c(np.asarray(inputs["L3_attn_lv_w"]).T)
    g["l3_lvb"] = _f32c(inputs["L3_attn_lv_b"]).reshape(1, HD)
    g["l3_bq_h"] = _f32c(np.asarray(inputs["L3_attn_bq"], np.float32).reshape(NHEAD, HD).T)
    g["l3_bv_h"] = _f32c(np.asarray(inputs["L3_attn_bv"], np.float32).reshape(NHEAD, HD).T)
    g["l3_lna_c"] = _f32c(np.asarray(inputs["L3_attn_lna_w"], np.float32).reshape(NCH, 128).T)
    g["l3_lnb_c"] = _f32c(np.asarray(inputs["L3_attn_lnb_w"], np.float32).reshape(NCH, 128).T)
    g["l3_bq_c"] = _f32c(np.asarray(inputs["L3_attn_bq"], np.float32).reshape(NCH, 128).T)
    g["l3_bv_c"] = _f32c(np.asarray(inputs["L3_attn_bv"], np.float32).reshape(NCH, 128).T)
    g["l3_bo_c"] = _f32c(np.asarray(inputs["L3_attn_bo"], np.float32).reshape(NCH, 128).T)
    g["l3_sc3"] = np.array([[np.float32(inputs["L3_attn_threshold"]),
                             np.float32(inputs["L3_attn_temp_base"]),
                             np.float32(inputs["L3_attn_factor"])]], dtype=np.float32)
    g["l3_wgate"] = _f32c(inputs["L3_ffn_w_gate"])
    g["l3_bgate"] = _f32c(inputs["L3_ffn_b_gate"]).reshape(NEXP, 1)
    g["l3_W1"] = _f32c(inputs["L3_ffn_W1"])
    g["l3_b1"] = _f32c(inputs["L3_ffn_b1"]).reshape(NEXP, DH)
    g["l3_b1_cols"] = _f32c(np.asarray(inputs["L3_ffn_b1"], np.float32)
                            .reshape(NEXP, NCHH, 128).transpose(0, 2, 1))
    g["l3_b2_cols"] = _f32c(np.asarray(inputs["L3_ffn_b2"], np.float32)
                            .reshape(NEXP, NCH, 128).transpose(0, 2, 1))
    g["l3_W2"] = _f32c(inputs["L3_ffn_W2"])
    g["l3_b2"] = _f32c(inputs["L3_ffn_b2"]).reshape(NEXP, D)

    g["ones_col"] = np.ones((128, 1), np.float32)
    g["ones_row"] = np.ones((1, 128), np.float32)
    g["ident"] = np.eye(128, dtype=np.float32)
    g["identb"] = np.eye(128, dtype=np.float32).astype(ml_dtypes.bfloat16)
    g["triu"] = np.triu(np.ones((128, 128), np.float32))
    g["iota_row"] = np.arange(T, dtype=np.float32).reshape(1, T)
    g["iota8"] = np.tile(np.arange(8, dtype=np.float32) + 1e6, (128, 1))

    maskv = dropout_mask_scaled()
    x = np.asarray(inputs["L0_attn_x"], dtype=np.float32)
    per_core = []
    for ci in range(8):
        m = dict(g)
        m["x_in"] = _f32c(x[2 * ci: 2 * ci + 2].transpose(0, 2, 1))
        m["drop_fm"] = np.ascontiguousarray(
            maskv[2 * ci: 2 * ci + 2].transpose(0, 2, 1)).astype(ml_dtypes.bfloat16)
        per_core.append(m)
    return per_core


def _run(builder, per_core, keys, extra=None):
    k = builder()
    maps = []
    for ci in range(8):
        m = {kk: per_core[ci][kk] for kk in keys}
        if extra is not None:
            m.update(extra[ci])
        maps.append(m)
    res = run_bass_kernel_spmd(k.nc, maps, core_ids=list(range(8)))
    k.p.close()
    return res


L0_KEYS = ["x_in", "l0_whgT", "l0_woutT", "l0_w1T", "l0_b1", "l0_w2T", "l0_b2",
           "l0_b2c", "ones_col", "ones_row", "identb"]


L3B_KEYS = ["d0_all", "qc1", "sk3", "vtm3", "out0", "l3_lqT", "l3_lqb",
            "l3_lnc", "l3_sc3", "iota_row", "ones_col", "ones_row", "identb"]

L3C_KEYS = ["d0_all", "d1_all", "x_in", "qc1", "out1", "sk3", "vtm3", "xn3",
            "l3_lqT", "l3_lqb", "l3_lnc", "l3_sc3", "iota_row", "l3_woT",
            "l3_bo_c", "l3_wgate", "l3_bgate", "l3_W1", "l3_b1_cols", "l3_W2",
            "l3_b2_cols", "triu", "iota8", "ident",
            "ones_col", "ones_row", "identb"]

L3A_KEYS = ["x_in", "l3_wqT", "l3_bq_c", "l3_wkT", "l3_wvT", "l3_bv_c",
            "l3_lna_c", "l3_lnb_c", "l3_lnc", "l3_lnd", "l3_lqT", "l3_lqb",
            "l3_lkT", "l3_lkb", "l3_lvT", "l3_lvb", "l3_sc3",
            "l3_bq_h", "l3_bv_h", "ones_col", "ones_row", "identb"]

L2_KEYS = ["x_in", "l2_rmsw_c", "l2_wA", "l2_bA_c", "l2_wx", "l2_wsg", "l2_wout",
           "l2_wo", "l2_w1T", "l2_b1_c", "l2_w2T", "l2_b2_c", "drop_fm",
           "ones_col", "ones_row", "identb"]

L1_KEYS = ["x_in", "l1_wqkv", "l1_bqkv_c", "l1_bq_h", "l1_bk_h", "l1_bv_row", "l1_wo", "l1_bo_c",
           "l1_w1T", "l1_convw_c", "l1_convb_c", "l1_w2T",
           "ones_col", "ones_row", "identb"]


def kernel(**inputs):
    per_core = host_prepare(inputs)
    r = _run(build_L0, per_core, L0_KEYS)
    for ci in range(8):
        per_core[ci]["x_in"] = np.asarray(r.results[ci]["x_out"])
    r = _run(build_L1, per_core, L1_KEYS)
    for ci in range(8):
        per_core[ci]["x_in"] = np.asarray(r.results[ci]["x_out"])
    r = _run(build_L2, per_core, L2_KEYS)
    for ci in range(8):
        per_core[ci]["x_in"] = np.asarray(r.results[ci]["x_out"])
    ra = _run(build_L3a, per_core, L3A_KEYS)
    d0 = np.concatenate([np.asarray(ra.results[ci]["d0_part"]).reshape(1)
                         for ci in range(8)]).reshape(1, 8)
    extra_b = []
    for ci in range(8):
        extra_b.append({
            "d0_all": np.ascontiguousarray(d0, np.float32),
            "qc1": np.asarray(ra.results[ci]["qc1"]),
            "sk3": np.asarray(ra.results[ci]["sk3"]),
            "vtm3": np.asarray(ra.results[ci]["vtm3"]),
            "out0": np.asarray(ra.results[ci]["out0"]),
        })
    rb = _run(build_L3b, per_core, [kk for kk in L3B_KEYS if kk not in
                                    ("d0_all", "qc1", "sk3", "vtm3", "out0")],
              extra=extra_b)
    d1 = np.concatenate([np.asarray(rb.results[ci]["d1_part"]).reshape(1)
                         for ci in range(8)]).reshape(1, 8)
    extra_c = []
    for ci in range(8):
        extra_c.append({
            "d0_all": np.ascontiguousarray(d0, np.float32),
            "d1_all": np.ascontiguousarray(d1, np.float32),
            "qc1": np.asarray(ra.results[ci]["qc1"]),
            "sk3": np.asarray(ra.results[ci]["sk3"]),
            "vtm3": np.asarray(ra.results[ci]["vtm3"]),
            "xn3": np.asarray(ra.results[ci]["xn3"]),
            "out1": np.asarray(rb.results[ci]["out1"]),
        })
    rc = _run(build_L3c, per_core, [kk for kk in L3C_KEYS if kk not in
                                    ("d0_all", "d1_all", "qc1", "sk3", "vtm3",
                                     "xn3", "out1")],
              extra=extra_c)
    out = np.empty((16, T, D), np.float32)
    for ci in range(8):
        xf = np.asarray(rc.results[ci]["x_out"])
        out[2 * ci: 2 * ci + 2] = xf.transpose(0, 2, 1)
    return out
